# version 22
# baseline (speedup 1.0000x reference)
import numpy as np
import ml_dtypes
import contextlib
import concourse.bass as bass
import concourse.mybir as mybir
from concourse.bass_utils import run_bass_kernel_spmd

F32 = mybir.dt.float32
BF16 = mybir.dt.bfloat16
AF = mybir.ActivationFunctionType
ALU = mybir.AluOpType
AX = mybir.AxisListType.X

D = 4096
T = 4096
QH = 32
KVH = 8
HD = 128
GQ = QH // KVH
OWN = 1024
NQT = OWN // 128
NBT = T // 128
DC = D // 128
HALF = D // 2
SCALE = float(HD) ** -0.5
EPS = 1e-5

ENGS = ("tensor", "vector", "scalar", "gpsimd", "sync")
SEMN = ("PE", "DV", "SC", "GP", "QC", "QX0", "QX1", "QK0", "QK1", "QV", "QV2",
        "QW0", "QW1", "QT0", "QT1", "QU0", "QU1", "QQ", "H2S", "YS")


class Prog:
    def __init__(self, sems):
        self.ops = {e: [] for e in ENGS}
        self.sems = sems
        self.cnt = {k: 0 for k in sems}

    def emit(self, eng, fn):
        self.ops[eng].append(fn)

    def wait(self, eng, deps):
        best = {}
        for s, v in deps:
            if v > 0 and v > best.get(s, 0):
                best[s] = v
        if best:
            deps2 = [(self.sems[s], v) for s, v in best.items()]
            def f(e, deps2=deps2):
                for h, v in deps2:
                    e.wait_ge(h, v)
            self.emit(eng, f)

    def join(self):
        s = dict(self.cnt)
        for eng in ENGS:
            self.wait(eng, [(k, s[k]) for k in SEMN])

    def flush(self, block):
        ops = self.ops
        self.ops = {e: [] for e in ENGS}
        if ops["gpsimd"]:
            @block.gpsimd
            def _(e):
                for f in ops["gpsimd"]:
                    f(e)
        if ops["tensor"]:
            @block.tensor
            def _(e):
                for f in ops["tensor"]:
                    f(e)
        if ops["vector"]:
            @block.vector
            def _(e):
                for f in ops["vector"]:
                    f(e)
        if ops["scalar"]:
            @block.scalar
            def _(e):
                for f in ops["scalar"]:
                    f(e)
        if ops["sync"]:
            @block.sync
            def _(e):
                for f in ops["sync"]:
                    f(e)


def build_program():
    nc = bass.Bass("TRN2", target_bir_lowering=False)

    xb = nc.dram_tensor("xb", [T, D], F32, kind="ExternalInput").ap()
    wq = nc.dram_tensor("wq", [D, D], F32, kind="ExternalInput").ap()
    wk = nc.dram_tensor("wk", [D, KVH * HD], F32, kind="ExternalInput").ap()
    wv = nc.dram_tensor("wv", [D, KVH * HD], F32, kind="ExternalInput").ap()
    wo = nc.dram_tensor("wo", [D, D], F32, kind="ExternalInput").ap()
    fw = nc.dram_tensor("fw", [D, D], F32, kind="ExternalInput").ap()
    bq_pc = nc.dram_tensor("bq_pc", [128, DC], F32, kind="ExternalInput").ap()
    bk_pc = nc.dram_tensor("bk_pc", [128, KVH], F32, kind="ExternalInput").ap()
    bv_pc = nc.dram_tensor("bv_pc", [128, KVH], F32, kind="ExternalInput").ap()
    bo_pc = nc.dram_tensor("bo_pc", [128, DC], F32, kind="ExternalInput").ap()
    fb_pc = nc.dram_tensor("fb_pc", [128, DC], F32, kind="ExternalInput").ap()
    fws_pc = nc.dram_tensor("fws_pc", [128, DC], F32, kind="ExternalInput").ap()
    fis_bc = nc.dram_tensor("fis_bc", [128, 1], F32, kind="ExternalInput").ap()
    ident = nc.dram_tensor("ident", [128, 128], BF16, kind="ExternalInput").ap()
    y = nc.dram_tensor("y", [OWN, D], F32, kind="ExternalOutput").ap()

    kT_d = nc.dram_tensor("kT_d", [KVH, 128, T], BF16, kind="Internal").ap()
    v_d = nc.dram_tensor("v_d", [NBT, 128, KVH * HD], BF16, kind="Internal").ap()
    h2_d = nc.dram_tensor("h2_d", [OWN, D], F32, kind="Internal").ap()

    es = contextlib.ExitStack()
    with es:
        identb = es.enter_context(nc.sbuf_tensor("identb", [128, 128], BF16))
        ones_col = es.enter_context(nc.sbuf_tensor("ones_col", [128, 1], BF16))
        ones_row_f = es.enter_context(nc.sbuf_tensor("ones_row_f", [1, 128], F32))
        fb_sb = es.enter_context(nc.sbuf_tensor("fb_sb", [128, DC], F32))
        fws_sb = es.enter_context(nc.sbuf_tensor("fws_sb", [128, DC], F32))
        nfws_sb = es.enter_context(nc.sbuf_tensor("nfws_sb", [128, DC], F32))
        fis_sb = es.enter_context(nc.sbuf_tensor("fis_sb", [128, 1], F32))
        recip_sb = es.enter_context(nc.sbuf_tensor("recip_sb", [128, DC], F32))
        wsis_sb = es.enter_context(nc.sbuf_tensor("wsis_sb", [128, DC], F32))
        lconst = es.enter_context(nc.sbuf_tensor("lconst", [128, 8], F32))
        bq_sb = es.enter_context(nc.sbuf_tensor("bq_sb", [128, DC], F32))
        bk_sb = es.enter_context(nc.sbuf_tensor("bk_sb", [128, KVH], F32))
        bv_sb = es.enter_context(nc.sbuf_tensor("bv_sb", [128, KVH], F32))
        bo_sb = es.enter_context(nc.sbuf_tensor("bo_sb", [128, DC], F32))

        sems = {}
        for name in SEMN:
            sems[name] = es.enter_context(nc.semaphore(name))
        block = es.enter_context(nc.Block())
        P = Prog(sems)

        def dma(queue_eng, sem_name, out, in_, deps=()):
            P.wait(queue_eng, deps)
            P.cnt[sem_name] += 16
            v = P.cnt[sem_name]
            h = sems[sem_name]
            P.emit(queue_eng, lambda e, out=out, in_=in_, h=h:
                   e.dma_start(out=out, in_=in_).then_inc(h, 16))
            return v

        def op(eng, sem_name, fn, deps=(), selfwait=True):
            if selfwait and eng in ("vector", "scalar", "gpsimd"):
                deps = list(deps) + [(sem_name, P.cnt[sem_name])]
            P.wait(eng, deps)
            P.cnt[sem_name] += 1
            v = P.cnt[sem_name]
            h = sems[sem_name]
            P.emit(eng, lambda e, fn=fn, h=h: fn(e).then_inc(h, 1))
            return v

        def waiter(eng, sem_name, v):
            if v > 0:
                P.emit(eng, lambda e, h=sems[sem_name], v=v: e.wait_ge(h, v))

        dma("sync", "QC", identb[:], ident)
        dma("sync", "QC", fb_sb[:], fb_pc)
        dma("sync", "QC", fws_sb[:], fws_pc)
        dma("sync", "QC", fis_sb[:], fis_bc)
        dma("sync", "QC", bq_sb[:], bq_pc)
        dma("sync", "QC", bk_sb[:], bk_pc)
        dma("sync", "QC", bv_sb[:], bv_pc)
        v_qc = dma("sync", "QC", bo_sb[:], bo_pc)
        op("vector", "DV", lambda e: e.memset(ones_col[:], 1.0))
        v_ones = op("vector", "DV", lambda e: e.memset(ones_row_f[:], 1.0))
        v_wsis = op("vector", "DV",
                    lambda e: e.tensor_single_scalar(wsis_sb[:], fws_sb[:], fis_sb[:], ALU.mult),
                    deps=[("QC", v_qc)])
        v_recip = op("vector", "DV", lambda e: e.reciprocal(recip_sb[:], wsis_sb[:]),
                     deps=[("DV", v_wsis)])
        v_nfws = op("vector", "DV",
                    lambda e: e.tensor_single_scalar(nfws_sb[:], fws_sb[:], -1.0, ALU.mult),
                    deps=[("QC", v_qc)])
        v_nfb = op("vector", "DV",
                   lambda e: e.tensor_single_scalar(fb_sb[:], fb_sb[:], -1.0, ALU.mult),
                   deps=[("QC", v_qc)])
        v_nrecip = op("vector", "DV",
                      lambda e: e.tensor_single_scalar(recip_sb[:], recip_sb[:], -1.0, ALU.mult),
                      deps=[("DV", v_recip)])
        P.wait("tensor", [("QC", v_qc)])

        def ladder_biases(kind, bits):
            delta = 0.0
            out = []
            for (t, engine) in bits:
                th = float(t) if kind == "floor" else float(t) - 0.5
                thr = th - delta
                if engine == "sc":
                    out.append(-thr)
                    delta -= float(t) / 2.0
            return out

        FLOOR_BITS = [(128.0, "sc"), (64.0, "dve"), (32.0, "sc"), (16.0, "dve"),
                      (8.0, "sc"), (4.0, "dve"), (2.0, "sc"), (1.0, "dve")]
        ROUND_BITS = [(4.0, "sc"), (2.0, "dve"), (1.0, "sc")]
        _cvals = ladder_biases("floor", FLOOR_BITS) + ladder_biases("round", ROUND_BITS) + [-8.0]
        cmap = {}
        v_lconst = 0
        for _i, _v in enumerate(_cvals):
            v_lconst = op("vector", "DV",
                          lambda e, _i=_i, _v=_v: e.memset(lconst[:, _i:_i + 1], _v))
            cmap[_v] = lconst[:, _i:_i + 1]

        def ln_stats(xt_ap, sq_scr_ap, st, deps_dv, deps_sc):
            v1 = op("vector", "DV", lambda e: e.reduce_sum(st["ssum"], xt_ap, axis=AX),
                    deps=deps_dv)
            v2 = op("scalar", "SC",
                    lambda e: e.activation(sq_scr_ap, xt_ap, AF.Square, accum_out=st["ssq"]),
                    deps=deps_sc)
            op("vector", "DV",
               lambda e: e.tensor_single_scalar(st["negm"], st["ssum"], -1.0 / D, ALU.mult),
               deps=[("DV", v1)])
            op("vector", "DV",
               lambda e: e.tensor_scalar(st["vare"], st["ssq"], 1.0 / D, EPS, ALU.mult, ALU.add),
               deps=[("SC", v2)])
            op("vector", "DV",
               lambda e: e.tensor_tensor(st["msq"], st["negm"], st["negm"], ALU.mult))
            v5 = op("vector", "DV",
                    lambda e: e.tensor_tensor(st["vare"], st["vare"], st["msq"], ALU.subtract))
            v6 = op("scalar", "SC", lambda e: e.activation(st["sd"], st["vare"], AF.Sqrt),
                    deps=[("DV", v5)])
            v_rstd = op("vector", "DV", lambda e: e.reciprocal(st["rstd"], st["sd"]),
                        deps=[("SC", v6)])
            return v_rstd

        def emit_ladder(kind, a_ap, r_ap, g_ap, out_ap, bits, deps):
            delta = 0.0
            cur = a_ap
            vr = list(deps)
            for (t, engine) in bits:
                th = float(t) if kind == "floor" else float(t) - 0.5
                thr = th - delta
                if engine == "dve":
                    vg = op("vector", "DV",
                            lambda e, cur=cur, thr=thr, t=t: e.tensor_scalar(
                                g_ap, cur, float(thr), -float(t), ALU.is_ge, ALU.mult),
                            deps=vr)
                    vnew = op("vector", "DV",
                              lambda e, cur=cur: e.tensor_tensor(r_ap, cur, g_ap, ALU.add),
                              deps=[("DV", vg)])
                else:
                    bias_ap = cmap[-float(thr)]
                    vs = op("scalar", "SC",
                            lambda e, cur=cur, bias_ap=bias_ap: e.activation(
                                g_ap, cur, AF.Sign, bias=bias_ap),
                            deps=list(vr) + [("DV", v_lconst)])
                    vnew = op("vector", "DV",
                              lambda e, cur=cur, t=t: e.scalar_tensor_tensor(
                                  r_ap, g_ap, -float(t) / 2.0, cur, ALU.mult, ALU.add),
                              deps=[("SC", vs)])
                    delta -= float(t) / 2.0
                cur = r_ap
                vr = [("DV", vnew)]
            vfin = op("vector", "DV",
                      lambda e, delta=delta: e.scalar_tensor_tensor(
                          out_ap, r_ap, float(delta), a_ap, ALU.add, ALU.subtract),
                      deps=vr)
            return vfin

        qT_d = nc.dram_tensor("qT_d", [QH, 128, OWN], BF16, kind="Internal").ap()
        with contextlib.ExitStack() as p1:
            xt = p1.enter_context(nc.sbuf_tensor("p1_xt", [128, 2, D], F32))
            u_tok = p1.enter_context(nc.sbuf_tensor("p1_u", [128, D], BF16))
            uT_g = p1.enter_context(nc.sbuf_tensor("p1_uT", [128, 2, DC, 1024], BF16))
            w_sb = p1.enter_context(nc.sbuf_tensor("p1_w", [128, 2, DC, 256], BF16))
            kv_stage = p1.enter_context(nc.sbuf_tensor("p1_st", [128, 2, 1024], BF16))
            vt_stage = p1.enter_context(nc.sbuf_tensor("p1_vt", [128, 1, 8, 128], BF16))
            sttt = p1.enter_context(nc.sbuf_tensor("p1_stt", [128, 8], F32))
            stt = {k: sttt[:, i:i + 1] for i, k in enumerate(
                ("ssum", "ssq", "negm", "msq", "vare", "sd", "rstd"))}
            psA = p1.enter_context(nc.psum_tensor("p1_psA", [128, 2, 1024], F32))
            psT = p1.enter_context(nc.psum_tensor("p1_psT", [128, 2, 1024], BF16))

            st = dict(dv_u=[0, 0], pe_u=0, pe_w=[0, 0], dv_ev=[0, 0],
                      kv_dma=[0, 0], vt_dma=[0, 0], uT_rdy=[0, 0],
                      wctr=0, pctr=0, tctr=0, dv_psT=[0, 0])

            def emit_tile_ln(g, i):
                t = g * 8 + i
                buf = t % 2
                qx = "QX%d" % buf
                vx = dma("sync", qx, xt[:, buf, :], xb[t * 128:(t + 1) * 128, :],
                         deps=[("DV", st["dv_u"][buf])])
                v_rstd = ln_stats(xt[:, buf, :], u_tok[:], stt,
                                  deps_dv=[(qx, vx)],
                                  deps_sc=[(qx, vx), ("PE", st["pe_u"]),
                                           ("DV", st["dv_u"][1 - buf])])
                vu = op("vector", "DV",
                        lambda e, buf=buf: e.tensor_scalar(
                            u_tok[:], xt[:, buf, :], stt["negm"], stt["rstd"],
                            ALU.add, ALU.mult),
                        deps=[("DV", v_rstd), ("PE", st["pe_u"])])
                st["dv_u"][buf] = vu
                st["vu_p"] = vu

            def emit_tile_tp(g, i):
                gb = g % 2
                vu = st["vu_p"]
                last_dv = 0
                pe_v = 0
                for gg in range(4):
                    hb = st["tctr"] % 2
                    st["tctr"] += 1
                    for k in range(8):
                        dc = gg * 8 + k
                        d_ = ([("DV", vu)] if (gg == 0 and k == 0) else []) + \
                             ([("DV", st["dv_psT"][hb])] if k == 0 else [])
                        pe_v = op("tensor", "PE",
                                  lambda e, dc=dc, k=k, hb=hb: e.transpose(
                                      psT[:, hb, k * 128:(k + 1) * 128],
                                      u_tok[:, dc * 128:(dc + 1) * 128], identb[:]),
                                  deps=d_)
                    last_dv = op("vector", "DV",
                                 lambda e, gg=gg, gb=gb, i=i, hb=hb: e.tensor_copy(
                                     uT_g[:, gb, gg * 8:(gg + 1) * 8, i * 128:(i + 1) * 128],
                                     psT[:, hb, :]),
                                 deps=[("PE", pe_v)])
                    st["dv_psT"][hb] = last_dv
                st["pe_u"] = pe_v
                if i == 7:
                    st["uT_rdy"][gb] = last_dv

            def emit_tile(g, i):
                emit_tile_ln(g, i)
                emit_tile_tp(g, i)

            def emit_b(g, kind, oc2):
                gb = g % 2
                wbuf = st["wctr"] % 2
                st["wctr"] += 1
                if kind == "q":
                    wsrc, bias_src = wq, bq_sb
                elif kind == "k":
                    wsrc, bias_src = wk, bk_sb
                else:
                    wsrc, bias_src = wv, bv_sb
                qw = "QW%d" % wbuf
                vw = dma("gpsimd", qw, w_sb[:, wbuf, :, :],
                         wsrc[:, oc2 * 256:(oc2 + 1) * 256].rearrange("(c p) n -> p c n", p=128),
                         deps=[("PE", st["pe_w"][wbuf])])
                for sub in range(2):
                    oc = 2 * oc2 + sub
                    pbuf = st["pctr"] % 2
                    st["pctr"] += 1
                    first_deps = [(qw, vw), ("DV", st["dv_ev"][pbuf]),
                                  ("DV", st["uT_rdy"][gb])]
                    pe_last = 0
                    for dc in range(DC):
                        for s in range(2):
                            pe_last = op("tensor", "PE",
                                         lambda e, dc=dc, s=s, gb=gb, pbuf=pbuf, wbuf=wbuf, sub=sub: e.matmul(
                                             psA[:, pbuf, s * 512:(s + 1) * 512],
                                             w_sb[:, wbuf, dc, sub * 128:(sub + 1) * 128],
                                             uT_g[:, gb, dc, s * 512:(s + 1) * 512],
                                             start=(dc == 0), stop=(dc == DC - 1)),
                                         deps=(first_deps if (dc == 0 and s == 0) else []))
                    st["pe_w"][wbuf] = pe_last
                    dv_e = op("vector", "DV",
                              lambda e, pbuf=pbuf, bias_src=bias_src, oc=oc: e.tensor_single_scalar(
                                  kv_stage[:, pbuf, :], psA[:, pbuf, :],
                                  bias_src[:, oc:oc + 1], ALU.add),
                              deps=[("PE", pe_last), ("QK%d" % pbuf, st["kv_dma"][pbuf]),
                                    ("QC", v_qc)])
                    st["dv_ev"][pbuf] = dv_e
                    if kind == "q":
                        st["kv_dma"][pbuf] = dma("sync", "QK%d" % pbuf, qT_d[oc, :, :],
                                                 kv_stage[:, pbuf, :], deps=[("DV", dv_e)])
                    elif kind == "k":
                        st["kv_dma"][pbuf] = dma(
                            "sync", "QK%d" % pbuf,
                            kT_d[oc, :, g * 1024:(g + 1) * 1024],
                            kv_stage[:, pbuf, :], deps=[("DV", dv_e)])
                    else:
                        hb = st["tctr"] % 2
                        st["tctr"] += 1
                        pe_tv = 0
                        for k in range(8):
                            d_ = ([("DV", dv_e)] if k == 0 else []) + \
                                 ([("DV", st["dv_psT"][hb])] if k == 0 else [])
                            pe_tv = op("tensor", "PE",
                                       lambda e, k=k, pbuf=pbuf, hb=hb: e.transpose(
                                           psT[:, hb, k * 128:(k + 1) * 128],
                                           kv_stage[:, pbuf, k * 128:(k + 1) * 128],
                                           identb[:]),
                                       deps=d_)
                        st["kv_dma"][pbuf] = P.cnt["QK%d" % pbuf]
                        dv_v = op("vector", "DV",
                                  lambda e, hb=hb: e.tensor_copy(
                                      vt_stage[:, 0, :, :], psT[:, hb, :]),
                                  deps=[("PE", pe_tv), ("QV", max(st["vt_dma"]))])
                        st["dv_psT"][hb] = dv_v
                        kt0 = g * 8
                        st["vt_dma"][pbuf] = dma(
                            "sync", "QV",
                            v_d[kt0:kt0 + 8, :, oc * 128:(oc + 1) * 128]
                            .rearrange("t p n -> p t n"),
                            vt_stage[:, 0, :, :],
                            deps=[("DV", dv_v)])

            for i in range(8):
                emit_tile(0, i)
            for g in range(4):
                bunits = []
                if g == 0:
                    bunits += [("q", oc2) for oc2 in range(QH // 2)]
                bunits += [("k", j) for j in range(KVH // 2)]
                bunits += [("v", j) for j in range(KVH // 2)]
                aunits = list(range(8)) if g < 3 else []
                na, nb = len(aunits), len(bunits)
                a_ln = 0
                a_tp = 0
                for bi, (kind, oc) in enumerate(bunits):
                    if na and a_ln < na and a_ln == a_tp and \
                            (bi + 2) * na >= (a_ln + 1) * nb:
                        emit_tile_ln(g + 1, aunits[a_ln])
                        a_ln += 1
                    emit_b(g, kind, oc)
                    while na and a_tp < a_ln and (bi + 1) * na >= (a_tp + 1) * nb:
                        emit_tile_tp(g + 1, aunits[a_tp])
                        a_tp += 1
                while na and a_tp < na:
                    if a_ln == a_tp:
                        emit_tile_ln(g + 1, aunits[a_ln])
                        a_ln += 1
                    emit_tile_tp(g + 1, aunits[a_tp])
                    a_tp += 1
            P.join()
            P.flush(block)

        AT_cm = nc.sbuf_tensor("AT", [128, QH, OWN], BF16, side="right")
        AT = AT_cm.__enter__()
        with contextlib.ExitStack() as p3:
            qT = p3.enter_context(nc.sbuf_tensor("p3_qT", [128, QH, OWN], BF16))
            kT_sb = p3.enter_context(nc.sbuf_tensor("p3_kT", [128, 2, T], BF16))
            v_tok = p3.enter_context(nc.sbuf_tensor("p3_v", [128, 2, NBT, 128], BF16))
            ptile = p3.enter_context(nc.sbuf_tensor("p3_p", [128, 2, 1024], BF16))
            sqs = p3.enter_context(nc.sbuf_tensor("p3_sqs", [128, T], BF16))
            mk_sb = p3.enter_context(nc.sbuf_tensor("p3_mk", [1, 16], F32))
            cb = p3.enter_context(nc.sbuf_tensor("p3_cb", [128, 4], F32))
            sig_sb = p3.enter_context(nc.sbuf_tensor("p3_sig", [1, 1024], F32))
            rs_sb = p3.enter_context(nc.sbuf_tensor("p3_rs", [1, 1024], F32))
            bc_sb = p3.enter_context(nc.sbuf_tensor("p3_bc", [128, 1024], F32))
            psS = p3.enter_context(nc.psum_tensor("p3_psS", [128, 2, 1024], F32))
            psA_ = p3.enter_context(nc.psum_tensor("p3_psA", [128, 1024], F32))
            psG = p3.enter_context(nc.psum_tensor("p3_psG", [128, 1024], F32))

            v_qt = dma("sync", "QQ", qT[:, :, :], qT_d.rearrange("o p n -> p o n"))

            pe_group_done = [0, 0]
            pe_sqs_read = 0
            dv_head_tail = 0
            sc_exp = [0, 0]
            pe_pv_sig = [0, 0]
            for j in range(KVH):
                gbuf = j % 2
                qt_s = "QT%d" % gbuf
                qu_s = "QU%d" % gbuf
                vk = dma("sync", qt_s, kT_sb[:, gbuf, :], kT_d[j, :, :],
                         deps=[("PE", pe_group_done[gbuf])])
                vv = 0
                for q4 in range(4):
                    vv = dma("sync", qu_s,
                             v_tok[:, gbuf, q4 * 8:(q4 + 1) * 8, :],
                             v_d[q4 * 8:(q4 + 1) * 8, :, j * 128:(j + 1) * 128]
                             .rearrange("t p n -> p t n"))
                dv_sq = op("vector", "DV",
                           lambda e, gbuf=gbuf: e.tensor_tensor(
                               sqs[:], kT_sb[:, gbuf, :], kT_sb[:, gbuf, :], ALU.mult),
                           deps=[(qt_s, vk), ("PE", pe_sqs_read)])
                dvm = [0, 0]
                for s in range(8):
                    slot = s % 2
                    d_ = [("DV", dv_sq)] if s == 0 else []
                    waiter("tensor", "DV", dvm[slot])
                    pe_n = op("tensor", "PE",
                              lambda e, s=s, slot=slot: e.matmul(
                                  psG[0:1, slot * 512:(slot + 1) * 512],
                                  ones_col[:], sqs[:, s * 512:(s + 1) * 512],
                                  start=True, stop=True),
                              deps=d_)
                    dvm[slot] = op("vector", "DV",
                                   lambda e, s=s, slot=slot: e.reduce_max(
                                       mk_sb[0:1, s:s + 1],
                                       psG[0:1, slot * 512:(slot + 1) * 512], axis=AX),
                                   deps=[("PE", pe_n)])
                qu_s2, vv2 = qu_s, vv
                op("vector", "DV",
                   lambda e: e.reduce_max(mk_sb[0:1, 8:9], mk_sb[0:1, 0:8], axis=AX))

                dv_cb = 0
                for hh in range(GQ):
                    h = j * GQ + hh
                    dv_qsq = op("vector", "DV",
                                lambda e, h=h: e.tensor_tensor(
                                    sqs[:, 0:1024], qT[:, h, :], qT[:, h, :], ALU.mult),
                                deps=[("PE", pe_sqs_read), ("QQ", v_qt)])
                    pe_n = 0
                    for s in range(2):
                        pe_n = op("tensor", "PE",
                                  lambda e, s=s: e.matmul(
                                      psG[0:1, s * 512:(s + 1) * 512],
                                      ones_col[:], sqs[:, s * 512:(s + 1) * 512],
                                      start=True, stop=True),
                                  deps=([("DV", dv_qsq)] if s == 0 else []))
                    pe_sqs_read = pe_n
                    op("vector", "DV",
                       lambda e: e.reduce_max(mk_sb[0:1, 9:10], psG[0:1, 0:512], axis=AX),
                       deps=[("PE", pe_n)])
                    op("vector", "DV",
                       lambda e: e.reduce_max(mk_sb[0:1, 10:11], psG[0:1, 512:1024], axis=AX))
                    op("vector", "DV",
                       lambda e: e.reduce_max(mk_sb[0:1, 11:12], mk_sb[0:1, 9:11], axis=AX))
                    dv_c1 = op("vector", "DV",
                               lambda e: e.tensor_tensor(mk_sb[0:1, 12:13], mk_sb[0:1, 11:12],
                                                         mk_sb[0:1, 8:9], ALU.add))
                    dv_c3 = op("vector", "DV",
                               lambda e: e.tensor_single_scalar(
                                   mk_sb[0:1, 14:15], mk_sb[0:1, 12:13], -SCALE / 2.0, ALU.mult),
                               deps=[("DV", dv_c1)])
                    pe_cb = op("tensor", "PE",
                               lambda e: e.matmul(psG[:, 0:1], ones_row_f[0:1, :],
                                                  mk_sb[0:1, 14:15], start=True, stop=True),
                               deps=[("DV", dv_c3)])
                    dv_cb = op("vector", "DV",
                               lambda e, hh=hh: e.tensor_copy(cb[:, hh:hh + 1], psG[:, 0:1]),
                               deps=[("PE", pe_cb)])
                waiter("tensor", "DV", dv_cb)
                dv_b_prev = 0

                for hh in range(GQ):
                    h = j * GQ + hh
                    def emit_s(kc):
                        sb_ = kc % 2
                        pe_s = 0
                        for s in range(2):
                            d_ = ([("SC", sc_exp[sb_])] if s == 0 else [])
                            pe_s = op("tensor", "PE",
                                      lambda e, kc=kc, s=s, sb_=sb_, gbuf=gbuf, h=h: e.matmul(
                                          psS[:, sb_, s * 512:(s + 1) * 512],
                                          kT_sb[:, gbuf, kc * 128:(kc + 1) * 128],
                                          qT[:, h, s * 512:(s + 1) * 512],
                                          start=True, stop=True),
                                      deps=d_)
                        return pe_s

                    def emit_exp(kc, pe_s):
                        sb_ = kc % 2
                        d_exp = [("PE", pe_s), ("PE", pe_pv_sig[sb_])]
                        if kc == 0:
                            d_exp.append(("DV", dv_cb))
                        vexp = op("scalar", "SC",
                                  lambda e, sb_=sb_, hh=hh: e.activation(
                                      ptile[:, sb_, :], psS[:, sb_, :], AF.Exp,
                                      bias=cb[:, hh:hh + 1], scale=SCALE),
                                  deps=d_exp, selfwait=False)
                        sc_exp[sb_] = vexp
                        return vexp

                    def emit_pv(kc, vexp):
                        sb_ = kc % 2
                        for s in range(2):
                            d_ = []
                            if s == 0:
                                d_ = [("SC", vexp)]
                                if kc == 0:
                                    d_ += [(qu_s2, vv2), ("DV", dv_head_tail)]
                            op("tensor", "PE",
                               lambda e, kc=kc, s=s, sb_=sb_, gbuf=gbuf: e.matmul(
                                   psA_[:, s * 512:(s + 1) * 512],
                                   v_tok[:, gbuf, kc, :],
                                   ptile[:, sb_, s * 512:(s + 1) * 512],
                                   start=(kc == 0), stop=(kc == NBT - 1)),
                               deps=d_)
                        pe_g = 0
                        for s in range(2):
                            d2_ = ([("DV", dv_b_prev)] if (kc == 0 and s == 0) else [])
                            pe_g = op("tensor", "PE",
                                      lambda e, kc=kc, s=s, sb_=sb_: e.matmul(
                                          psG[0:1, s * 512:(s + 1) * 512],
                                          ones_col[:],
                                          ptile[:, sb_, s * 512:(s + 1) * 512],
                                          start=(kc == 0), stop=(kc == NBT - 1)),
                                      deps=d2_)
                        pe_pv_sig[sb_] = pe_g

                    pe_s_arr = {0: emit_s(0), 1: emit_s(1)}
                    for kc in range(NBT):
                        vexp = emit_exp(kc, pe_s_arr[kc])
                        if kc + 2 < NBT:
                            pe_s_arr[kc + 2] = emit_s(kc + 2)
                        emit_pv(kc, vexp)
                    pe_head_mms = P.cnt["PE"]
                    dv_s2 = op("vector", "DV",
                               lambda e: e.reciprocal(rs_sb[:], psG[0:1, :]),
                               deps=[("PE", pe_head_mms)])
                    pe_b = 0
                    for s in range(2):
                        d_ = ([("DV", dv_s2)] if s == 0 else [])
                        pe_b = op("tensor", "PE",
                                  lambda e, s=s: e.matmul(
                                      psG[:, s * 512:(s + 1) * 512],
                                      ones_row_f[0:1, :],
                                      rs_sb[0:1, s * 512:(s + 1) * 512],
                                      start=True, stop=True),
                                  deps=d_)
                    dv_b = op("vector", "DV",
                              lambda e: e.tensor_copy(bc_sb[:], psG[:, :]),
                              deps=[("PE", pe_b)])
                    dv_b_prev = dv_b
                    dv_n = op("vector", "DV",
                              lambda e, h=h: e.tensor_tensor(
                                  AT[:, h, :], psA_[:], bc_sb[:], ALU.mult),
                              deps=[("DV", dv_b)])
                    dv_head_tail = dv_n
                pe_group_done[gbuf] = P.cnt["PE"]
            P.join()
            P.flush(block)

        big_cm = nc.sbuf_tensor("big_tok", [128, NQT, D], BF16)
        big_tok = big_cm.__enter__()
        with contextlib.ExitStack() as p4:
            wo_sb = p4.enter_context(nc.sbuf_tensor("p4_wo", [128, 2, DC, 128], BF16))
            ot_stage = p4.enter_context(nc.sbuf_tensor("p4_ot", [128, 2, 1024], BF16))
            psA = p4.enter_context(nc.psum_tensor("p4_psA", [128, 2, 1024], F32))
            psT = p4.enter_context(nc.psum_tensor("p4_psT", [128, 2, 1024], BF16))

            pe_o = [0, 0]
            dv_o_evac = [0, 0]
            pe_ot_read = [0, 0]
            dv_psT = [0, 0]
            for oc in range(DC):
                wbuf = oc % 2
                qw = "QW%d" % wbuf
                vw = dma("gpsimd", qw, wo_sb[:, wbuf, :, :],
                         wo[:, oc * 128:(oc + 1) * 128].rearrange("(c p) n -> p c n", p=128),
                         deps=[("PE", pe_o[wbuf])])
                pbuf = oc % 2
                first_deps = [(qw, vw), ("DV", dv_o_evac[pbuf])]
                pe_last = 0
                for dc in range(DC):
                    for s in range(2):
                        pe_last = op("tensor", "PE",
                                     lambda e, dc=dc, s=s, pbuf=pbuf, wbuf=wbuf: e.matmul(
                                         psA[:, pbuf, s * 512:(s + 1) * 512],
                                         wo_sb[:, wbuf, dc, :],
                                         AT[:, dc, s * 512:(s + 1) * 512],
                                         start=(dc == 0), stop=(dc == DC - 1)),
                                     deps=(first_deps if (dc == 0 and s == 0) else []))
                pe_o[wbuf] = pe_last
                dv_e = op("vector", "DV",
                          lambda e, pbuf=pbuf, oc=oc: e.tensor_single_scalar(
                              ot_stage[:, pbuf, :], psA[:, pbuf, :],
                              bo_sb[:, oc:oc + 1], ALU.add),
                          deps=[("PE", pe_last), ("PE", pe_ot_read[pbuf]),
                                ("QC", v_qc)])
                dv_o_evac[pbuf] = dv_e
                hb = oc % 2
                pe_t = 0
                for k in range(8):
                    d_ = ([("DV", dv_e), ("DV", dv_psT[hb])] if k == 0 else [])
                    pe_t = op("tensor", "PE",
                              lambda e, k=k, pbuf=pbuf, hb=hb: e.transpose(
                                  psT[:, hb, k * 128:(k + 1) * 128],
                                  ot_stage[:, pbuf, k * 128:(k + 1) * 128],
                                  identb[:]),
                              deps=d_)
                pe_ot_read[pbuf] = pe_t
                dv_t = op("vector", "DV",
                          lambda e, oc=oc, hb=hb: e.tensor_copy(
                              big_tok[:, :, oc * 128:(oc + 1) * 128], psT[:, hb, :]),
                          deps=[("PE", pe_t)])
                dv_psT[hb] = dv_t
            P.join()
            P.flush(block)

        AT_cm.__exit__(None, None, None)

        xiT_cm = nc.sbuf_tensor("xiT", [128, DC, OWN], BF16, side="right")
        xiT = xiT_cm.__enter__()
        with contextlib.ExitStack() as p4b:
            xtile = p4b.enter_context(nc.sbuf_tensor("p4b_x", [128, D], F32))
            h2t = p4b.enter_context(nc.sbuf_tensor("p4b_h2", [128, D], F32))
            au = p4b.enter_context(nc.sbuf_tensor("p4b_au", [128, D], F32))
            xs_h = p4b.enter_context(nc.sbuf_tensor("p4b_xs", [128, 2, HALF], BF16))
            r_h = p4b.enter_context(nc.sbuf_tensor("p4b_r", [128, 2, HALF], F32))
            stt = {k: p4b.enter_context(nc.sbuf_tensor("p4b_s_" + k, [128, 1], F32))
                   for k in ("ssum", "ssq", "negm", "msq", "vare", "sd", "rstd")}
            stt = {k: v[:] for k, v in stt.items()}
            rstd3f = p4b.enter_context(nc.sbuf_tensor("p4b_r3f", [128, 1], F32))
            psT = p4b.enter_context(nc.psum_tensor("p4b_psT", [128, 2, 1024], BF16))

            pe_xi_read_ = [0, 0]
            dv_psT4 = [("DV", 0), ("DV", 0)]
            for qt in range(NQT):
                P.join()
                vx = dma("sync", "QX0", xtile[:], xb[qt * 128:(qt + 1) * 128, :])
                v_rstd2 = ln_stats(big_tok[:, qt, :], au[:], stt, deps_dv=[], deps_sc=[])
                dv_y1 = op("vector", "DV",
                           lambda e, qt=qt: e.tensor_scalar(
                               h2t[:], big_tok[:, qt, :], stt["negm"], stt["rstd"],
                               ALU.add, ALU.mult),
                           deps=[("DV", v_rstd2)])
                dv_h2 = op("vector", "DV",
                           lambda e: e.tensor_tensor(h2t[:], h2t[:], xtile[:], ALU.add),
                           deps=[("DV", dv_y1), ("QX0", vx)])
                dma("sync", "H2S", h2_d[qt * 128:(qt + 1) * 128, :], h2t[:],
                    deps=[("DV", dv_h2)])
                v_rstd3 = ln_stats(h2t[:], au[:], stt,
                                   deps_dv=[("DV", dv_h2)],
                                   deps_sc=[("DV", dv_h2), ("DV", dv_y1)])
                dv_r3f = op("vector", "DV",
                            lambda e: e.tensor_tensor(rstd3f[:], stt["rstd"], fis_sb[:],
                                                      ALU.mult),
                            deps=[("DV", v_rstd3)])
                dv_xs_ = [0, 0]
                for dh in range(2):
                    dv_xs_[dh] = op("vector", "DV",
                                    lambda e, dh=dh: e.tensor_scalar(
                                        r_h[:, dh, :], h2t[:, dh * HALF:(dh + 1) * HALF],
                                        stt["negm"], rstd3f[:], ALU.add, ALU.mult),
                                    deps=[("DV", dv_r3f)])
                sc_a_ = [0, 0]
                sc_ns_ = [0, 0]
                for dh in range(2):
                    sc_a_[dh] = op("scalar", "SC",
                                   lambda e, dh=dh: e.activation(
                                       au[:, dh * HALF:(dh + 1) * HALF],
                                       r_h[:, dh, :], AF.Abs),
                                   deps=[("DV", dv_xs_[dh])])
                ns_ap = [big_tok[:, qt, dh * HALF:(dh + 1) * HALF] for dh in range(2)]
                for dh in range(2):
                    sc_ns_[dh] = op("scalar", "SC",
                                    lambda e, dh=dh, ns=ns_ap[dh]: e.activation(
                                        ns, r_h[:, dh, :], AF.Sign, scale=-1.0),
                                    deps=[("DV", dv_xs_[dh])])
                cur_ = [au[:, 0:HALF], au[:, HALF:2 * HALF]]
                vr_ = [[("SC", sc_a_[0]), ("SC", sc_ns_[0]), ("DV", v_lconst),
                        ("PE", pe_xi_read_[0])],
                       [("SC", sc_a_[1]), ("SC", sc_ns_[1]), ("DV", v_lconst),
                        ("PE", pe_xi_read_[1])]]
                delta = 0.0
                for (t, engine) in FLOOR_BITS:
                    th = float(t)
                    thr = th - delta
                    vc_ = [0, 0]
                    if engine == "dve":
                        for dh in range(2):
                            vc_[dh] = op("vector", "DV",
                                         lambda e, cur=cur_[dh], thr=thr, t=t, dh=dh:
                                         e.tensor_scalar(
                                             xs_h[:, dh, :], cur, float(thr), -float(t),
                                             ALU.is_ge, ALU.mult),
                                         deps=vr_[dh])
                        for dh in range(2):
                            vnew = op("vector", "DV",
                                      lambda e, cur=cur_[dh], dh=dh:
                                      e.tensor_tensor(
                                          r_h[:, dh, :], cur, xs_h[:, dh, :], ALU.add),
                                      deps=[("DV", vc_[dh])])
                            cur_[dh] = r_h[:, dh, :]
                            vr_[dh] = [("DV", vnew)]
                    else:
                        bias_ap = cmap[-float(thr)]
                        for dh in range(2):
                            vc_[dh] = op("scalar", "SC",
                                         lambda e, cur=cur_[dh], bias_ap=bias_ap, dh=dh:
                                         e.activation(
                                             xs_h[:, dh, :], cur, AF.Sign, bias=bias_ap),
                                         deps=vr_[dh])
                        for dh in range(2):
                            vnew = op("vector", "DV",
                                      lambda e, cur=cur_[dh], t=t, dh=dh:
                                      e.scalar_tensor_tensor(
                                          r_h[:, dh, :], xs_h[:, dh, :], -float(t) / 2.0,
                                          cur, ALU.mult, ALU.add),
                                      deps=[("SC", vc_[dh])])
                            cur_[dh] = r_h[:, dh, :]
                            vr_[dh] = [("DV", vnew)]
                        delta -= float(t) / 2.0
                dv_xi_ = [0, 0]
                for dh in range(2):
                    vfin = op("vector", "DV",
                              lambda e, delta=delta, dh=dh: e.scalar_tensor_tensor(
                                  xs_h[:, dh, :], r_h[:, dh, :], float(delta),
                                  au[:, dh * HALF:(dh + 1) * HALF], ALU.add, ALU.subtract),
                              deps=vr_[dh])
                    dv_xi_[dh] = op("vector", "DV",
                                    lambda e, dh=dh, ns=ns_ap[dh]: e.tensor_tensor(
                                        xs_h[:, dh, :], xs_h[:, dh, :], ns,
                                        ALU.mult),
                                    deps=[("DV", vfin)])
                for dh in range(2):
                    for g in range(2):
                        hb = g
                        d0 = [("DV", dv_xi_[dh])] if g == 0 else []
                        for k in range(8):
                            d_ = (d0 + [dv_psT4[hb]]) if k == 0 else []
                            pe_v = op("tensor", "PE",
                                      lambda e, g=g, k=k, hb=hb, dh=dh: e.transpose(
                                          psT[:, hb, k * 128:(k + 1) * 128],
                                          xs_h[:, dh, (g * 8 + k) * 128:(g * 8 + k + 1) * 128],
                                          identb[:]),
                                      deps=d_)
                        sc_e = op("scalar", "SC",
                                  lambda e, g=g, dh=dh, qt=qt, hb=hb: e.activation(
                                      xiT[:, dh * 16 + g * 8:dh * 16 + (g + 1) * 8,
                                          qt * 128:(qt + 1) * 128],
                                      psT[:, hb, :], AF.Copy),
                                  deps=[("PE", pe_v)])
                        dv_psT4[hb] = ("SC", sc_e)
                    pe_xi_read_[dh] = pe_v
            P.join()
            P.flush(block)

        big_cm.__exit__(None, None, None)

        ffn_d = nc.dram_tensor("ffn_d", [OWN, D], BF16, kind="Internal").ap()
        with contextlib.ExitStack() as p5:
            fw_sb = p5.enter_context(nc.sbuf_tensor("p5_fw", [128, 2, D], F32))
            q_b = p5.enter_context(nc.sbuf_tensor("p5_q", [128, D], F32))
            r_b = p5.enter_context(nc.sbuf_tensor("p5_r", [128, D], F32))
            g_b = p5.enter_context(nc.sbuf_tensor("p5_g", [128, D], F32))
            wn_b = p5.enter_context(nc.sbuf_tensor("p5_wn", [128, D], BF16))
            wT_sb = p5.enter_context(nc.sbuf_tensor("p5_wT", [128, 4, DC, 128], BF16))
            ft_stage = p5.enter_context(nc.sbuf_tensor("p5_ft", [128, 2, 1024], BF16))
            tt_stage = p5.enter_context(nc.sbuf_tensor("p5_tt", [128, 2, 8, 128], BF16))
            psA = p5.enter_context(nc.psum_tensor("p5_psA", [128, 2, 1024], F32))
            psT = p5.enter_context(nc.psum_tensor("p5_psT", [128, 4, 1024], BF16))

            dv_fw_read = [0, 0]
            pe_wn_read = [0, 0]
            tctr5 = [0]
            dv_psT5 = [("DV", 0), ("DV", 0), ("DV", 0), ("DV", 0)]
            pe_wT = [0, 0, 0, 0]
            dv_f_evac = [0, 0]
            pe_ft_read = [0, 0]
            tt_dma = [0, 0]
            dv_wT_of = [("DV", 0), ("DV", 0), ("DV", 0), ("DV", 0)]
            pe_mm5 = [0, 0]
            pend5 = [None]

            def emit_prep_a(oc):
                wbuf = oc % 2
                wb4 = oc % 4
                qw = "QW%d" % wbuf
                vw = dma("gpsimd", qw, fw_sb[:, wbuf, :], fw[oc * 128:(oc + 1) * 128, :],
                         deps=[("DV", dv_fw_read[wbuf])])
                vq = op("vector", "DV",
                        lambda e, oc=oc, wbuf=wbuf: e.tensor_scalar(
                            q_b[:], fw_sb[:, wbuf, :],
                            fws_sb[:, oc:oc + 1], 4.5, ALU.mult, ALU.add),
                        deps=[(qw, vw), ("QC", v_qc)])
                dv_fw_read[wbuf] = vq
                cur = q_b[:]
                vr = [("DV", vq), ("DV", v_lconst)]
                delta = 0.0
                for t in (8.0, 4.0, 2.0, 1.0):
                    vs = op("scalar", "SC",
                            lambda e, cur=cur: e.activation(
                                g_b[:], cur, AF.Sign, bias=cmap[-8.0]),
                            deps=vr)
                    vnew = op("vector", "DV",
                              lambda e, cur=cur, t=t: e.scalar_tensor_tensor(
                                  r_b[:], g_b[:], -t / 2.0, cur, ALU.mult, ALU.add),
                              deps=[("SC", vs)])
                    cur = r_b[:]
                    vr = [("DV", vnew)]
                    delta -= t / 2.0
                vwn = op("vector", "DV",
                         lambda e, delta=delta: e.scalar_tensor_tensor(
                             wn_b[:], r_b[:], delta + 4.0,
                             q_b[:], ALU.add, ALU.subtract),
                         deps=vr + [("PE", pe_wn_read[0])])
                pend = []
                pe_v = 0
                for grp in range(4):
                    hb = tctr5[0] % 4
                    tctr5[0] += 1
                    d0 = [("DV", vwn)] if grp == 0 else []
                    for k in range(8):
                        d_ = (d0 + [dv_psT5[hb]]) if k == 0 else []
                        pe_v = op("tensor", "PE",
                                  lambda e, grp=grp, k=k, hb=hb: e.transpose(
                                      psT[:, hb, k * 128:(k + 1) * 128],
                                      wn_b[:, (grp * 8 + k) * 128:(grp * 8 + k + 1) * 128],
                                      identb[:]),
                                  deps=d_)
                    pend.append((grp, hb, pe_v))
                pe_wn_read[0] = pe_v
                pend5[0] = (wb4, pend)

            def emit_prep_b():
                wb4, pend = pend5[0]
                for (grp, hb, pe_v) in pend:
                    sc_e = op("scalar", "SC",
                              lambda e, grp=grp, wb4=wb4, hb=hb: e.activation(
                                  wT_sb[:, wb4, grp * 8:(grp + 1) * 8, :],
                                  psT[:, hb, :], AF.Copy),
                              deps=[("PE", pe_v)])
                    dv_psT5[hb] = ("SC", sc_e)
                dv_wT_of[wb4] = ("SC", P.cnt["SC"])

            def emit_consume_mm(oc):
                wb4 = oc % 4
                pbuf = oc % 2
                first_deps = [dv_wT_of[wb4], ("DV", dv_f_evac[pbuf])]
                pe_last = 0
                for dc in range(DC):
                    for s in range(2):
                        pe_last = op("tensor", "PE",
                                     lambda e, dc=dc, s=s, pbuf=pbuf, wb4=wb4: e.matmul(
                                         psA[:, pbuf, s * 512:(s + 1) * 512],
                                         wT_sb[:, wb4, dc, :],
                                         xiT[:, dc, s * 512:(s + 1) * 512],
                                         start=(dc == 0), stop=(dc == DC - 1)),
                                     deps=(first_deps if (dc == 0 and s == 0) else []))
                pe_wT[wb4] = pe_last
                pe_mm5[pbuf] = pe_last

            def emit_consume_evac(oc):
                pbuf = oc % 2
                dv_e = op("vector", "DV",
                          lambda e, pbuf=pbuf, oc=oc: e.tensor_scalar(
                              ft_stage[:, pbuf, :], psA[:, pbuf, :],
                              fb_sb[:, oc:oc + 1], recip_sb[:, oc:oc + 1],
                              ALU.add, ALU.mult),
                          deps=[("PE", pe_mm5[pbuf]), ("DV", v_nrecip), ("DV", v_nfb),
                                ("PE", pe_ft_read[pbuf])])
                dv_f_evac[pbuf] = dv_e

            def emit_consume_tail(oc):
                pbuf = oc % 2
                dv_e = dv_f_evac[pbuf]
                hb = tctr5[0] % 4
                tctr5[0] += 1
                pe_t = 0
                for k in range(8):
                    d_ = ([("DV", dv_e), dv_psT5[hb]] if k == 0 else [])
                    pe_t = op("tensor", "PE",
                              lambda e, k=k, pbuf=pbuf, hb=hb: e.transpose(
                                  psT[:, hb, k * 128:(k + 1) * 128],
                                  ft_stage[:, pbuf, k * 128:(k + 1) * 128],
                                  identb[:]),
                              deps=d_)
                pe_ft_read[pbuf] = pe_t
                qv_s = ("QV", "QV2")[pbuf]
                sc_t = op("scalar", "SC",
                          lambda e, pbuf=pbuf, hb=hb: e.activation(
                              tt_stage[:, pbuf, :, :], psT[:, hb, :], AF.Copy),
                          deps=[("PE", pe_t), (qv_s, tt_dma[pbuf])])
                dv_psT5[hb] = ("SC", sc_t)
                tt_dma[pbuf] = dma(
                    "sync", qv_s,
                    ffn_d[:, oc * 128:(oc + 1) * 128].rearrange("(t p) n -> p t n", p=128),
                    tt_stage[:, pbuf, :, :],
                    deps=[("SC", sc_t)])

            emit_prep_a(0)
            emit_prep_b()
            emit_prep_a(1)
            emit_prep_b()
            for oc in range(DC):
                emit_consume_mm(oc)
                if oc + 2 < DC:
                    emit_prep_a(oc + 2)
                emit_consume_evac(oc)
                if oc + 2 < DC:
                    emit_prep_b()
                emit_consume_tail(oc)
            P.join()
            P.flush(block)

        xiT_cm.__exit__(None, None, None)

        with contextlib.ExitStack() as p5b:
            h2t = p5b.enter_context(nc.sbuf_tensor("p5b_h2", [128, 2, D], F32))
            ft_t = p5b.enter_context(nc.sbuf_tensor("p5b_ft", [128, 2, D], BF16))
            yt = p5b.enter_context(nc.sbuf_tensor("p5b_y", [128, D], F32))
            sq5 = p5b.enter_context(nc.sbuf_tensor("p5b_sq", [128, D], BF16))
            stt = {k: p5b.enter_context(nc.sbuf_tensor("p5b_s_" + k, [128, 1], F32))
                   for k in ("ssum", "ssq", "negm", "msq", "vare", "sd", "rstd")}
            stt = {k: v[:] for k, v in stt.items()}

            v_out_final = 0
            prev_apply = 0
            for qt in range(NQT):
                buf = qt % 2
                qx = "QX%d" % buf
                qk = "QK%d" % buf
                vh = dma("sync", qx, h2t[:, buf, :], h2_d[qt * 128:(qt + 1) * 128, :],
                         deps=[("DV", prev_apply)])
                vff = dma("sync", qk, ft_t[:, buf, :], ffn_d[qt * 128:(qt + 1) * 128, :],
                          deps=[("DV", prev_apply)])
                v_rstd4 = ln_stats(ft_t[:, buf, :], sq5[:], stt,
                                   deps_dv=[(qk, vff)],
                                   deps_sc=[(qk, vff), ("DV", prev_apply)])
                dv_y2 = op("vector", "DV",
                           lambda e, qt=qt, buf=buf: e.tensor_scalar(
                               yt[:], ft_t[:, buf, :], stt["negm"], stt["rstd"],
                               ALU.add, ALU.mult),
                           deps=[("DV", v_rstd4), ("YS", v_out_final)])
                dv_out = op("vector", "DV",
                            lambda e, buf=buf: e.tensor_tensor(yt[:], yt[:],
                                                               h2t[:, buf, :], ALU.add),
                            deps=[("DV", dv_y2), (qx, vh)])
                prev_apply = dv_out
                v_out_final = dma("sync", "YS", y[qt * 128:(qt + 1) * 128, :], yt[:],
                                  deps=[("DV", dv_out)])
            waiter("sync", "YS", v_out_final)
            P.flush(block)

    return nc


def kernel(**inputs):
    x = np.asarray(inputs["L0_attn_x"])
    wq_ = np.asarray(inputs["L0_attn_w_q"])
    bq_ = np.ascontiguousarray(np.asarray(inputs["L0_attn_b_q"]).reshape(DC, 128).T)
    wk_ = np.asarray(inputs["L0_attn_w_k"])
    bk_ = np.ascontiguousarray(np.asarray(inputs["L0_attn_b_k"]).reshape(KVH, 128).T)
    wv_ = np.asarray(inputs["L0_attn_w_v"])
    bv_ = np.ascontiguousarray(np.asarray(inputs["L0_attn_b_v"]).reshape(KVH, 128).T)
    wo_ = np.asarray(inputs["L0_attn_w_o"])
    bo_ = np.ascontiguousarray(np.asarray(inputs["L0_attn_b_o"]).reshape(DC, 128).T)
    fw_ = np.asarray(inputs["L0_ffn_weight"])
    fb_ = np.asarray(inputs["L0_ffn_bias"])
    fws_ = np.asarray(inputs["L0_ffn_weight_scale"])
    fis_ = np.asarray(inputs["L0_ffn_input_scale"])

    fb_pc = np.ascontiguousarray(fb_.reshape(DC, 128).T)
    fws_pc = np.ascontiguousarray(fws_.reshape(DC, 128).T)
    fis_bc = np.ascontiguousarray(np.broadcast_to(fis_.reshape(1, 1), (128, 1)))
    identm = np.eye(128, dtype=ml_dtypes.bfloat16)

    nc = build_program()
    in_maps = []
    for c in range(8):
        b = c // 4
        q0 = (c % 4) * OWN
        xr = np.ascontiguousarray(np.roll(x[b], -q0, axis=0))
        in_maps.append(dict(
            xb=xr, wq=wq_, wk=wk_, wv=wv_, wo=wo_, fw=fw_,
            bq_pc=bq_, bk_pc=bk_, bv_pc=bv_, bo_pc=bo_,
            fb_pc=fb_pc, fws_pc=fws_pc, fis_bc=fis_bc,
            ident=identm,
        ))
    res = run_bass_kernel_spmd(nc, in_maps, core_ids=list(range(8)))
    outs = [res.results[c]["y"] for c in range(8)]
    full = np.concatenate(outs, axis=0).reshape(2, T, D)
    return full.astype(np.float32)


# version 23
# speedup vs baseline: 1.0848x; 1.0848x over previous
import numpy as np
import ml_dtypes
import contextlib
import concourse.bass as bass
import concourse.mybir as mybir
from concourse.bass_utils import run_bass_kernel_spmd

F32 = mybir.dt.float32
BF16 = mybir.dt.bfloat16
AF = mybir.ActivationFunctionType
ALU = mybir.AluOpType
AX = mybir.AxisListType.X

D = 4096
T = 4096
QH = 32
KVH = 8
HD = 128
GQ = QH // KVH
OWN = 1024
NQT = OWN // 128
NBT = T // 128
DC = D // 128
HALF = D // 2
SCALE = float(HD) ** -0.5
EPS = 1e-5

ENGS = ("tensor", "vector", "scalar", "gpsimd", "sync")
SEMN = ("PE", "DV", "SC", "GP", "QC", "QX0", "QX1", "QK0", "QK1", "QV", "QV2",
        "QW0", "QW1", "QT0", "QT1", "QU0", "QU1", "QQ", "H2S", "YS")


class Prog:
    def __init__(self, sems):
        self.ops = {e: [] for e in ENGS}
        self.sems = sems
        self.cnt = {k: 0 for k in sems}

    def emit(self, eng, fn):
        self.ops[eng].append(fn)

    def wait(self, eng, deps):
        best = {}
        for s, v in deps:
            if v > 0 and v > best.get(s, 0):
                best[s] = v
        if best:
            deps2 = [(self.sems[s], v) for s, v in best.items()]
            def f(e, deps2=deps2):
                for h, v in deps2:
                    e.wait_ge(h, v)
            self.emit(eng, f)

    def join(self):
        s = dict(self.cnt)
        for eng in ENGS:
            self.wait(eng, [(k, s[k]) for k in SEMN])

    def flush(self, block):
        ops = self.ops
        self.ops = {e: [] for e in ENGS}
        if ops["gpsimd"]:
            @block.gpsimd
            def _(e):
                for f in ops["gpsimd"]:
                    f(e)
        if ops["tensor"]:
            @block.tensor
            def _(e):
                for f in ops["tensor"]:
                    f(e)
        if ops["vector"]:
            @block.vector
            def _(e):
                for f in ops["vector"]:
                    f(e)
        if ops["scalar"]:
            @block.scalar
            def _(e):
                for f in ops["scalar"]:
                    f(e)
        if ops["sync"]:
            @block.sync
            def _(e):
                for f in ops["sync"]:
                    f(e)


def build_program():
    nc = bass.Bass("TRN2", target_bir_lowering=False)

    xb = nc.dram_tensor("xb", [T, D], F32, kind="ExternalInput").ap()
    wq = nc.dram_tensor("wq", [D, D], F32, kind="ExternalInput").ap()
    wk = nc.dram_tensor("wk", [D, KVH * HD], F32, kind="ExternalInput").ap()
    wv = nc.dram_tensor("wv", [D, KVH * HD], F32, kind="ExternalInput").ap()
    wo = nc.dram_tensor("wo", [D, D], F32, kind="ExternalInput").ap()
    fw = nc.dram_tensor("fw", [D, D], F32, kind="ExternalInput").ap()
    bq_pc = nc.dram_tensor("bq_pc", [128, DC], F32, kind="ExternalInput").ap()
    bk_pc = nc.dram_tensor("bk_pc", [128, KVH], F32, kind="ExternalInput").ap()
    bv_pc = nc.dram_tensor("bv_pc", [128, KVH], F32, kind="ExternalInput").ap()
    bo_pc = nc.dram_tensor("bo_pc", [128, DC], F32, kind="ExternalInput").ap()
    fb_pc = nc.dram_tensor("fb_pc", [128, DC], F32, kind="ExternalInput").ap()
    fws_pc = nc.dram_tensor("fws_pc", [128, DC], F32, kind="ExternalInput").ap()
    fis_bc = nc.dram_tensor("fis_bc", [128, 1], F32, kind="ExternalInput").ap()
    ident = nc.dram_tensor("ident", [128, 128], BF16, kind="ExternalInput").ap()
    y = nc.dram_tensor("y", [OWN, D], F32, kind="ExternalOutput").ap()

    kT_d = nc.dram_tensor("kT_d", [KVH, 128, T], BF16, kind="Internal").ap()
    v_d = nc.dram_tensor("v_d", [NBT, 128, KVH * HD], BF16, kind="Internal").ap()
    h2_d = nc.dram_tensor("h2_d", [OWN, D], F32, kind="Internal").ap()

    es = contextlib.ExitStack()
    with es:
        identb = es.enter_context(nc.sbuf_tensor("identb", [128, 128], BF16))
        ones_col = es.enter_context(nc.sbuf_tensor("ones_col", [128, 1], BF16))
        ones_row_f = es.enter_context(nc.sbuf_tensor("ones_row_f", [1, 128], F32))
        fb_sb = es.enter_context(nc.sbuf_tensor("fb_sb", [128, DC], F32))
        fws_sb = es.enter_context(nc.sbuf_tensor("fws_sb", [128, DC], F32))
        nfws_sb = es.enter_context(nc.sbuf_tensor("nfws_sb", [128, DC], F32))
        fis_sb = es.enter_context(nc.sbuf_tensor("fis_sb", [128, 1], F32))
        recip_sb = es.enter_context(nc.sbuf_tensor("recip_sb", [128, DC], F32))
        wsis_sb = es.enter_context(nc.sbuf_tensor("wsis_sb", [128, DC], F32))
        lconst = es.enter_context(nc.sbuf_tensor("lconst", [128, 8], F32))
        bq_sb = es.enter_context(nc.sbuf_tensor("bq_sb", [128, DC], F32))
        bk_sb = es.enter_context(nc.sbuf_tensor("bk_sb", [128, KVH], F32))
        bv_sb = es.enter_context(nc.sbuf_tensor("bv_sb", [128, KVH], F32))
        bo_sb = es.enter_context(nc.sbuf_tensor("bo_sb", [128, DC], F32))

        sems = {}
        for name in SEMN:
            sems[name] = es.enter_context(nc.semaphore(name))
        block = es.enter_context(nc.Block())
        P = Prog(sems)

        def dma(queue_eng, sem_name, out, in_, deps=()):
            P.wait(queue_eng, deps)
            P.cnt[sem_name] += 16
            v = P.cnt[sem_name]
            h = sems[sem_name]
            P.emit(queue_eng, lambda e, out=out, in_=in_, h=h:
                   e.dma_start(out=out, in_=in_).then_inc(h, 16))
            return v

        def op(eng, sem_name, fn, deps=(), selfwait=True):
            if selfwait and eng in ("vector", "scalar", "gpsimd"):
                deps = list(deps) + [(sem_name, P.cnt[sem_name])]
            P.wait(eng, deps)
            P.cnt[sem_name] += 1
            v = P.cnt[sem_name]
            h = sems[sem_name]
            P.emit(eng, lambda e, fn=fn, h=h: fn(e).then_inc(h, 1))
            return v

        def waiter(eng, sem_name, v):
            if v > 0:
                P.emit(eng, lambda e, h=sems[sem_name], v=v: e.wait_ge(h, v))

        dma("sync", "QC", identb[:], ident)
        dma("sync", "QC", fb_sb[:], fb_pc)
        dma("sync", "QC", fws_sb[:], fws_pc)
        dma("sync", "QC", fis_sb[:], fis_bc)
        dma("sync", "QC", bq_sb[:], bq_pc)
        dma("sync", "QC", bk_sb[:], bk_pc)
        dma("sync", "QC", bv_sb[:], bv_pc)
        v_qc = dma("sync", "QC", bo_sb[:], bo_pc)
        op("vector", "DV", lambda e: e.memset(ones_col[:], 1.0))
        v_ones = op("vector", "DV", lambda e: e.memset(ones_row_f[:], 1.0))
        v_wsis = op("vector", "DV",
                    lambda e: e.tensor_single_scalar(wsis_sb[:], fws_sb[:], fis_sb[:], ALU.mult),
                    deps=[("QC", v_qc)])
        v_recip = op("vector", "DV", lambda e: e.reciprocal(recip_sb[:], wsis_sb[:]),
                     deps=[("DV", v_wsis)])
        v_nfws = op("vector", "DV",
                    lambda e: e.tensor_single_scalar(nfws_sb[:], fws_sb[:], -1.0, ALU.mult),
                    deps=[("QC", v_qc)])
        v_nfb = op("vector", "DV",
                   lambda e: e.tensor_single_scalar(fb_sb[:], fb_sb[:], -1.0, ALU.mult),
                   deps=[("QC", v_qc)])
        v_nrecip = op("vector", "DV",
                      lambda e: e.tensor_single_scalar(recip_sb[:], recip_sb[:], -1.0, ALU.mult),
                      deps=[("DV", v_recip)])
        P.wait("tensor", [("QC", v_qc)])

        def ladder_biases(kind, bits):
            delta = 0.0
            out = []
            for (t, engine) in bits:
                th = float(t) if kind == "floor" else float(t) - 0.5
                thr = th - delta
                if engine == "sc":
                    out.append(-thr)
                    delta -= float(t) / 2.0
            return out

        FLOOR_BITS = [(128.0, "sc"), (64.0, "sc"), (32.0, "sc"), (16.0, "sc"),
                      (8.0, "sc"), (4.0, "sc"), (2.0, "sc"), (1.0, "sc")]
        ROUND_BITS = [(4.0, "sc"), (2.0, "dve"), (1.0, "sc")]
        _cvals = ladder_biases("floor", FLOOR_BITS) + ladder_biases("round", ROUND_BITS) + [-8.0]
        _cvals = list(dict.fromkeys(_cvals))
        cmap = {}
        v_lconst = 0
        for _i, _v in enumerate(_cvals):
            v_lconst = op("vector", "DV",
                          lambda e, _i=_i, _v=_v: e.memset(lconst[:, _i:_i + 1], _v))
            cmap[_v] = lconst[:, _i:_i + 1]

        def ln_stats(xt_ap, sq_scr_ap, st, deps_dv, deps_sc):
            v1 = op("vector", "DV", lambda e: e.reduce_sum(st["ssum"], xt_ap, axis=AX),
                    deps=deps_dv)
            v2 = op("scalar", "SC",
                    lambda e: e.activation(sq_scr_ap, xt_ap, AF.Square, accum_out=st["ssq"]),
                    deps=deps_sc)
            op("vector", "DV",
               lambda e: e.tensor_single_scalar(st["negm"], st["ssum"], -1.0 / D, ALU.mult),
               deps=[("DV", v1)])
            op("vector", "DV",
               lambda e: e.tensor_scalar(st["vare"], st["ssq"], 1.0 / D, EPS, ALU.mult, ALU.add),
               deps=[("SC", v2)])
            op("vector", "DV",
               lambda e: e.tensor_tensor(st["msq"], st["negm"], st["negm"], ALU.mult))
            v5 = op("vector", "DV",
                    lambda e: e.tensor_tensor(st["vare"], st["vare"], st["msq"], ALU.subtract))
            v6 = op("scalar", "SC", lambda e: e.activation(st["sd"], st["vare"], AF.Sqrt),
                    deps=[("DV", v5)])
            v_rstd = op("vector", "DV", lambda e: e.reciprocal(st["rstd"], st["sd"]),
                        deps=[("SC", v6)])
            return v_rstd

        def emit_ladder(kind, a_ap, r_ap, g_ap, out_ap, bits, deps):
            delta = 0.0
            cur = a_ap
            vr = list(deps)
            for (t, engine) in bits:
                th = float(t) if kind == "floor" else float(t) - 0.5
                thr = th - delta
                if engine == "dve":
                    vg = op("vector", "DV",
                            lambda e, cur=cur, thr=thr, t=t: e.tensor_scalar(
                                g_ap, cur, float(thr), -float(t), ALU.is_ge, ALU.mult),
                            deps=vr)
                    vnew = op("vector", "DV",
                              lambda e, cur=cur: e.tensor_tensor(r_ap, cur, g_ap, ALU.add),
                              deps=[("DV", vg)])
                else:
                    bias_ap = cmap[-float(thr)]
                    vs = op("scalar", "SC",
                            lambda e, cur=cur, bias_ap=bias_ap: e.activation(
                                g_ap, cur, AF.Sign, bias=bias_ap),
                            deps=list(vr) + [("DV", v_lconst)])
                    vnew = op("vector", "DV",
                              lambda e, cur=cur, t=t: e.scalar_tensor_tensor(
                                  r_ap, g_ap, -float(t) / 2.0, cur, ALU.mult, ALU.add),
                              deps=[("SC", vs)])
                    delta -= float(t) / 2.0
                cur = r_ap
                vr = [("DV", vnew)]
            vfin = op("vector", "DV",
                      lambda e, delta=delta: e.scalar_tensor_tensor(
                          out_ap, r_ap, float(delta), a_ap, ALU.add, ALU.subtract),
                      deps=vr)
            return vfin

        qT_d = nc.dram_tensor("qT_d", [QH, 128, OWN], BF16, kind="Internal").ap()
        with contextlib.ExitStack() as p1:
            xt = p1.enter_context(nc.sbuf_tensor("p1_xt", [128, 2, D], F32))
            u_tok = p1.enter_context(nc.sbuf_tensor("p1_u", [128, D], BF16))
            uT_g = p1.enter_context(nc.sbuf_tensor("p1_uT", [128, 2, DC, 1024], BF16))
            w_sb = p1.enter_context(nc.sbuf_tensor("p1_w", [128, 2, DC, 256], BF16))
            kv_stage = p1.enter_context(nc.sbuf_tensor("p1_st", [128, 2, 1024], BF16))
            vt_stage = p1.enter_context(nc.sbuf_tensor("p1_vt", [128, 1, 8, 128], BF16))
            sttt = p1.enter_context(nc.sbuf_tensor("p1_stt", [128, 8], F32))
            stt = {k: sttt[:, i:i + 1] for i, k in enumerate(
                ("ssum", "ssq", "negm", "msq", "vare", "sd", "rstd"))}
            psA = p1.enter_context(nc.psum_tensor("p1_psA", [128, 2, 1024], F32))
            psT = p1.enter_context(nc.psum_tensor("p1_psT", [128, 2, 1024], BF16))

            st = dict(dv_u=[0, 0], pe_u=0, pe_w=[0, 0], dv_ev=[0, 0],
                      kv_dma=[0, 0], vt_dma=[0, 0], uT_rdy=[0, 0],
                      wctr=0, pctr=0, tctr=0, dv_psT=[0, 0])

            def emit_tile_ln(g, i):
                t = g * 8 + i
                buf = t % 2
                qx = "QX%d" % buf
                vx = dma("sync", qx, xt[:, buf, :], xb[t * 128:(t + 1) * 128, :],
                         deps=[("DV", st["dv_u"][buf])])
                v_rstd = ln_stats(xt[:, buf, :], u_tok[:], stt,
                                  deps_dv=[(qx, vx)],
                                  deps_sc=[(qx, vx), ("PE", st["pe_u"]),
                                           ("DV", st["dv_u"][1 - buf])])
                vu = op("vector", "DV",
                        lambda e, buf=buf: e.tensor_scalar(
                            u_tok[:], xt[:, buf, :], stt["negm"], stt["rstd"],
                            ALU.add, ALU.mult),
                        deps=[("DV", v_rstd), ("PE", st["pe_u"])])
                st["dv_u"][buf] = vu
                st["vu_p"] = vu

            def emit_tile_tp(g, i):
                gb = g % 2
                vu = st["vu_p"]
                last_dv = 0
                pe_v = 0
                for gg in range(4):
                    hb = st["tctr"] % 2
                    st["tctr"] += 1
                    for k in range(8):
                        dc = gg * 8 + k
                        d_ = ([("DV", vu)] if (gg == 0 and k == 0) else []) + \
                             ([("DV", st["dv_psT"][hb])] if k == 0 else [])
                        pe_v = op("tensor", "PE",
                                  lambda e, dc=dc, k=k, hb=hb: e.transpose(
                                      psT[:, hb, k * 128:(k + 1) * 128],
                                      u_tok[:, dc * 128:(dc + 1) * 128], identb[:]),
                                  deps=d_)
                    last_dv = op("vector", "DV",
                                 lambda e, gg=gg, gb=gb, i=i, hb=hb: e.tensor_copy(
                                     uT_g[:, gb, gg * 8:(gg + 1) * 8, i * 128:(i + 1) * 128],
                                     psT[:, hb, :]),
                                 deps=[("PE", pe_v)])
                    st["dv_psT"][hb] = last_dv
                st["pe_u"] = pe_v
                if i == 7:
                    st["uT_rdy"][gb] = last_dv

            def emit_tile(g, i):
                emit_tile_ln(g, i)
                emit_tile_tp(g, i)

            def emit_b(g, kind, oc2):
                gb = g % 2
                wbuf = st["wctr"] % 2
                st["wctr"] += 1
                if kind == "q":
                    wsrc, bias_src = wq, bq_sb
                elif kind == "k":
                    wsrc, bias_src = wk, bk_sb
                else:
                    wsrc, bias_src = wv, bv_sb
                qw = "QW%d" % wbuf
                vw = dma("gpsimd", qw, w_sb[:, wbuf, :, :],
                         wsrc[:, oc2 * 256:(oc2 + 1) * 256].rearrange("(c p) n -> p c n", p=128),
                         deps=[("PE", st["pe_w"][wbuf])])
                for sub in range(2):
                    oc = 2 * oc2 + sub
                    pbuf = st["pctr"] % 2
                    st["pctr"] += 1
                    first_deps = [(qw, vw), ("DV", st["dv_ev"][pbuf]),
                                  ("DV", st["uT_rdy"][gb])]
                    pe_last = 0
                    for dc in range(DC):
                        for s in range(2):
                            pe_last = op("tensor", "PE",
                                         lambda e, dc=dc, s=s, gb=gb, pbuf=pbuf, wbuf=wbuf, sub=sub: e.matmul(
                                             psA[:, pbuf, s * 512:(s + 1) * 512],
                                             w_sb[:, wbuf, dc, sub * 128:(sub + 1) * 128],
                                             uT_g[:, gb, dc, s * 512:(s + 1) * 512],
                                             start=(dc == 0), stop=(dc == DC - 1)),
                                         deps=(first_deps if (dc == 0 and s == 0) else []))
                    st["pe_w"][wbuf] = pe_last
                    dv_e = op("vector", "DV",
                              lambda e, pbuf=pbuf, bias_src=bias_src, oc=oc: e.tensor_single_scalar(
                                  kv_stage[:, pbuf, :], psA[:, pbuf, :],
                                  bias_src[:, oc:oc + 1], ALU.add),
                              deps=[("PE", pe_last), ("QK%d" % pbuf, st["kv_dma"][pbuf]),
                                    ("QC", v_qc)])
                    st["dv_ev"][pbuf] = dv_e
                    if kind == "q":
                        st["kv_dma"][pbuf] = dma("sync", "QK%d" % pbuf, qT_d[oc, :, :],
                                                 kv_stage[:, pbuf, :], deps=[("DV", dv_e)])
                    elif kind == "k":
                        st["kv_dma"][pbuf] = dma(
                            "sync", "QK%d" % pbuf,
                            kT_d[oc, :, g * 1024:(g + 1) * 1024],
                            kv_stage[:, pbuf, :], deps=[("DV", dv_e)])
                    else:
                        hb = st["tctr"] % 2
                        st["tctr"] += 1
                        pe_tv = 0
                        for k in range(8):
                            d_ = ([("DV", dv_e)] if k == 0 else []) + \
                                 ([("DV", st["dv_psT"][hb])] if k == 0 else [])
                            pe_tv = op("tensor", "PE",
                                       lambda e, k=k, pbuf=pbuf, hb=hb: e.transpose(
                                           psT[:, hb, k * 128:(k + 1) * 128],
                                           kv_stage[:, pbuf, k * 128:(k + 1) * 128],
                                           identb[:]),
                                       deps=d_)
                        st["kv_dma"][pbuf] = P.cnt["QK%d" % pbuf]
                        dv_v = op("vector", "DV",
                                  lambda e, hb=hb: e.tensor_copy(
                                      vt_stage[:, 0, :, :], psT[:, hb, :]),
                                  deps=[("PE", pe_tv), ("QV", max(st["vt_dma"]))])
                        st["dv_psT"][hb] = dv_v
                        kt0 = g * 8
                        st["vt_dma"][pbuf] = dma(
                            "sync", "QV",
                            v_d[kt0:kt0 + 8, :, oc * 128:(oc + 1) * 128]
                            .rearrange("t p n -> p t n"),
                            vt_stage[:, 0, :, :],
                            deps=[("DV", dv_v)])

            for i in range(8):
                emit_tile(0, i)
            for g in range(4):
                bunits = []
                if g == 0:
                    bunits += [("q", oc2) for oc2 in range(QH // 2)]
                bunits += [("k", j) for j in range(KVH // 2)]
                bunits += [("v", j) for j in range(KVH // 2)]
                aunits = list(range(8)) if g < 3 else []
                na, nb = len(aunits), len(bunits)
                a_ln = 0
                a_tp = 0
                for bi, (kind, oc) in enumerate(bunits):
                    if na and a_ln < na and a_ln == a_tp and \
                            (bi + 2) * na >= (a_ln + 1) * nb:
                        emit_tile_ln(g + 1, aunits[a_ln])
                        a_ln += 1
                    emit_b(g, kind, oc)
                    while na and a_tp < a_ln and (bi + 1) * na >= (a_tp + 1) * nb:
                        emit_tile_tp(g + 1, aunits[a_tp])
                        a_tp += 1
                while na and a_tp < na:
                    if a_ln == a_tp:
                        emit_tile_ln(g + 1, aunits[a_ln])
                        a_ln += 1
                    emit_tile_tp(g + 1, aunits[a_tp])
                    a_tp += 1
            P.join()
            P.flush(block)

        AT_cm = nc.sbuf_tensor("AT", [128, QH, OWN], BF16, side="right")
        AT = AT_cm.__enter__()
        with contextlib.ExitStack() as p3:
            qT = p3.enter_context(nc.sbuf_tensor("p3_qT", [128, QH, OWN], BF16))
            kT_sb = p3.enter_context(nc.sbuf_tensor("p3_kT", [128, 2, T], BF16))
            v_tok = p3.enter_context(nc.sbuf_tensor("p3_v", [128, 2, NBT, 128], BF16))
            ptile = p3.enter_context(nc.sbuf_tensor("p3_p", [128, 2, 1024], BF16))
            sqs = p3.enter_context(nc.sbuf_tensor("p3_sqs", [128, T], BF16))
            mk_sb = p3.enter_context(nc.sbuf_tensor("p3_mk", [1, 16], F32))
            cb = p3.enter_context(nc.sbuf_tensor("p3_cb", [128, 4], F32))
            sig_sb = p3.enter_context(nc.sbuf_tensor("p3_sig", [1, 1024], F32))
            rs_sb = p3.enter_context(nc.sbuf_tensor("p3_rs", [1, 1024], F32))
            bc_sb = p3.enter_context(nc.sbuf_tensor("p3_bc", [128, 1024], F32))
            psS = p3.enter_context(nc.psum_tensor("p3_psS", [128, 2, 1024], F32))
            psA_ = p3.enter_context(nc.psum_tensor("p3_psA", [128, 1024], F32))
            psG = p3.enter_context(nc.psum_tensor("p3_psG", [128, 1024], F32))

            v_qt = dma("sync", "QQ", qT[:, :, :], qT_d.rearrange("o p n -> p o n"))

            pe_group_done = [0, 0]
            pe_sqs_read = 0
            dv_head_tail = 0
            sc_exp = [0, 0]
            pe_pv_sig = [0, 0]
            for j in range(KVH):
                gbuf = j % 2
                qt_s = "QT%d" % gbuf
                qu_s = "QU%d" % gbuf
                vk = dma("sync", qt_s, kT_sb[:, gbuf, :], kT_d[j, :, :],
                         deps=[("PE", pe_group_done[gbuf])])
                vv = 0
                for q4 in range(4):
                    vv = dma("sync", qu_s,
                             v_tok[:, gbuf, q4 * 8:(q4 + 1) * 8, :],
                             v_d[q4 * 8:(q4 + 1) * 8, :, j * 128:(j + 1) * 128]
                             .rearrange("t p n -> p t n"))
                dv_sq = op("vector", "DV",
                           lambda e, gbuf=gbuf: e.tensor_tensor(
                               sqs[:], kT_sb[:, gbuf, :], kT_sb[:, gbuf, :], ALU.mult),
                           deps=[(qt_s, vk), ("PE", pe_sqs_read)])
                dvm = [0, 0]
                for s in range(8):
                    slot = s % 2
                    d_ = [("DV", dv_sq)] if s == 0 else []
                    waiter("tensor", "DV", dvm[slot])
                    pe_n = op("tensor", "PE",
                              lambda e, s=s, slot=slot: e.matmul(
                                  psG[0:1, slot * 512:(slot + 1) * 512],
                                  ones_col[:], sqs[:, s * 512:(s + 1) * 512],
                                  start=True, stop=True),
                              deps=d_)
                    dvm[slot] = op("vector", "DV",
                                   lambda e, s=s, slot=slot: e.reduce_max(
                                       mk_sb[0:1, s:s + 1],
                                       psG[0:1, slot * 512:(slot + 1) * 512], axis=AX),
                                   deps=[("PE", pe_n)])
                qu_s2, vv2 = qu_s, vv
                op("vector", "DV",
                   lambda e: e.reduce_max(mk_sb[0:1, 8:9], mk_sb[0:1, 0:8], axis=AX))

                dv_cb = 0
                for hh in range(GQ):
                    h = j * GQ + hh
                    dv_qsq = op("vector", "DV",
                                lambda e, h=h: e.tensor_tensor(
                                    sqs[:, 0:1024], qT[:, h, :], qT[:, h, :], ALU.mult),
                                deps=[("PE", pe_sqs_read), ("QQ", v_qt)])
                    pe_n = 0
                    for s in range(2):
                        pe_n = op("tensor", "PE",
                                  lambda e, s=s: e.matmul(
                                      psG[0:1, s * 512:(s + 1) * 512],
                                      ones_col[:], sqs[:, s * 512:(s + 1) * 512],
                                      start=True, stop=True),
                                  deps=([("DV", dv_qsq)] if s == 0 else []))
                    pe_sqs_read = pe_n
                    op("vector", "DV",
                       lambda e: e.reduce_max(mk_sb[0:1, 9:10], psG[0:1, 0:512], axis=AX),
                       deps=[("PE", pe_n)])
                    op("vector", "DV",
                       lambda e: e.reduce_max(mk_sb[0:1, 10:11], psG[0:1, 512:1024], axis=AX))
                    op("vector", "DV",
                       lambda e: e.reduce_max(mk_sb[0:1, 11:12], mk_sb[0:1, 9:11], axis=AX))
                    dv_c1 = op("vector", "DV",
                               lambda e: e.tensor_tensor(mk_sb[0:1, 12:13], mk_sb[0:1, 11:12],
                                                         mk_sb[0:1, 8:9], ALU.add))
                    dv_c3 = op("vector", "DV",
                               lambda e: e.tensor_single_scalar(
                                   mk_sb[0:1, 14:15], mk_sb[0:1, 12:13], -SCALE / 2.0, ALU.mult),
                               deps=[("DV", dv_c1)])
                    pe_cb = op("tensor", "PE",
                               lambda e: e.matmul(psG[:, 0:1], ones_row_f[0:1, :],
                                                  mk_sb[0:1, 14:15], start=True, stop=True),
                               deps=[("DV", dv_c3)])
                    dv_cb = op("vector", "DV",
                               lambda e, hh=hh: e.tensor_copy(cb[:, hh:hh + 1], psG[:, 0:1]),
                               deps=[("PE", pe_cb)])
                waiter("tensor", "DV", dv_cb)
                dv_b_prev = 0

                for hh in range(GQ):
                    h = j * GQ + hh
                    def emit_s(kc):
                        sb_ = kc % 2
                        pe_s = 0
                        for s in range(2):
                            d_ = ([("SC", sc_exp[sb_])] if s == 0 else [])
                            pe_s = op("tensor", "PE",
                                      lambda e, kc=kc, s=s, sb_=sb_, gbuf=gbuf, h=h: e.matmul(
                                          psS[:, sb_, s * 512:(s + 1) * 512],
                                          kT_sb[:, gbuf, kc * 128:(kc + 1) * 128],
                                          qT[:, h, s * 512:(s + 1) * 512],
                                          start=True, stop=True),
                                      deps=d_)
                        return pe_s

                    def emit_exp(kc, pe_s):
                        sb_ = kc % 2
                        d_exp = [("PE", pe_s), ("PE", pe_pv_sig[sb_])]
                        if kc == 0:
                            d_exp.append(("DV", dv_cb))
                        vexp = op("scalar", "SC",
                                  lambda e, sb_=sb_, hh=hh: e.activation(
                                      ptile[:, sb_, :], psS[:, sb_, :], AF.Exp,
                                      bias=cb[:, hh:hh + 1], scale=SCALE),
                                  deps=d_exp, selfwait=False)
                        sc_exp[sb_] = vexp
                        return vexp

                    def emit_pv(kc, vexp):
                        sb_ = kc % 2
                        for s in range(2):
                            d_ = []
                            if s == 0:
                                d_ = [("SC", vexp)]
                                if kc == 0:
                                    d_ += [(qu_s2, vv2), ("DV", dv_head_tail)]
                            op("tensor", "PE",
                               lambda e, kc=kc, s=s, sb_=sb_, gbuf=gbuf: e.matmul(
                                   psA_[:, s * 512:(s + 1) * 512],
                                   v_tok[:, gbuf, kc, :],
                                   ptile[:, sb_, s * 512:(s + 1) * 512],
                                   start=(kc == 0), stop=(kc == NBT - 1)),
                               deps=d_)
                        pe_g = 0
                        for s in range(2):
                            d2_ = ([("DV", dv_b_prev)] if (kc == 0 and s == 0) else [])
                            pe_g = op("tensor", "PE",
                                      lambda e, kc=kc, s=s, sb_=sb_: e.matmul(
                                          psG[0:1, s * 512:(s + 1) * 512],
                                          ones_col[:],
                                          ptile[:, sb_, s * 512:(s + 1) * 512],
                                          start=(kc == 0), stop=(kc == NBT - 1)),
                                      deps=d2_)
                        pe_pv_sig[sb_] = pe_g

                    pe_s_arr = {0: emit_s(0), 1: emit_s(1)}
                    for kc in range(NBT):
                        vexp = emit_exp(kc, pe_s_arr[kc])
                        if kc + 2 < NBT:
                            pe_s_arr[kc + 2] = emit_s(kc + 2)
                        emit_pv(kc, vexp)
                    pe_head_mms = P.cnt["PE"]
                    dv_s2 = op("vector", "DV",
                               lambda e: e.reciprocal(rs_sb[:], psG[0:1, :]),
                               deps=[("PE", pe_head_mms)])
                    pe_b = 0
                    for s in range(2):
                        d_ = ([("DV", dv_s2)] if s == 0 else [])
                        pe_b = op("tensor", "PE",
                                  lambda e, s=s: e.matmul(
                                      psG[:, s * 512:(s + 1) * 512],
                                      ones_row_f[0:1, :],
                                      rs_sb[0:1, s * 512:(s + 1) * 512],
                                      start=True, stop=True),
                                  deps=d_)
                    dv_b = op("vector", "DV",
                              lambda e: e.tensor_copy(bc_sb[:], psG[:, :]),
                              deps=[("PE", pe_b)])
                    dv_b_prev = dv_b
                    dv_n = op("vector", "DV",
                              lambda e, h=h: e.tensor_tensor(
                                  AT[:, h, :], psA_[:], bc_sb[:], ALU.mult),
                              deps=[("DV", dv_b)])
                    dv_head_tail = dv_n
                pe_group_done[gbuf] = P.cnt["PE"]
            P.join()
            P.flush(block)

        big_cm = nc.sbuf_tensor("big_tok", [128, NQT, D], BF16)
        big_tok = big_cm.__enter__()
        with contextlib.ExitStack() as p4:
            wo_sb = p4.enter_context(nc.sbuf_tensor("p4_wo", [128, 2, DC, 128], BF16))
            ot_stage = p4.enter_context(nc.sbuf_tensor("p4_ot", [128, 2, 1024], BF16))
            psA = p4.enter_context(nc.psum_tensor("p4_psA", [128, 2, 1024], F32))
            psT = p4.enter_context(nc.psum_tensor("p4_psT", [128, 2, 1024], BF16))

            pe_o = [0, 0]
            dv_o_evac = [0, 0]
            pe_ot_read = [0, 0]
            dv_psT = [0, 0]
            for oc in range(DC):
                wbuf = oc % 2
                qw = "QW%d" % wbuf
                vw = dma("gpsimd", qw, wo_sb[:, wbuf, :, :],
                         wo[:, oc * 128:(oc + 1) * 128].rearrange("(c p) n -> p c n", p=128),
                         deps=[("PE", pe_o[wbuf])])
                pbuf = oc % 2
                first_deps = [(qw, vw), ("DV", dv_o_evac[pbuf])]
                pe_last = 0
                for dc in range(DC):
                    for s in range(2):
                        pe_last = op("tensor", "PE",
                                     lambda e, dc=dc, s=s, pbuf=pbuf, wbuf=wbuf: e.matmul(
                                         psA[:, pbuf, s * 512:(s + 1) * 512],
                                         wo_sb[:, wbuf, dc, :],
                                         AT[:, dc, s * 512:(s + 1) * 512],
                                         start=(dc == 0), stop=(dc == DC - 1)),
                                     deps=(first_deps if (dc == 0 and s == 0) else []))
                pe_o[wbuf] = pe_last
                dv_e = op("vector", "DV",
                          lambda e, pbuf=pbuf, oc=oc: e.tensor_single_scalar(
                              ot_stage[:, pbuf, :], psA[:, pbuf, :],
                              bo_sb[:, oc:oc + 1], ALU.add),
                          deps=[("PE", pe_last), ("PE", pe_ot_read[pbuf]),
                                ("QC", v_qc)])
                dv_o_evac[pbuf] = dv_e
                hb = oc % 2
                pe_t = 0
                for k in range(8):
                    d_ = ([("DV", dv_e), ("DV", dv_psT[hb])] if k == 0 else [])
                    pe_t = op("tensor", "PE",
                              lambda e, k=k, pbuf=pbuf, hb=hb: e.transpose(
                                  psT[:, hb, k * 128:(k + 1) * 128],
                                  ot_stage[:, pbuf, k * 128:(k + 1) * 128],
                                  identb[:]),
                              deps=d_)
                pe_ot_read[pbuf] = pe_t
                dv_t = op("vector", "DV",
                          lambda e, oc=oc, hb=hb: e.tensor_copy(
                              big_tok[:, :, oc * 128:(oc + 1) * 128], psT[:, hb, :]),
                          deps=[("PE", pe_t)])
                dv_psT[hb] = dv_t
            P.join()
            P.flush(block)

        AT_cm.__exit__(None, None, None)

        xiT_cm = nc.sbuf_tensor("xiT", [128, DC, OWN], BF16, side="right")
        xiT = xiT_cm.__enter__()
        with contextlib.ExitStack() as p4b:
            xtile = p4b.enter_context(nc.sbuf_tensor("p4b_x", [128, D], F32))
            h2t = p4b.enter_context(nc.sbuf_tensor("p4b_h2", [128, D], F32))
            au = p4b.enter_context(nc.sbuf_tensor("p4b_au", [128, D], F32))
            xs_h = p4b.enter_context(nc.sbuf_tensor("p4b_xs", [128, 2, HALF], BF16))
            r_h = p4b.enter_context(nc.sbuf_tensor("p4b_r", [128, 2, HALF], F32))
            stt = {k: p4b.enter_context(nc.sbuf_tensor("p4b_s_" + k, [128, 1], F32))
                   for k in ("ssum", "ssq", "negm", "msq", "vare", "sd", "rstd")}
            stt = {k: v[:] for k, v in stt.items()}
            rstd3f = p4b.enter_context(nc.sbuf_tensor("p4b_r3f", [128, 1], F32))
            psT = p4b.enter_context(nc.psum_tensor("p4b_psT", [128, 2, 1024], BF16))

            pe_xi_read_ = [0, 0]
            dv_psT4 = [("DV", 0), ("DV", 0)]
            for qt in range(NQT):
                P.join()
                vx = dma("sync", "QX0", xtile[:], xb[qt * 128:(qt + 1) * 128, :])
                v_rstd2 = ln_stats(big_tok[:, qt, :], au[:], stt, deps_dv=[], deps_sc=[])
                dv_y1 = op("vector", "DV",
                           lambda e, qt=qt: e.tensor_scalar(
                               h2t[:], big_tok[:, qt, :], stt["negm"], stt["rstd"],
                               ALU.add, ALU.mult),
                           deps=[("DV", v_rstd2)])
                dv_h2 = op("vector", "DV",
                           lambda e: e.tensor_tensor(h2t[:], h2t[:], xtile[:], ALU.add),
                           deps=[("DV", dv_y1), ("QX0", vx)])
                dma("sync", "H2S", h2_d[qt * 128:(qt + 1) * 128, :], h2t[:],
                    deps=[("DV", dv_h2)])
                v_rstd3 = ln_stats(h2t[:], au[:], stt,
                                   deps_dv=[("DV", dv_h2)],
                                   deps_sc=[("DV", dv_h2), ("DV", dv_y1)])
                dv_r3f = op("vector", "DV",
                            lambda e: e.tensor_tensor(rstd3f[:], stt["rstd"], fis_sb[:],
                                                      ALU.mult),
                            deps=[("DV", v_rstd3)])
                dv_xs_ = [0, 0]
                for dh in range(2):
                    dv_xs_[dh] = op("vector", "DV",
                                    lambda e, dh=dh: e.tensor_scalar(
                                        r_h[:, dh, :], h2t[:, dh * HALF:(dh + 1) * HALF],
                                        stt["negm"], rstd3f[:], ALU.add, ALU.mult),
                                    deps=[("DV", dv_r3f)])
                sc_a_ = [0, 0]
                sc_ns_ = [0, 0]
                for dh in range(2):
                    sc_a_[dh] = op("scalar", "SC",
                                   lambda e, dh=dh: e.activation(
                                       au[:, dh * HALF:(dh + 1) * HALF],
                                       r_h[:, dh, :], AF.Abs),
                                   deps=[("DV", dv_xs_[dh])])
                ns_ap = [big_tok[:, qt, dh * HALF:(dh + 1) * HALF] for dh in range(2)]
                for dh in range(2):
                    sc_ns_[dh] = op("scalar", "SC",
                                    lambda e, dh=dh, ns=ns_ap[dh]: e.activation(
                                        ns, r_h[:, dh, :], AF.Sign, scale=-1.0),
                                    deps=[("DV", dv_xs_[dh])])
                cur_ = [au[:, 0:HALF], au[:, HALF:2 * HALF]]
                vr_ = [[("SC", sc_a_[0]), ("SC", sc_ns_[0]), ("DV", v_lconst),
                        ("PE", pe_xi_read_[0])],
                       [("SC", sc_a_[1]), ("SC", sc_ns_[1]), ("DV", v_lconst),
                        ("PE", pe_xi_read_[1])]]
                delta = 0.0
                for (t, engine) in FLOOR_BITS:
                    th = float(t)
                    thr = th - delta
                    vc_ = [0, 0]
                    if engine == "dve":
                        for dh in range(2):
                            vc_[dh] = op("vector", "DV",
                                         lambda e, cur=cur_[dh], thr=thr, t=t, dh=dh:
                                         e.tensor_scalar(
                                             xs_h[:, dh, :], cur, float(thr), -float(t),
                                             ALU.is_ge, ALU.mult),
                                         deps=vr_[dh])
                        for dh in range(2):
                            vnew = op("vector", "DV",
                                      lambda e, cur=cur_[dh], dh=dh:
                                      e.tensor_tensor(
                                          r_h[:, dh, :], cur, xs_h[:, dh, :], ALU.add),
                                      deps=[("DV", vc_[dh])])
                            cur_[dh] = r_h[:, dh, :]
                            vr_[dh] = [("DV", vnew)]
                    else:
                        bias_ap = cmap[-float(thr)]
                        for dh in range(2):
                            vc_[dh] = op("scalar", "SC",
                                         lambda e, cur=cur_[dh], bias_ap=bias_ap, dh=dh:
                                         e.activation(
                                             xs_h[:, dh, :], cur, AF.Sign, bias=bias_ap),
                                         deps=vr_[dh])
                        for dh in range(2):
                            vnew = op("vector", "DV",
                                      lambda e, cur=cur_[dh], t=t, dh=dh:
                                      e.scalar_tensor_tensor(
                                          r_h[:, dh, :], xs_h[:, dh, :], -float(t) / 2.0,
                                          cur, ALU.mult, ALU.add),
                                      deps=[("SC", vc_[dh])])
                            cur_[dh] = r_h[:, dh, :]
                            vr_[dh] = [("DV", vnew)]
                        delta -= float(t) / 2.0
                dv_xi_ = [0, 0]
                for dh in range(2):
                    vfin = op("vector", "DV",
                              lambda e, delta=delta, dh=dh: e.scalar_tensor_tensor(
                                  xs_h[:, dh, :], r_h[:, dh, :], float(delta),
                                  au[:, dh * HALF:(dh + 1) * HALF], ALU.add, ALU.subtract),
                              deps=vr_[dh])
                    dv_xi_[dh] = op("vector", "DV",
                                    lambda e, dh=dh, ns=ns_ap[dh]: e.tensor_tensor(
                                        xs_h[:, dh, :], xs_h[:, dh, :], ns,
                                        ALU.mult),
                                    deps=[("DV", vfin)])
                for dh in range(2):
                    for g in range(2):
                        hb = g
                        d0 = [("DV", dv_xi_[dh])] if g == 0 else []
                        for k in range(8):
                            d_ = (d0 + [dv_psT4[hb]]) if k == 0 else []
                            pe_v = op("tensor", "PE",
                                      lambda e, g=g, k=k, hb=hb, dh=dh: e.transpose(
                                          psT[:, hb, k * 128:(k + 1) * 128],
                                          xs_h[:, dh, (g * 8 + k) * 128:(g * 8 + k + 1) * 128],
                                          identb[:]),
                                      deps=d_)
                        sc_e = op("scalar", "SC",
                                  lambda e, g=g, dh=dh, qt=qt, hb=hb: e.activation(
                                      xiT[:, dh * 16 + g * 8:dh * 16 + (g + 1) * 8,
                                          qt * 128:(qt + 1) * 128],
                                      psT[:, hb, :], AF.Copy),
                                  deps=[("PE", pe_v)])
                        dv_psT4[hb] = ("SC", sc_e)
                    pe_xi_read_[dh] = pe_v
            P.join()
            P.flush(block)

        big_cm.__exit__(None, None, None)

        ffn_d = nc.dram_tensor("ffn_d", [OWN, D], BF16, kind="Internal").ap()
        with contextlib.ExitStack() as p5:
            fw_sb = p5.enter_context(nc.sbuf_tensor("p5_fw", [128, 2, D], F32))
            q_b = p5.enter_context(nc.sbuf_tensor("p5_q", [128, 2, HALF], F32))
            r_b = p5.enter_context(nc.sbuf_tensor("p5_r", [128, 2, HALF], F32))
            g_b = p5.enter_context(nc.sbuf_tensor("p5_g", [128, 2, HALF], F32))
            wn_b = p5.enter_context(nc.sbuf_tensor("p5_wn", [128, 2, HALF], BF16))
            wT_sb = p5.enter_context(nc.sbuf_tensor("p5_wT", [128, 4, DC, 128], BF16))
            ft_stage = p5.enter_context(nc.sbuf_tensor("p5_ft", [128, 2, 1024], BF16))
            tt_stage = p5.enter_context(nc.sbuf_tensor("p5_tt", [128, 2, 8, 128], BF16))
            psA = p5.enter_context(nc.psum_tensor("p5_psA", [128, 2, 1024], F32))
            psT = p5.enter_context(nc.psum_tensor("p5_psT", [128, 4, 1024], BF16))

            dv_fw_read = [0, 0]
            pe_wn_read = [0, 0]
            tctr5 = [0]
            dv_psT5 = [("DV", 0), ("DV", 0), ("DV", 0), ("DV", 0)]
            pe_wT = [0, 0, 0, 0]
            dv_f_evac = [0, 0]
            pe_ft_read = [0, 0]
            tt_dma = [0, 0]
            dv_wT_of = [("DV", 0), ("DV", 0), ("DV", 0), ("DV", 0)]
            pe_mm5 = [0, 0]
            pend5 = [None]

            def emit_prep_a(oc):
                wbuf = oc % 2
                wb4 = oc % 4
                qw = "QW%d" % wbuf
                vw = dma("gpsimd", qw, fw_sb[:, wbuf, :], fw[oc * 128:(oc + 1) * 128, :],
                         deps=[("DV", dv_fw_read[wbuf])])
                vq_ = [0, 0]
                for par in range(2):
                    vq_[par] = op("vector", "DV",
                                  lambda e, oc=oc, par=par, wbuf=wbuf: e.tensor_scalar(
                                      q_b[:, par, :], fw_sb[:, wbuf, par * HALF:(par + 1) * HALF],
                                      fws_sb[:, oc:oc + 1], 4.5, ALU.mult, ALU.add),
                                  deps=[(qw, vw), ("QC", v_qc)])
                dv_fw_read[wbuf] = vq_[1]
                cur_ = [q_b[:, 0, :], q_b[:, 1, :]]
                vr_ = [[("DV", vq_[0]), ("DV", v_lconst)], [("DV", vq_[1]), ("DV", v_lconst)]]
                delta = 0.0
                for t in (8.0, 4.0, 2.0, 1.0):
                    vs_ = [0, 0]
                    for par in range(2):
                        vs_[par] = op("scalar", "SC",
                                      lambda e, cur=cur_[par], par=par: e.activation(
                                          g_b[:, par, :], cur, AF.Sign, bias=cmap[-8.0]),
                                      deps=vr_[par])
                    for par in range(2):
                        vnew = op("vector", "DV",
                                  lambda e, cur=cur_[par], t=t, par=par: e.scalar_tensor_tensor(
                                      r_b[:, par, :], g_b[:, par, :], -t / 2.0, cur,
                                      ALU.mult, ALU.add),
                                  deps=[("SC", vs_[par])])
                        cur_[par] = r_b[:, par, :]
                        vr_[par] = [("DV", vnew)]
                    delta -= t / 2.0
                vwn_ = [0, 0]
                for par in range(2):
                    vwn_[par] = op("vector", "DV",
                                   lambda e, par=par, delta=delta: e.scalar_tensor_tensor(
                                       wn_b[:, par, :], r_b[:, par, :], delta + 4.0,
                                       q_b[:, par, :], ALU.add, ALU.subtract),
                                   deps=vr_[par] + [("PE", pe_wn_read[par])])
                pend = []
                for par in range(2):
                    pe_v = 0
                    for g in range(2):
                        hb = tctr5[0] % 4
                        tctr5[0] += 1
                        d0 = [("DV", vwn_[par])] if g == 0 else []
                        for k in range(8):
                            d_ = (d0 + [dv_psT5[hb]]) if k == 0 else []
                            pe_v = op("tensor", "PE",
                                      lambda e, g=g, k=k, par=par, hb=hb: e.transpose(
                                          psT[:, hb, k * 128:(k + 1) * 128],
                                          wn_b[:, par, (g * 8 + k) * 128:(g * 8 + k + 1) * 128],
                                          identb[:]),
                                      deps=d_)
                        pend.append((par, g, hb, pe_v))
                    pe_wn_read[par] = pe_v
                pend5[0] = (wb4, pend)

            def emit_prep_b():
                wb4, pend = pend5[0]
                for (par, g, hb, pe_v) in pend:
                    sc_e = op("scalar", "SC",
                              lambda e, g=g, par=par, wb4=wb4, hb=hb: e.activation(
                                  wT_sb[:, wb4, par * 16 + g * 8:par * 16 + (g + 1) * 8, :],
                                  psT[:, hb, :], AF.Copy),
                              deps=[("PE", pe_v)])
                    dv_psT5[hb] = ("SC", sc_e)
                dv_wT_of[wb4] = ("SC", P.cnt["SC"])

            def emit_consume_mm(oc):
                wb4 = oc % 4
                pbuf = oc % 2
                first_deps = [dv_wT_of[wb4], ("DV", dv_f_evac[pbuf])]
                pe_last = 0
                for dc in range(DC):
                    for s in range(2):
                        pe_last = op("tensor", "PE",
                                     lambda e, dc=dc, s=s, pbuf=pbuf, wb4=wb4: e.matmul(
                                         psA[:, pbuf, s * 512:(s + 1) * 512],
                                         wT_sb[:, wb4, dc, :],
                                         xiT[:, dc, s * 512:(s + 1) * 512],
                                         start=(dc == 0), stop=(dc == DC - 1)),
                                     deps=(first_deps if (dc == 0 and s == 0) else []))
                pe_wT[wb4] = pe_last
                pe_mm5[pbuf] = pe_last

            def emit_consume_evac(oc):
                pbuf = oc % 2
                dv_e = op("vector", "DV",
                          lambda e, pbuf=pbuf, oc=oc: e.tensor_scalar(
                              ft_stage[:, pbuf, :], psA[:, pbuf, :],
                              fb_sb[:, oc:oc + 1], recip_sb[:, oc:oc + 1],
                              ALU.add, ALU.mult),
                          deps=[("PE", pe_mm5[pbuf]), ("DV", v_nrecip), ("DV", v_nfb),
                                ("PE", pe_ft_read[pbuf])])
                dv_f_evac[pbuf] = dv_e

            def emit_consume_tail(oc):
                pbuf = oc % 2
                dv_e = dv_f_evac[pbuf]
                hb = tctr5[0] % 4
                tctr5[0] += 1
                pe_t = 0
                for k in range(8):
                    d_ = ([("DV", dv_e), dv_psT5[hb]] if k == 0 else [])
                    pe_t = op("tensor", "PE",
                              lambda e, k=k, pbuf=pbuf, hb=hb: e.transpose(
                                  psT[:, hb, k * 128:(k + 1) * 128],
                                  ft_stage[:, pbuf, k * 128:(k + 1) * 128],
                                  identb[:]),
                              deps=d_)
                pe_ft_read[pbuf] = pe_t
                qv_s = ("QV", "QV2")[pbuf]
                sc_t = op("scalar", "SC",
                          lambda e, pbuf=pbuf, hb=hb: e.activation(
                              tt_stage[:, pbuf, :, :], psT[:, hb, :], AF.Copy),
                          deps=[("PE", pe_t), (qv_s, tt_dma[pbuf])])
                dv_psT5[hb] = ("SC", sc_t)
                tt_dma[pbuf] = dma(
                    "sync", qv_s,
                    ffn_d[:, oc * 128:(oc + 1) * 128].rearrange("(t p) n -> p t n", p=128),
                    tt_stage[:, pbuf, :, :],
                    deps=[("SC", sc_t)])

            emit_prep_a(0)
            emit_prep_b()
            emit_prep_a(1)
            emit_prep_b()
            for oc in range(DC):
                emit_consume_mm(oc)
                if oc + 2 < DC:
                    emit_prep_a(oc + 2)
                emit_consume_evac(oc)
                if oc + 2 < DC:
                    emit_prep_b()
                emit_consume_tail(oc)
            P.join()
            P.flush(block)

        xiT_cm.__exit__(None, None, None)

        with contextlib.ExitStack() as p5b:
            h2t = p5b.enter_context(nc.sbuf_tensor("p5b_h2", [128, 2, D], F32))
            ft_t = p5b.enter_context(nc.sbuf_tensor("p5b_ft", [128, 2, D], BF16))
            yt = p5b.enter_context(nc.sbuf_tensor("p5b_y", [128, D], F32))
            sq5 = p5b.enter_context(nc.sbuf_tensor("p5b_sq", [128, D], BF16))
            stt = {k: p5b.enter_context(nc.sbuf_tensor("p5b_s_" + k, [128, 1], F32))
                   for k in ("ssum", "ssq", "negm", "msq", "vare", "sd", "rstd")}
            stt = {k: v[:] for k, v in stt.items()}

            v_out_final = 0
            prev_apply = 0
            for qt in range(NQT):
                buf = qt % 2
                qx = "QX%d" % buf
                qk = "QK%d" % buf
                vh = dma("sync", qx, h2t[:, buf, :], h2_d[qt * 128:(qt + 1) * 128, :],
                         deps=[("DV", prev_apply)])
                vff = dma("sync", qk, ft_t[:, buf, :], ffn_d[qt * 128:(qt + 1) * 128, :],
                          deps=[("DV", prev_apply)])
                v_rstd4 = ln_stats(ft_t[:, buf, :], sq5[:], stt,
                                   deps_dv=[(qk, vff)],
                                   deps_sc=[(qk, vff), ("DV", prev_apply)])
                dv_y2 = op("vector", "DV",
                           lambda e, qt=qt, buf=buf: e.tensor_scalar(
                               yt[:], ft_t[:, buf, :], stt["negm"], stt["rstd"],
                               ALU.add, ALU.mult),
                           deps=[("DV", v_rstd4), ("YS", v_out_final)])
                dv_out = op("vector", "DV",
                            lambda e, buf=buf: e.tensor_tensor(yt[:], yt[:],
                                                               h2t[:, buf, :], ALU.add),
                            deps=[("DV", dv_y2), (qx, vh)])
                prev_apply = dv_out
                v_out_final = dma("sync", "YS", y[qt * 128:(qt + 1) * 128, :], yt[:],
                                  deps=[("DV", dv_out)])
            waiter("sync", "YS", v_out_final)
            P.flush(block)

    return nc


def kernel(**inputs):
    x = np.asarray(inputs["L0_attn_x"])
    wq_ = np.asarray(inputs["L0_attn_w_q"])
    bq_ = np.ascontiguousarray(np.asarray(inputs["L0_attn_b_q"]).reshape(DC, 128).T)
    wk_ = np.asarray(inputs["L0_attn_w_k"])
    bk_ = np.ascontiguousarray(np.asarray(inputs["L0_attn_b_k"]).reshape(KVH, 128).T)
    wv_ = np.asarray(inputs["L0_attn_w_v"])
    bv_ = np.ascontiguousarray(np.asarray(inputs["L0_attn_b_v"]).reshape(KVH, 128).T)
    wo_ = np.asarray(inputs["L0_attn_w_o"])
    bo_ = np.ascontiguousarray(np.asarray(inputs["L0_attn_b_o"]).reshape(DC, 128).T)
    fw_ = np.asarray(inputs["L0_ffn_weight"])
    fb_ = np.asarray(inputs["L0_ffn_bias"])
    fws_ = np.asarray(inputs["L0_ffn_weight_scale"])
    fis_ = np.asarray(inputs["L0_ffn_input_scale"])

    fb_pc = np.ascontiguousarray(fb_.reshape(DC, 128).T)
    fws_pc = np.ascontiguousarray(fws_.reshape(DC, 128).T)
    fis_bc = np.ascontiguousarray(np.broadcast_to(fis_.reshape(1, 1), (128, 1)))
    identm = np.eye(128, dtype=ml_dtypes.bfloat16)

    nc = build_program()
    in_maps = []
    for c in range(8):
        b = c // 4
        q0 = (c % 4) * OWN
        xr = np.ascontiguousarray(np.roll(x[b], -q0, axis=0))
        in_maps.append(dict(
            xb=xr, wq=wq_, wk=wk_, wv=wv_, wo=wo_, fw=fw_,
            bq_pc=bq_, bk_pc=bk_, bv_pc=bv_, bo_pc=bo_,
            fb_pc=fb_pc, fws_pc=fws_pc, fis_bc=fis_bc,
            ident=identm,
        ))
    res = run_bass_kernel_spmd(nc, in_maps, core_ids=list(range(8)))
    outs = [res.results[c]["y"] for c in range(8)]
    full = np.concatenate(outs, axis=0).reshape(2, T, D)
    return full.astype(np.float32)


# version 25
# speedup vs baseline: 1.0859x; 1.0010x over previous
import numpy as np
import ml_dtypes
import contextlib
import concourse.bass as bass
import concourse.mybir as mybir
from concourse.bass_utils import run_bass_kernel_spmd

F32 = mybir.dt.float32
BF16 = mybir.dt.bfloat16
AF = mybir.ActivationFunctionType
ALU = mybir.AluOpType
AX = mybir.AxisListType.X

D = 4096
T = 4096
QH = 32
KVH = 8
HD = 128
GQ = QH // KVH
OWN = 1024
NQT = OWN // 128
NBT = T // 128
DC = D // 128
HALF = D // 2
SCALE = float(HD) ** -0.5
EPS = 1e-5

ENGS = ("tensor", "vector", "scalar", "gpsimd", "sync")
SEMN = ("PE", "DV", "SC", "GP", "QC", "QX0", "QX1", "QK0", "QK1", "QV", "QV2",
        "QW0", "QW1", "QT0", "QT1", "QU0", "QU1", "QQ", "H2S", "YS")


class Prog:
    def __init__(self, sems):
        self.ops = {e: [] for e in ENGS}
        self.sems = sems
        self.cnt = {k: 0 for k in sems}

    def emit(self, eng, fn):
        self.ops[eng].append(fn)

    def wait(self, eng, deps):
        best = {}
        for s, v in deps:
            if v > 0 and v > best.get(s, 0):
                best[s] = v
        if best:
            deps2 = [(self.sems[s], v) for s, v in best.items()]
            def f(e, deps2=deps2):
                for h, v in deps2:
                    e.wait_ge(h, v)
            self.emit(eng, f)

    def join(self):
        s = dict(self.cnt)
        for eng in ENGS:
            self.wait(eng, [(k, s[k]) for k in SEMN])

    def flush(self, block):
        ops = self.ops
        self.ops = {e: [] for e in ENGS}
        if ops["gpsimd"]:
            @block.gpsimd
            def _(e):
                for f in ops["gpsimd"]:
                    f(e)
        if ops["tensor"]:
            @block.tensor
            def _(e):
                for f in ops["tensor"]:
                    f(e)
        if ops["vector"]:
            @block.vector
            def _(e):
                for f in ops["vector"]:
                    f(e)
        if ops["scalar"]:
            @block.scalar
            def _(e):
                for f in ops["scalar"]:
                    f(e)
        if ops["sync"]:
            @block.sync
            def _(e):
                for f in ops["sync"]:
                    f(e)


def build_program():
    nc = bass.Bass("TRN2", target_bir_lowering=False)

    xb = nc.dram_tensor("xb", [T, D], F32, kind="ExternalInput").ap()
    wq = nc.dram_tensor("wq", [D, D], F32, kind="ExternalInput").ap()
    wk = nc.dram_tensor("wk", [D, KVH * HD], F32, kind="ExternalInput").ap()
    wv = nc.dram_tensor("wv", [D, KVH * HD], F32, kind="ExternalInput").ap()
    wo = nc.dram_tensor("wo", [D, D], F32, kind="ExternalInput").ap()
    fw = nc.dram_tensor("fw", [D, D], F32, kind="ExternalInput").ap()
    bq_pc = nc.dram_tensor("bq_pc", [128, DC], F32, kind="ExternalInput").ap()
    bk_pc = nc.dram_tensor("bk_pc", [128, KVH], F32, kind="ExternalInput").ap()
    bv_pc = nc.dram_tensor("bv_pc", [128, KVH], F32, kind="ExternalInput").ap()
    bo_pc = nc.dram_tensor("bo_pc", [128, DC], F32, kind="ExternalInput").ap()
    fb_pc = nc.dram_tensor("fb_pc", [128, DC], F32, kind="ExternalInput").ap()
    fws_pc = nc.dram_tensor("fws_pc", [128, DC], F32, kind="ExternalInput").ap()
    fis_bc = nc.dram_tensor("fis_bc", [128, 1], F32, kind="ExternalInput").ap()
    ident = nc.dram_tensor("ident", [128, 128], BF16, kind="ExternalInput").ap()
    y = nc.dram_tensor("y", [OWN, D], F32, kind="ExternalOutput").ap()

    kT_d = nc.dram_tensor("kT_d", [KVH, 128, T], BF16, kind="Internal").ap()
    v_d = nc.dram_tensor("v_d", [NBT, 128, KVH * HD], BF16, kind="Internal").ap()
    h2_d = nc.dram_tensor("h2_d", [OWN, D], F32, kind="Internal").ap()

    es = contextlib.ExitStack()
    with es:
        identb = es.enter_context(nc.sbuf_tensor("identb", [128, 128], BF16))
        ones_col = es.enter_context(nc.sbuf_tensor("ones_col", [128, 1], BF16))
        ones_row_f = es.enter_context(nc.sbuf_tensor("ones_row_f", [1, 128], F32))
        fb_sb = es.enter_context(nc.sbuf_tensor("fb_sb", [128, DC], F32))
        fws_sb = es.enter_context(nc.sbuf_tensor("fws_sb", [128, DC], F32))
        nfws_sb = es.enter_context(nc.sbuf_tensor("nfws_sb", [128, DC], F32))
        fis_sb = es.enter_context(nc.sbuf_tensor("fis_sb", [128, 1], F32))
        recip_sb = es.enter_context(nc.sbuf_tensor("recip_sb", [128, DC], F32))
        wsis_sb = es.enter_context(nc.sbuf_tensor("wsis_sb", [128, DC], F32))
        lconst = es.enter_context(nc.sbuf_tensor("lconst", [128, 8], F32))
        bq_sb = es.enter_context(nc.sbuf_tensor("bq_sb", [128, DC], F32))
        bk_sb = es.enter_context(nc.sbuf_tensor("bk_sb", [128, KVH], F32))
        bv_sb = es.enter_context(nc.sbuf_tensor("bv_sb", [128, KVH], F32))
        bo_sb = es.enter_context(nc.sbuf_tensor("bo_sb", [128, DC], F32))

        sems = {}
        for name in SEMN:
            sems[name] = es.enter_context(nc.semaphore(name))
        block = es.enter_context(nc.Block())
        P = Prog(sems)

        def dma(queue_eng, sem_name, out, in_, deps=()):
            P.wait(queue_eng, deps)
            P.cnt[sem_name] += 16
            v = P.cnt[sem_name]
            h = sems[sem_name]
            P.emit(queue_eng, lambda e, out=out, in_=in_, h=h:
                   e.dma_start(out=out, in_=in_).then_inc(h, 16))
            return v

        def op(eng, sem_name, fn, deps=(), selfwait=True):
            if selfwait and eng in ("vector", "scalar", "gpsimd"):
                deps = list(deps) + [(sem_name, P.cnt[sem_name])]
            P.wait(eng, deps)
            P.cnt[sem_name] += 1
            v = P.cnt[sem_name]
            h = sems[sem_name]
            P.emit(eng, lambda e, fn=fn, h=h: fn(e).then_inc(h, 1))
            return v

        def waiter(eng, sem_name, v):
            if v > 0:
                P.emit(eng, lambda e, h=sems[sem_name], v=v: e.wait_ge(h, v))

        dma("sync", "QC", identb[:], ident)
        dma("sync", "QC", fb_sb[:], fb_pc)
        dma("sync", "QC", fws_sb[:], fws_pc)
        dma("sync", "QC", fis_sb[:], fis_bc)
        dma("sync", "QC", bq_sb[:], bq_pc)
        dma("sync", "QC", bk_sb[:], bk_pc)
        dma("sync", "QC", bv_sb[:], bv_pc)
        v_qc = dma("sync", "QC", bo_sb[:], bo_pc)
        op("vector", "DV", lambda e: e.memset(ones_col[:], 1.0))
        v_ones = op("vector", "DV", lambda e: e.memset(ones_row_f[:], 1.0))
        v_wsis = op("vector", "DV",
                    lambda e: e.tensor_single_scalar(wsis_sb[:], fws_sb[:], fis_sb[:], ALU.mult),
                    deps=[("QC", v_qc)])
        v_recip = op("vector", "DV", lambda e: e.reciprocal(recip_sb[:], wsis_sb[:]),
                     deps=[("DV", v_wsis)])
        v_nfws = op("vector", "DV",
                    lambda e: e.tensor_single_scalar(nfws_sb[:], fws_sb[:], -1.0, ALU.mult),
                    deps=[("QC", v_qc)])
        v_nfb = op("vector", "DV",
                   lambda e: e.tensor_single_scalar(fb_sb[:], fb_sb[:], -1.0, ALU.mult),
                   deps=[("QC", v_qc)])
        v_nrecip = op("vector", "DV",
                      lambda e: e.tensor_single_scalar(recip_sb[:], recip_sb[:], -1.0, ALU.mult),
                      deps=[("DV", v_recip)])
        P.wait("tensor", [("QC", v_qc)])

        def ladder_biases(kind, bits):
            delta = 0.0
            out = []
            for (t, engine) in bits:
                th = float(t) if kind == "floor" else float(t) - 0.5
                thr = th - delta
                if engine == "sc":
                    out.append(-thr)
                    delta -= float(t) / 2.0
            return out

        FLOOR_BITS = [(128.0, "sc"), (64.0, "sc"), (32.0, "sc"), (16.0, "sc"),
                      (8.0, "sc"), (4.0, "sc"), (2.0, "sc"), (1.0, "sc")]
        ROUND_BITS = [(4.0, "sc"), (2.0, "dve"), (1.0, "sc")]
        _cvals = ladder_biases("floor", FLOOR_BITS) + ladder_biases("round", ROUND_BITS) + [-8.0]
        _cvals = list(dict.fromkeys(_cvals))
        cmap = {}
        v_lconst = 0
        for _i, _v in enumerate(_cvals):
            v_lconst = op("vector", "DV",
                          lambda e, _i=_i, _v=_v: e.memset(lconst[:, _i:_i + 1], _v))
            cmap[_v] = lconst[:, _i:_i + 1]

        def ln_stats(xt_ap, sq_scr_ap, st, deps_dv, deps_sc):
            v1 = op("vector", "DV", lambda e: e.reduce_sum(st["ssum"], xt_ap, axis=AX),
                    deps=deps_dv)
            v2 = op("scalar", "SC",
                    lambda e: e.activation(sq_scr_ap, xt_ap, AF.Square, accum_out=st["ssq"]),
                    deps=deps_sc)
            op("vector", "DV",
               lambda e: e.tensor_single_scalar(st["negm"], st["ssum"], -1.0 / D, ALU.mult),
               deps=[("DV", v1)])
            op("vector", "DV",
               lambda e: e.tensor_scalar(st["vare"], st["ssq"], 1.0 / D, EPS, ALU.mult, ALU.add),
               deps=[("SC", v2)])
            op("vector", "DV",
               lambda e: e.tensor_tensor(st["msq"], st["negm"], st["negm"], ALU.mult))
            v5 = op("vector", "DV",
                    lambda e: e.tensor_tensor(st["vare"], st["vare"], st["msq"], ALU.subtract))
            v6 = op("scalar", "SC", lambda e: e.activation(st["sd"], st["vare"], AF.Sqrt),
                    deps=[("DV", v5)])
            v_rstd = op("vector", "DV", lambda e: e.reciprocal(st["rstd"], st["sd"]),
                        deps=[("SC", v6)])
            return v_rstd

        def emit_ladder(kind, a_ap, r_ap, g_ap, out_ap, bits, deps):
            delta = 0.0
            cur = a_ap
            vr = list(deps)
            for (t, engine) in bits:
                th = float(t) if kind == "floor" else float(t) - 0.5
                thr = th - delta
                if engine == "dve":
                    vg = op("vector", "DV",
                            lambda e, cur=cur, thr=thr, t=t: e.tensor_scalar(
                                g_ap, cur, float(thr), -float(t), ALU.is_ge, ALU.mult),
                            deps=vr)
                    vnew = op("vector", "DV",
                              lambda e, cur=cur: e.tensor_tensor(r_ap, cur, g_ap, ALU.add),
                              deps=[("DV", vg)])
                else:
                    bias_ap = cmap[-float(thr)]
                    vs = op("scalar", "SC",
                            lambda e, cur=cur, bias_ap=bias_ap: e.activation(
                                g_ap, cur, AF.Sign, bias=bias_ap),
                            deps=list(vr) + [("DV", v_lconst)])
                    vnew = op("vector", "DV",
                              lambda e, cur=cur, t=t: e.scalar_tensor_tensor(
                                  r_ap, g_ap, -float(t) / 2.0, cur, ALU.mult, ALU.add),
                              deps=[("SC", vs)])
                    delta -= float(t) / 2.0
                cur = r_ap
                vr = [("DV", vnew)]
            vfin = op("vector", "DV",
                      lambda e, delta=delta: e.scalar_tensor_tensor(
                          out_ap, r_ap, float(delta), a_ap, ALU.add, ALU.subtract),
                      deps=vr)
            return vfin

        qT_d = nc.dram_tensor("qT_d", [QH, 128, OWN], BF16, kind="Internal").ap()
        with contextlib.ExitStack() as p1:
            xt = p1.enter_context(nc.sbuf_tensor("p1_xt", [128, 2, D], F32))
            u_tok = p1.enter_context(nc.sbuf_tensor("p1_u", [128, D], BF16))
            uT_g = p1.enter_context(nc.sbuf_tensor("p1_uT", [128, 2, DC, 1024], BF16))
            w_sb = p1.enter_context(nc.sbuf_tensor("p1_w", [128, 2, DC, 256], BF16))
            kv_stage = p1.enter_context(nc.sbuf_tensor("p1_st", [128, 2, 1024], BF16))
            vt_stage = p1.enter_context(nc.sbuf_tensor("p1_vt", [128, 1, 8, 128], BF16))
            sttt = p1.enter_context(nc.sbuf_tensor("p1_stt", [128, 8], F32))
            stt = {k: sttt[:, i:i + 1] for i, k in enumerate(
                ("ssum", "ssq", "negm", "msq", "vare", "sd", "rstd"))}
            psA = p1.enter_context(nc.psum_tensor("p1_psA", [128, 2, 1024], F32))
            psT = p1.enter_context(nc.psum_tensor("p1_psT", [128, 2, 1024], BF16))

            st = dict(dv_u=[0, 0], pe_u=0, pe_w=[0, 0], dv_ev=[0, 0],
                      kv_dma=[0, 0], vt_dma=[0, 0], uT_rdy=[0, 0],
                      wctr=0, pctr=0, tctr=0, dv_psT=[0, 0])

            def emit_tile_ln(g, i):
                t = g * 8 + i
                buf = t % 2
                qx = "QX%d" % buf
                vx = dma("sync", qx, xt[:, buf, :], xb[t * 128:(t + 1) * 128, :],
                         deps=[("DV", st["dv_u"][buf])])
                v_rstd = ln_stats(xt[:, buf, :], u_tok[:], stt,
                                  deps_dv=[(qx, vx)],
                                  deps_sc=[(qx, vx), ("PE", st["pe_u"]),
                                           ("DV", st["dv_u"][1 - buf])])
                vu = op("vector", "DV",
                        lambda e, buf=buf: e.tensor_scalar(
                            u_tok[:], xt[:, buf, :], stt["negm"], stt["rstd"],
                            ALU.add, ALU.mult),
                        deps=[("DV", v_rstd), ("PE", st["pe_u"])])
                st["dv_u"][buf] = vu
                st["vu_p"] = vu

            def emit_tile_tp(g, i):
                gb = g % 2
                vu = st["vu_p"]
                last_dv = 0
                pe_v = 0
                for gg in range(4):
                    hb = st["tctr"] % 2
                    st["tctr"] += 1
                    for k in range(8):
                        dc = gg * 8 + k
                        d_ = ([("DV", vu)] if (gg == 0 and k == 0) else []) + \
                             ([("DV", st["dv_psT"][hb])] if k == 0 else [])
                        pe_v = op("tensor", "PE",
                                  lambda e, dc=dc, k=k, hb=hb: e.transpose(
                                      psT[:, hb, k * 128:(k + 1) * 128],
                                      u_tok[:, dc * 128:(dc + 1) * 128], identb[:]),
                                  deps=d_)
                    last_dv = op("vector", "DV",
                                 lambda e, gg=gg, gb=gb, i=i, hb=hb: e.tensor_copy(
                                     uT_g[:, gb, gg * 8:(gg + 1) * 8, i * 128:(i + 1) * 128],
                                     psT[:, hb, :]),
                                 deps=[("PE", pe_v)])
                    st["dv_psT"][hb] = last_dv
                st["pe_u"] = pe_v
                if i == 7:
                    st["uT_rdy"][gb] = last_dv

            def emit_tile(g, i):
                emit_tile_ln(g, i)
                emit_tile_tp(g, i)

            def emit_b(g, kind, oc2):
                gb = g % 2
                wbuf = st["wctr"] % 2
                st["wctr"] += 1
                if kind == "q":
                    wsrc, bias_src = wq, bq_sb
                elif kind == "k":
                    wsrc, bias_src = wk, bk_sb
                else:
                    wsrc, bias_src = wv, bv_sb
                qw = "QW%d" % wbuf
                vw = dma("gpsimd", qw, w_sb[:, wbuf, :, :],
                         wsrc[:, oc2 * 256:(oc2 + 1) * 256].rearrange("(c p) n -> p c n", p=128),
                         deps=[("PE", st["pe_w"][wbuf])])
                for sub in range(2):
                    oc = 2 * oc2 + sub
                    pbuf = st["pctr"] % 2
                    st["pctr"] += 1
                    first_deps = [(qw, vw), ("DV", st["dv_ev"][pbuf]),
                                  ("DV", st["uT_rdy"][gb])]
                    pe_last = 0
                    for dc in range(DC):
                        for s in range(2):
                            pe_last = op("tensor", "PE",
                                         lambda e, dc=dc, s=s, gb=gb, pbuf=pbuf, wbuf=wbuf, sub=sub: e.matmul(
                                             psA[:, pbuf, s * 512:(s + 1) * 512],
                                             w_sb[:, wbuf, dc, sub * 128:(sub + 1) * 128],
                                             uT_g[:, gb, dc, s * 512:(s + 1) * 512],
                                             start=(dc == 0), stop=(dc == DC - 1)),
                                         deps=(first_deps if (dc == 0 and s == 0) else []))
                    st["pe_w"][wbuf] = pe_last
                    dv_e = op("vector", "DV",
                              lambda e, pbuf=pbuf, bias_src=bias_src, oc=oc: e.tensor_single_scalar(
                                  kv_stage[:, pbuf, :], psA[:, pbuf, :],
                                  bias_src[:, oc:oc + 1], ALU.add),
                              deps=[("PE", pe_last), ("QK%d" % pbuf, st["kv_dma"][pbuf]),
                                    ("QC", v_qc)])
                    st["dv_ev"][pbuf] = dv_e
                    if kind == "q":
                        st["kv_dma"][pbuf] = dma("sync", "QK%d" % pbuf, qT_d[oc, :, :],
                                                 kv_stage[:, pbuf, :], deps=[("DV", dv_e)])
                    elif kind == "k":
                        st["kv_dma"][pbuf] = dma(
                            "sync", "QK%d" % pbuf,
                            kT_d[oc, :, g * 1024:(g + 1) * 1024],
                            kv_stage[:, pbuf, :], deps=[("DV", dv_e)])
                    else:
                        hb = st["tctr"] % 2
                        st["tctr"] += 1
                        pe_tv = 0
                        for k in range(8):
                            d_ = ([("DV", dv_e)] if k == 0 else []) + \
                                 ([("DV", st["dv_psT"][hb])] if k == 0 else [])
                            pe_tv = op("tensor", "PE",
                                       lambda e, k=k, pbuf=pbuf, hb=hb: e.transpose(
                                           psT[:, hb, k * 128:(k + 1) * 128],
                                           kv_stage[:, pbuf, k * 128:(k + 1) * 128],
                                           identb[:]),
                                       deps=d_)
                        st["kv_dma"][pbuf] = P.cnt["QK%d" % pbuf]
                        dv_v = op("vector", "DV",
                                  lambda e, hb=hb: e.tensor_copy(
                                      vt_stage[:, 0, :, :], psT[:, hb, :]),
                                  deps=[("PE", pe_tv), ("QV", max(st["vt_dma"]))])
                        st["dv_psT"][hb] = dv_v
                        kt0 = g * 8
                        st["vt_dma"][pbuf] = dma(
                            "sync", "QV",
                            v_d[kt0:kt0 + 8, :, oc * 128:(oc + 1) * 128]
                            .rearrange("t p n -> p t n"),
                            vt_stage[:, 0, :, :],
                            deps=[("DV", dv_v)])

            for i in range(8):
                emit_tile(0, i)
            for g in range(4):
                bunits = []
                if g == 0:
                    bunits += [("q", oc2) for oc2 in range(QH // 2)]
                bunits += [("k", j) for j in range(KVH // 2)]
                bunits += [("v", j) for j in range(KVH // 2)]
                aunits = list(range(8)) if g < 3 else []
                na, nb = len(aunits), len(bunits)
                a_ln = 0
                a_tp = 0
                for bi, (kind, oc) in enumerate(bunits):
                    if na and a_ln < na and a_ln == a_tp and \
                            (bi + 2) * na >= (a_ln + 1) * nb:
                        emit_tile_ln(g + 1, aunits[a_ln])
                        a_ln += 1
                    emit_b(g, kind, oc)
                    while na and a_tp < a_ln and (bi + 1) * na >= (a_tp + 1) * nb:
                        emit_tile_tp(g + 1, aunits[a_tp])
                        a_tp += 1
                while na and a_tp < na:
                    if a_ln == a_tp:
                        emit_tile_ln(g + 1, aunits[a_ln])
                        a_ln += 1
                    emit_tile_tp(g + 1, aunits[a_tp])
                    a_tp += 1
            P.join()
            P.flush(block)

        AT_cm = nc.sbuf_tensor("AT", [128, QH, OWN], BF16, side="right")
        AT = AT_cm.__enter__()
        with contextlib.ExitStack() as p3:
            qT = p3.enter_context(nc.sbuf_tensor("p3_qT", [128, QH, OWN], BF16))
            kT_sb = p3.enter_context(nc.sbuf_tensor("p3_kT", [128, 2, T], BF16))
            v_tok = p3.enter_context(nc.sbuf_tensor("p3_v", [128, 2, NBT, 128], BF16))
            ptile = p3.enter_context(nc.sbuf_tensor("p3_p", [128, 2, 1024], BF16))
            sqs = p3.enter_context(nc.sbuf_tensor("p3_sqs", [128, T], BF16))
            mk_sb = p3.enter_context(nc.sbuf_tensor("p3_mk", [1, 16], F32))
            cb = p3.enter_context(nc.sbuf_tensor("p3_cb", [128, 4], F32))
            sig_sb = p3.enter_context(nc.sbuf_tensor("p3_sig", [1, 1024], F32))
            rs_sb = p3.enter_context(nc.sbuf_tensor("p3_rs", [1, 1024], F32))
            bc_sb = p3.enter_context(nc.sbuf_tensor("p3_bc", [128, 1024], F32))
            psS = p3.enter_context(nc.psum_tensor("p3_psS", [128, 2, 1024], F32))
            psA_ = p3.enter_context(nc.psum_tensor("p3_psA", [128, 1024], F32))
            psG = p3.enter_context(nc.psum_tensor("p3_psG", [128, 1024], F32))

            v_qt = dma("sync", "QQ", qT[:, :, :], qT_d.rearrange("o p n -> p o n"))

            pe_group_done = [0, 0]
            pe_sqs_read = 0
            dv_head_tail = 0
            sc_exp = [0, 0]
            pe_pv_sig = [0, 0]
            for j in range(KVH):
                gbuf = j % 2
                qt_s = "QT%d" % gbuf
                qu_s = "QU%d" % gbuf
                vk = dma("sync", qt_s, kT_sb[:, gbuf, :], kT_d[j, :, :],
                         deps=[("PE", pe_group_done[gbuf])])
                vv = 0
                for q4 in range(4):
                    vv = dma("sync", qu_s,
                             v_tok[:, gbuf, q4 * 8:(q4 + 1) * 8, :],
                             v_d[q4 * 8:(q4 + 1) * 8, :, j * 128:(j + 1) * 128]
                             .rearrange("t p n -> p t n"))
                dv_sq = op("vector", "DV",
                           lambda e, gbuf=gbuf: e.tensor_tensor(
                               sqs[:], kT_sb[:, gbuf, :], kT_sb[:, gbuf, :], ALU.mult),
                           deps=[(qt_s, vk), ("PE", pe_sqs_read)])
                dvm = [0, 0]
                for s in range(8):
                    slot = s % 2
                    d_ = [("DV", dv_sq)] if s == 0 else []
                    waiter("tensor", "DV", dvm[slot])
                    pe_n = op("tensor", "PE",
                              lambda e, s=s, slot=slot: e.matmul(
                                  psG[0:1, slot * 512:(slot + 1) * 512],
                                  ones_col[:], sqs[:, s * 512:(s + 1) * 512],
                                  start=True, stop=True),
                              deps=d_)
                    dvm[slot] = op("vector", "DV",
                                   lambda e, s=s, slot=slot: e.reduce_max(
                                       mk_sb[0:1, s:s + 1],
                                       psG[0:1, slot * 512:(slot + 1) * 512], axis=AX),
                                   deps=[("PE", pe_n)])
                qu_s2, vv2 = qu_s, vv
                op("vector", "DV",
                   lambda e: e.reduce_max(mk_sb[0:1, 8:9], mk_sb[0:1, 0:8], axis=AX))

                dv_cb = 0
                for hh in range(GQ):
                    h = j * GQ + hh
                    dv_qsq = op("vector", "DV",
                                lambda e, h=h: e.tensor_tensor(
                                    sqs[:, 0:1024], qT[:, h, :], qT[:, h, :], ALU.mult),
                                deps=[("PE", pe_sqs_read), ("QQ", v_qt)])
                    pe_n = 0
                    for s in range(2):
                        pe_n = op("tensor", "PE",
                                  lambda e, s=s: e.matmul(
                                      psG[0:1, s * 512:(s + 1) * 512],
                                      ones_col[:], sqs[:, s * 512:(s + 1) * 512],
                                      start=True, stop=True),
                                  deps=([("DV", dv_qsq)] if s == 0 else []))
                    pe_sqs_read = pe_n
                    op("vector", "DV",
                       lambda e: e.reduce_max(mk_sb[0:1, 9:10], psG[0:1, 0:512], axis=AX),
                       deps=[("PE", pe_n)])
                    op("vector", "DV",
                       lambda e: e.reduce_max(mk_sb[0:1, 10:11], psG[0:1, 512:1024], axis=AX))
                    op("vector", "DV",
                       lambda e: e.reduce_max(mk_sb[0:1, 11:12], mk_sb[0:1, 9:11], axis=AX))
                    dv_c1 = op("vector", "DV",
                               lambda e: e.tensor_tensor(mk_sb[0:1, 12:13], mk_sb[0:1, 11:12],
                                                         mk_sb[0:1, 8:9], ALU.add))
                    dv_c3 = op("vector", "DV",
                               lambda e: e.tensor_single_scalar(
                                   mk_sb[0:1, 14:15], mk_sb[0:1, 12:13], -SCALE / 2.0, ALU.mult),
                               deps=[("DV", dv_c1)])
                    pe_cb = op("tensor", "PE",
                               lambda e: e.matmul(psG[:, 0:1], ones_row_f[0:1, :],
                                                  mk_sb[0:1, 14:15], start=True, stop=True),
                               deps=[("DV", dv_c3)])
                    dv_cb = op("vector", "DV",
                               lambda e, hh=hh: e.tensor_copy(cb[:, hh:hh + 1], psG[:, 0:1]),
                               deps=[("PE", pe_cb)])
                waiter("tensor", "DV", dv_cb)
                dv_b_prev = 0

                for hh in range(GQ):
                    h = j * GQ + hh
                    def emit_s(kc):
                        sb_ = kc % 2
                        pe_s = 0
                        for s in range(2):
                            d_ = ([("SC", sc_exp[sb_])] if s == 0 else [])
                            pe_s = op("tensor", "PE",
                                      lambda e, kc=kc, s=s, sb_=sb_, gbuf=gbuf, h=h: e.matmul(
                                          psS[:, sb_, s * 512:(s + 1) * 512],
                                          kT_sb[:, gbuf, kc * 128:(kc + 1) * 128],
                                          qT[:, h, s * 512:(s + 1) * 512],
                                          start=True, stop=True),
                                      deps=d_)
                        return pe_s

                    def emit_exp(kc, pe_s):
                        sb_ = kc % 2
                        d_exp = [("PE", pe_s), ("PE", pe_pv_sig[sb_])]
                        if kc == 0:
                            d_exp.append(("DV", dv_cb))
                        vexp = op("scalar", "SC",
                                  lambda e, sb_=sb_, hh=hh: e.activation(
                                      ptile[:, sb_, :], psS[:, sb_, :], AF.Exp,
                                      bias=cb[:, hh:hh + 1], scale=SCALE),
                                  deps=d_exp, selfwait=False)
                        sc_exp[sb_] = vexp
                        return vexp

                    def emit_pv(kc, vexp):
                        sb_ = kc % 2
                        for s in range(2):
                            d_ = []
                            if s == 0:
                                d_ = [("SC", vexp)]
                                if kc == 0:
                                    d_ += [(qu_s2, vv2), ("DV", dv_head_tail)]
                            op("tensor", "PE",
                               lambda e, kc=kc, s=s, sb_=sb_, gbuf=gbuf: e.matmul(
                                   psA_[:, s * 512:(s + 1) * 512],
                                   v_tok[:, gbuf, kc, :],
                                   ptile[:, sb_, s * 512:(s + 1) * 512],
                                   start=(kc == 0), stop=(kc == NBT - 1)),
                               deps=d_)
                        pe_g = 0
                        for s in range(2):
                            d2_ = ([("DV", dv_b_prev)] if (kc == 0 and s == 0) else [])
                            pe_g = op("tensor", "PE",
                                      lambda e, kc=kc, s=s, sb_=sb_: e.matmul(
                                          psG[0:1, s * 512:(s + 1) * 512],
                                          ones_col[:],
                                          ptile[:, sb_, s * 512:(s + 1) * 512],
                                          start=(kc == 0), stop=(kc == NBT - 1)),
                                      deps=d2_)
                        pe_pv_sig[sb_] = pe_g

                    pe_s_arr = {0: emit_s(0), 1: emit_s(1)}
                    for kc in range(NBT):
                        vexp = emit_exp(kc, pe_s_arr[kc])
                        if kc + 2 < NBT:
                            pe_s_arr[kc + 2] = emit_s(kc + 2)
                        emit_pv(kc, vexp)
                    pe_head_mms = P.cnt["PE"]
                    dv_s2 = op("vector", "DV",
                               lambda e: e.reciprocal(rs_sb[:], psG[0:1, :]),
                               deps=[("PE", pe_head_mms)])
                    pe_b = 0
                    for s in range(2):
                        d_ = ([("DV", dv_s2)] if s == 0 else [])
                        pe_b = op("tensor", "PE",
                                  lambda e, s=s: e.matmul(
                                      psG[:, s * 512:(s + 1) * 512],
                                      ones_row_f[0:1, :],
                                      rs_sb[0:1, s * 512:(s + 1) * 512],
                                      start=True, stop=True),
                                  deps=d_)
                    dv_b = op("vector", "DV",
                              lambda e: e.tensor_copy(bc_sb[:], psG[:, :]),
                              deps=[("PE", pe_b)])
                    dv_b_prev = dv_b
                    dv_n = op("vector", "DV",
                              lambda e, h=h: e.tensor_tensor(
                                  AT[:, h, :], psA_[:], bc_sb[:], ALU.mult),
                              deps=[("DV", dv_b)])
                    dv_head_tail = dv_n
                pe_group_done[gbuf] = P.cnt["PE"]
            P.join()
            P.flush(block)

        big_cm = nc.sbuf_tensor("big_tok", [128, NQT, D], BF16)
        big_tok = big_cm.__enter__()
        with contextlib.ExitStack() as p4:
            wo_sb = p4.enter_context(nc.sbuf_tensor("p4_wo", [128, 2, DC, 128], BF16))
            ot_stage = p4.enter_context(nc.sbuf_tensor("p4_ot", [128, 2, 1024], BF16))
            psA = p4.enter_context(nc.psum_tensor("p4_psA", [128, 2, 1024], F32))
            psT = p4.enter_context(nc.psum_tensor("p4_psT", [128, 2, 1024], BF16))

            pe_o = [0, 0]
            dv_o_evac = [0, 0]
            pe_ot_read = [0, 0]
            dv_psT = [0, 0]
            for oc in range(DC):
                wbuf = oc % 2
                qw = "QW%d" % wbuf
                vw = dma("gpsimd", qw, wo_sb[:, wbuf, :, :],
                         wo[:, oc * 128:(oc + 1) * 128].rearrange("(c p) n -> p c n", p=128),
                         deps=[("PE", pe_o[wbuf])])
                pbuf = oc % 2
                first_deps = [(qw, vw), ("DV", dv_o_evac[pbuf])]
                pe_last = 0
                for dc in range(DC):
                    for s in range(2):
                        pe_last = op("tensor", "PE",
                                     lambda e, dc=dc, s=s, pbuf=pbuf, wbuf=wbuf: e.matmul(
                                         psA[:, pbuf, s * 512:(s + 1) * 512],
                                         wo_sb[:, wbuf, dc, :],
                                         AT[:, dc, s * 512:(s + 1) * 512],
                                         start=(dc == 0), stop=(dc == DC - 1)),
                                     deps=(first_deps if (dc == 0 and s == 0) else []))
                pe_o[wbuf] = pe_last
                dv_e = op("vector", "DV",
                          lambda e, pbuf=pbuf, oc=oc: e.tensor_single_scalar(
                              ot_stage[:, pbuf, :], psA[:, pbuf, :],
                              bo_sb[:, oc:oc + 1], ALU.add),
                          deps=[("PE", pe_last), ("PE", pe_ot_read[pbuf]),
                                ("QC", v_qc)])
                dv_o_evac[pbuf] = dv_e
                hb = oc % 2
                pe_t = 0
                for k in range(8):
                    d_ = ([("DV", dv_e), ("DV", dv_psT[hb])] if k == 0 else [])
                    pe_t = op("tensor", "PE",
                              lambda e, k=k, pbuf=pbuf, hb=hb: e.transpose(
                                  psT[:, hb, k * 128:(k + 1) * 128],
                                  ot_stage[:, pbuf, k * 128:(k + 1) * 128],
                                  identb[:]),
                              deps=d_)
                pe_ot_read[pbuf] = pe_t
                dv_t = op("vector", "DV",
                          lambda e, oc=oc, hb=hb: e.tensor_copy(
                              big_tok[:, :, oc * 128:(oc + 1) * 128], psT[:, hb, :]),
                          deps=[("PE", pe_t)])
                dv_psT[hb] = dv_t
            P.join()
            P.flush(block)

        AT_cm.__exit__(None, None, None)

        xiT_cm = nc.sbuf_tensor("xiT", [128, DC, OWN], BF16, side="right")
        xiT = xiT_cm.__enter__()
        with contextlib.ExitStack() as p4b:
            xtile = p4b.enter_context(nc.sbuf_tensor("p4b_x", [128, D], F32))
            h2t = p4b.enter_context(nc.sbuf_tensor("p4b_h2", [128, D], F32))
            au = p4b.enter_context(nc.sbuf_tensor("p4b_au", [128, D], F32))
            xs_h = p4b.enter_context(nc.sbuf_tensor("p4b_xs", [128, 2, HALF], BF16))
            r_h = p4b.enter_context(nc.sbuf_tensor("p4b_r", [128, 2, HALF], F32))
            stt = {k: p4b.enter_context(nc.sbuf_tensor("p4b_s_" + k, [128, 1], F32))
                   for k in ("ssum", "ssq", "negm", "msq", "vare", "sd", "rstd")}
            stt = {k: v[:] for k, v in stt.items()}
            rstd3f = p4b.enter_context(nc.sbuf_tensor("p4b_r3f", [128, 1], F32))
            psT = p4b.enter_context(nc.psum_tensor("p4b_psT", [128, 2, 1024], BF16))

            pe_xi_read_ = [0, 0]
            dv_psT4 = [("DV", 0), ("DV", 0)]
            prev = dict(dv_h2=0, h2s=0, sc_a0=0, sc_lad=0, dv_r3f=0,
                        vfin=[0, 0], xi2=[0, 0], xi1=[0, 0])
            for qt in range(NQT):
                if qt >= 2:
                    scr = big_tok[:, qt - 2, :]
                    scr_dep = [("DV", prev["xi2"][0]), ("DV", prev["xi2"][1])]
                elif qt == 1:
                    scr = au[:]
                    scr_dep = [("DV", prev["vfin"][0]), ("DV", prev["vfin"][1])]
                else:
                    scr = au[:]
                    scr_dep = []
                vx = dma("sync", "QX0", xtile[:], xb[qt * 128:(qt + 1) * 128, :],
                         deps=[("DV", prev["dv_h2"])])
                v_rstd2 = ln_stats(big_tok[:, qt, :], scr, stt, deps_dv=[],
                                   deps_sc=scr_dep + [("DV", prev["dv_r3f"])])
                dv_y1 = op("vector", "DV",
                           lambda e, qt=qt: e.tensor_scalar(
                               h2t[:], big_tok[:, qt, :], stt["negm"], stt["rstd"],
                               ALU.add, ALU.mult),
                           deps=[("DV", v_rstd2), ("H2S", prev["h2s"]),
                                 ("SC", prev["sc_a0"])])
                dv_h2 = op("vector", "DV",
                           lambda e: e.tensor_tensor(h2t[:], h2t[:], xtile[:], ALU.add),
                           deps=[("DV", dv_y1), ("QX0", vx)])
                h2s_v = dma("sync", "H2S", h2_d[qt * 128:(qt + 1) * 128, :], h2t[:],
                            deps=[("DV", dv_h2)])
                v_rstd3 = ln_stats(h2t[:], scr, stt,
                                   deps_dv=[("DV", dv_h2)],
                                   deps_sc=[("DV", dv_h2), ("DV", dv_y1)])
                dv_r3f = op("vector", "DV",
                            lambda e: e.tensor_tensor(rstd3f[:], stt["rstd"], fis_sb[:],
                                                      ALU.mult),
                            deps=[("DV", v_rstd3)])
                dv_xs_ = [0, 0]
                for dh in range(2):
                    dv_xs_[dh] = op("vector", "DV",
                                    lambda e, dh=dh: e.tensor_scalar(
                                        r_h[:, dh, :], h2t[:, dh * HALF:(dh + 1) * HALF],
                                        stt["negm"], rstd3f[:], ALU.add, ALU.mult),
                                    deps=[("DV", dv_r3f), ("SC", prev["sc_lad"])])
                sc_a_ = [0, 0]
                sc_ns_ = [0, 0]
                for dh in range(2):
                    sc_a_[dh] = op("scalar", "SC",
                                   lambda e, dh=dh: e.activation(
                                       au[:, dh * HALF:(dh + 1) * HALF],
                                       r_h[:, dh, :], AF.Abs),
                                   deps=[("DV", dv_xs_[dh])])
                ns_ap = [big_tok[:, qt, dh * HALF:(dh + 1) * HALF] for dh in range(2)]
                for dh in range(2):
                    sc_ns_[dh] = op("scalar", "SC",
                                    lambda e, dh=dh, ns=ns_ap[dh]: e.activation(
                                        ns, r_h[:, dh, :], AF.Sign, scale=-1.0),
                                    deps=[("DV", dv_xs_[dh])])
                cur_ = [au[:, 0:HALF], au[:, HALF:2 * HALF]]
                vr_ = [[("SC", sc_a_[0]), ("SC", sc_ns_[0]), ("DV", v_lconst),
                        ("PE", pe_xi_read_[0])],
                       [("SC", sc_a_[1]), ("SC", sc_ns_[1]), ("DV", v_lconst),
                        ("PE", pe_xi_read_[1])]]
                delta = 0.0
                for (t, engine) in FLOOR_BITS:
                    th = float(t)
                    thr = th - delta
                    vc_ = [0, 0]
                    if engine == "dve":
                        for dh in range(2):
                            vc_[dh] = op("vector", "DV",
                                         lambda e, cur=cur_[dh], thr=thr, t=t, dh=dh:
                                         e.tensor_scalar(
                                             xs_h[:, dh, :], cur, float(thr), -float(t),
                                             ALU.is_ge, ALU.mult),
                                         deps=vr_[dh])
                        for dh in range(2):
                            vnew = op("vector", "DV",
                                      lambda e, cur=cur_[dh], dh=dh:
                                      e.tensor_tensor(
                                          r_h[:, dh, :], cur, xs_h[:, dh, :], ALU.add),
                                      deps=[("DV", vc_[dh])])
                            cur_[dh] = r_h[:, dh, :]
                            vr_[dh] = [("DV", vnew)]
                    else:
                        bias_ap = cmap[-float(thr)]
                        for dh in range(2):
                            vc_[dh] = op("scalar", "SC",
                                         lambda e, cur=cur_[dh], bias_ap=bias_ap, dh=dh:
                                         e.activation(
                                             xs_h[:, dh, :], cur, AF.Sign, bias=bias_ap),
                                         deps=vr_[dh])
                        for dh in range(2):
                            vnew = op("vector", "DV",
                                      lambda e, cur=cur_[dh], t=t, dh=dh:
                                      e.scalar_tensor_tensor(
                                          r_h[:, dh, :], xs_h[:, dh, :], -float(t) / 2.0,
                                          cur, ALU.mult, ALU.add),
                                      deps=[("SC", vc_[dh])])
                            cur_[dh] = r_h[:, dh, :]
                            vr_[dh] = [("DV", vnew)]
                        delta -= float(t) / 2.0
                sc_lad = P.cnt["SC"]
                dv_xi_ = [0, 0]
                vfin_ = [0, 0]
                for dh in range(2):
                    vfin_[dh] = op("vector", "DV",
                                   lambda e, delta=delta, dh=dh: e.scalar_tensor_tensor(
                                       xs_h[:, dh, :], r_h[:, dh, :], float(delta),
                                       au[:, dh * HALF:(dh + 1) * HALF], ALU.add,
                                       ALU.subtract),
                                   deps=vr_[dh])
                    dv_xi_[dh] = op("vector", "DV",
                                    lambda e, dh=dh, ns=ns_ap[dh]: e.tensor_tensor(
                                        xs_h[:, dh, :], xs_h[:, dh, :], ns,
                                        ALU.mult),
                                    deps=[("DV", vfin_[dh])])
                for dh in range(2):
                    for g in range(2):
                        hb = g
                        d0 = [("DV", dv_xi_[dh])] if g == 0 else []
                        for k in range(8):
                            d_ = (d0 + [dv_psT4[hb]]) if k == 0 else []
                            pe_v = op("tensor", "PE",
                                      lambda e, g=g, k=k, hb=hb, dh=dh: e.transpose(
                                          psT[:, hb, k * 128:(k + 1) * 128],
                                          xs_h[:, dh, (g * 8 + k) * 128:(g * 8 + k + 1) * 128],
                                          identb[:]),
                                      deps=d_)
                        sc_e = op("scalar", "SC",
                                  lambda e, g=g, dh=dh, qt=qt, hb=hb: e.activation(
                                      xiT[:, dh * 16 + g * 8:dh * 16 + (g + 1) * 8,
                                          qt * 128:(qt + 1) * 128],
                                      psT[:, hb, :], AF.Copy),
                                  deps=[("PE", pe_v)])
                        dv_psT4[hb] = ("SC", sc_e)
                    pe_xi_read_[dh] = pe_v
                prev = dict(dv_h2=dv_h2, h2s=h2s_v, sc_a0=sc_a_[0], sc_lad=sc_lad,
                            dv_r3f=dv_r3f, vfin=vfin_, xi2=prev["xi1"], xi1=dv_xi_)
            P.join()
            P.flush(block)

        big_cm.__exit__(None, None, None)

        ffn_d = nc.dram_tensor("ffn_d", [OWN, D], BF16, kind="Internal").ap()
        with contextlib.ExitStack() as p5:
            fw_sb = p5.enter_context(nc.sbuf_tensor("p5_fw", [128, 2, D], F32))
            q_b = p5.enter_context(nc.sbuf_tensor("p5_q", [128, 2, HALF], F32))
            r_b = p5.enter_context(nc.sbuf_tensor("p5_r", [128, 2, HALF], F32))
            g_b = p5.enter_context(nc.sbuf_tensor("p5_g", [128, 2, HALF], F32))
            wn_b = p5.enter_context(nc.sbuf_tensor("p5_wn", [128, 2, HALF], BF16))
            wT_sb = p5.enter_context(nc.sbuf_tensor("p5_wT", [128, 4, DC, 128], BF16))
            ft_stage = p5.enter_context(nc.sbuf_tensor("p5_ft", [128, 2, 1024], BF16))
            tt_stage = p5.enter_context(nc.sbuf_tensor("p5_tt", [128, 2, 8, 128], BF16))
            psA = p5.enter_context(nc.psum_tensor("p5_psA", [128, 2, 1024], F32))
            psT = p5.enter_context(nc.psum_tensor("p5_psT", [128, 4, 1024], BF16))

            dv_fw_read = [0, 0]
            pe_wn_read = [0, 0]
            tctr5 = [0]
            dv_psT5 = [("DV", 0), ("DV", 0), ("DV", 0), ("DV", 0)]
            pe_wT = [0, 0, 0, 0]
            dv_f_evac = [0, 0]
            pe_ft_read = [0, 0]
            tt_dma = [0, 0]
            dv_wT_of = [("DV", 0), ("DV", 0), ("DV", 0), ("DV", 0)]
            pe_mm5 = [0, 0]
            pend5 = [None]

            def emit_prep_a(oc):
                wbuf = oc % 2
                wb4 = oc % 4
                qw = "QW%d" % wbuf
                vw = dma("gpsimd", qw, fw_sb[:, wbuf, :], fw[oc * 128:(oc + 1) * 128, :],
                         deps=[("DV", dv_fw_read[wbuf])])
                vq_ = [0, 0]
                for par in range(2):
                    vq_[par] = op("vector", "DV",
                                  lambda e, oc=oc, par=par, wbuf=wbuf: e.tensor_scalar(
                                      q_b[:, par, :], fw_sb[:, wbuf, par * HALF:(par + 1) * HALF],
                                      fws_sb[:, oc:oc + 1], 4.5, ALU.mult, ALU.add),
                                  deps=[(qw, vw), ("QC", v_qc)])
                dv_fw_read[wbuf] = vq_[1]
                cur_ = [q_b[:, 0, :], q_b[:, 1, :]]
                vr_ = [[("DV", vq_[0]), ("DV", v_lconst)], [("DV", vq_[1]), ("DV", v_lconst)]]
                delta = 0.0
                for t in (8.0, 4.0, 2.0, 1.0):
                    vs_ = [0, 0]
                    for par in range(2):
                        vs_[par] = op("scalar", "SC",
                                      lambda e, cur=cur_[par], par=par: e.activation(
                                          g_b[:, par, :], cur, AF.Sign, bias=cmap[-8.0]),
                                      deps=vr_[par])
                    for par in range(2):
                        vnew = op("vector", "DV",
                                  lambda e, cur=cur_[par], t=t, par=par: e.scalar_tensor_tensor(
                                      r_b[:, par, :], g_b[:, par, :], -t / 2.0, cur,
                                      ALU.mult, ALU.add),
                                  deps=[("SC", vs_[par])])
                        cur_[par] = r_b[:, par, :]
                        vr_[par] = [("DV", vnew)]
                    delta -= t / 2.0
                vwn_ = [0, 0]
                for par in range(2):
                    vwn_[par] = op("vector", "DV",
                                   lambda e, par=par, delta=delta: e.scalar_tensor_tensor(
                                       wn_b[:, par, :], r_b[:, par, :], delta + 4.0,
                                       q_b[:, par, :], ALU.add, ALU.subtract),
                                   deps=vr_[par] + [("PE", pe_wn_read[par])])
                pend = []
                for par in range(2):
                    pe_v = 0
                    for g in range(2):
                        hb = tctr5[0] % 4
                        tctr5[0] += 1
                        d0 = [("DV", vwn_[par])] if g == 0 else []
                        for k in range(8):
                            d_ = (d0 + [dv_psT5[hb]]) if k == 0 else []
                            pe_v = op("tensor", "PE",
                                      lambda e, g=g, k=k, par=par, hb=hb: e.transpose(
                                          psT[:, hb, k * 128:(k + 1) * 128],
                                          wn_b[:, par, (g * 8 + k) * 128:(g * 8 + k + 1) * 128],
                                          identb[:]),
                                      deps=d_)
                        pend.append((par, g, hb, pe_v))
                    pe_wn_read[par] = pe_v
                pend5[0] = (wb4, pend)

            def emit_prep_b():
                wb4, pend = pend5[0]
                for (par, g, hb, pe_v) in pend:
                    sc_e = op("scalar", "SC",
                              lambda e, g=g, par=par, wb4=wb4, hb=hb: e.activation(
                                  wT_sb[:, wb4, par * 16 + g * 8:par * 16 + (g + 1) * 8, :],
                                  psT[:, hb, :], AF.Copy),
                              deps=[("PE", pe_v)])
                    dv_psT5[hb] = ("SC", sc_e)
                dv_wT_of[wb4] = ("SC", P.cnt["SC"])

            def emit_consume_mm(oc):
                wb4 = oc % 4
                pbuf = oc % 2
                first_deps = [dv_wT_of[wb4], ("DV", dv_f_evac[pbuf])]
                pe_last = 0
                for dc in range(DC):
                    for s in range(2):
                        pe_last = op("tensor", "PE",
                                     lambda e, dc=dc, s=s, pbuf=pbuf, wb4=wb4: e.matmul(
                                         psA[:, pbuf, s * 512:(s + 1) * 512],
                                         wT_sb[:, wb4, dc, :],
                                         xiT[:, dc, s * 512:(s + 1) * 512],
                                         start=(dc == 0), stop=(dc == DC - 1)),
                                     deps=(first_deps if (dc == 0 and s == 0) else []))
                pe_wT[wb4] = pe_last
                pe_mm5[pbuf] = pe_last

            def emit_consume_evac(oc):
                pbuf = oc % 2
                dv_e = op("vector", "DV",
                          lambda e, pbuf=pbuf, oc=oc: e.tensor_scalar(
                              ft_stage[:, pbuf, :], psA[:, pbuf, :],
                              fb_sb[:, oc:oc + 1], recip_sb[:, oc:oc + 1],
                              ALU.add, ALU.mult),
                          deps=[("PE", pe_mm5[pbuf]), ("DV", v_nrecip), ("DV", v_nfb),
                                ("PE", pe_ft_read[pbuf])])
                dv_f_evac[pbuf] = dv_e

            def emit_consume_tail(oc):
                pbuf = oc % 2
                dv_e = dv_f_evac[pbuf]
                hb = tctr5[0] % 4
                tctr5[0] += 1
                pe_t = 0
                for k in range(8):
                    d_ = ([("DV", dv_e), dv_psT5[hb]] if k == 0 else [])
                    pe_t = op("tensor", "PE",
                              lambda e, k=k, pbuf=pbuf, hb=hb: e.transpose(
                                  psT[:, hb, k * 128:(k + 1) * 128],
                                  ft_stage[:, pbuf, k * 128:(k + 1) * 128],
                                  identb[:]),
                              deps=d_)
                pe_ft_read[pbuf] = pe_t
                qv_s = ("QV", "QV2")[pbuf]
                sc_t = op("scalar", "SC",
                          lambda e, pbuf=pbuf, hb=hb: e.activation(
                              tt_stage[:, pbuf, :, :], psT[:, hb, :], AF.Copy),
                          deps=[("PE", pe_t), (qv_s, tt_dma[pbuf])])
                dv_psT5[hb] = ("SC", sc_t)
                tt_dma[pbuf] = dma(
                    "sync", qv_s,
                    ffn_d[:, oc * 128:(oc + 1) * 128].rearrange("(t p) n -> p t n", p=128),
                    tt_stage[:, pbuf, :, :],
                    deps=[("SC", sc_t)])

            emit_prep_a(0)
            emit_prep_b()
            emit_prep_a(1)
            emit_prep_b()
            for oc in range(DC):
                emit_consume_mm(oc)
                if oc + 2 < DC:
                    emit_prep_a(oc + 2)
                emit_consume_evac(oc)
                if oc + 2 < DC:
                    emit_prep_b()
                emit_consume_tail(oc)
            P.join()
            P.flush(block)

        xiT_cm.__exit__(None, None, None)

        with contextlib.ExitStack() as p5b:
            h2t = p5b.enter_context(nc.sbuf_tensor("p5b_h2", [128, 2, D], F32))
            ft_t = p5b.enter_context(nc.sbuf_tensor("p5b_ft", [128, 2, D], BF16))
            yt = p5b.enter_context(nc.sbuf_tensor("p5b_y", [128, D], F32))
            sq5 = p5b.enter_context(nc.sbuf_tensor("p5b_sq", [128, D], BF16))
            stt = {k: p5b.enter_context(nc.sbuf_tensor("p5b_s_" + k, [128, 1], F32))
                   for k in ("ssum", "ssq", "negm", "msq", "vare", "sd", "rstd")}
            stt = {k: v[:] for k, v in stt.items()}

            v_out_final = 0
            prev_apply = 0
            for qt in range(NQT):
                buf = qt % 2
                qx = "QX%d" % buf
                qk = "QK%d" % buf
                vh = dma("sync", qx, h2t[:, buf, :], h2_d[qt * 128:(qt + 1) * 128, :],
                         deps=[("DV", prev_apply)])
                vff = dma("sync", qk, ft_t[:, buf, :], ffn_d[qt * 128:(qt + 1) * 128, :],
                          deps=[("DV", prev_apply)])
                v_rstd4 = ln_stats(ft_t[:, buf, :], sq5[:], stt,
                                   deps_dv=[(qk, vff)],
                                   deps_sc=[(qk, vff), ("DV", prev_apply)])
                dv_y2 = op("vector", "DV",
                           lambda e, qt=qt, buf=buf: e.tensor_scalar(
                               yt[:], ft_t[:, buf, :], stt["negm"], stt["rstd"],
                               ALU.add, ALU.mult),
                           deps=[("DV", v_rstd4), ("YS", v_out_final)])
                dv_out = op("vector", "DV",
                            lambda e, buf=buf: e.tensor_tensor(yt[:], yt[:],
                                                               h2t[:, buf, :], ALU.add),
                            deps=[("DV", dv_y2), (qx, vh)])
                prev_apply = dv_out
                v_out_final = dma("sync", "YS", y[qt * 128:(qt + 1) * 128, :], yt[:],
                                  deps=[("DV", dv_out)])
            waiter("sync", "YS", v_out_final)
            P.flush(block)

    return nc


def kernel(**inputs):
    x = np.asarray(inputs["L0_attn_x"])
    wq_ = np.asarray(inputs["L0_attn_w_q"])
    bq_ = np.ascontiguousarray(np.asarray(inputs["L0_attn_b_q"]).reshape(DC, 128).T)
    wk_ = np.asarray(inputs["L0_attn_w_k"])
    bk_ = np.ascontiguousarray(np.asarray(inputs["L0_attn_b_k"]).reshape(KVH, 128).T)
    wv_ = np.asarray(inputs["L0_attn_w_v"])
    bv_ = np.ascontiguousarray(np.asarray(inputs["L0_attn_b_v"]).reshape(KVH, 128).T)
    wo_ = np.asarray(inputs["L0_attn_w_o"])
    bo_ = np.ascontiguousarray(np.asarray(inputs["L0_attn_b_o"]).reshape(DC, 128).T)
    fw_ = np.asarray(inputs["L0_ffn_weight"])
    fb_ = np.asarray(inputs["L0_ffn_bias"])
    fws_ = np.asarray(inputs["L0_ffn_weight_scale"])
    fis_ = np.asarray(inputs["L0_ffn_input_scale"])

    fb_pc = np.ascontiguousarray(fb_.reshape(DC, 128).T)
    fws_pc = np.ascontiguousarray(fws_.reshape(DC, 128).T)
    fis_bc = np.ascontiguousarray(np.broadcast_to(fis_.reshape(1, 1), (128, 1)))
    identm = np.eye(128, dtype=ml_dtypes.bfloat16)

    nc = build_program()
    in_maps = []
    for c in range(8):
        b = c // 4
        q0 = (c % 4) * OWN
        xr = np.ascontiguousarray(np.roll(x[b], -q0, axis=0))
        in_maps.append(dict(
            xb=xr, wq=wq_, wk=wk_, wv=wv_, wo=wo_, fw=fw_,
            bq_pc=bq_, bk_pc=bk_, bv_pc=bv_, bo_pc=bo_,
            fb_pc=fb_pc, fws_pc=fws_pc, fis_bc=fis_bc,
            ident=identm,
        ))
    res = run_bass_kernel_spmd(nc, in_maps, core_ids=list(range(8)))
    outs = [res.results[c]["y"] for c in range(8)]
    full = np.concatenate(outs, axis=0).reshape(2, T, D)
    return full.astype(np.float32)


# version 26 (final)
# speedup vs baseline: 1.0902x; 1.0039x over previous
import numpy as np
import ml_dtypes
import contextlib
import concourse.bass as bass
import concourse.mybir as mybir
from concourse.bass_utils import run_bass_kernel_spmd

F32 = mybir.dt.float32
BF16 = mybir.dt.bfloat16
AF = mybir.ActivationFunctionType
ALU = mybir.AluOpType
AX = mybir.AxisListType.X

D = 4096
T = 4096
QH = 32
KVH = 8
HD = 128
GQ = QH // KVH
OWN = 1024
NQT = OWN // 128
NBT = T // 128
DC = D // 128
HALF = D // 2
SCALE = float(HD) ** -0.5
EPS = 1e-5

ENGS = ("tensor", "vector", "scalar", "gpsimd", "sync")
SEMN = ("PE", "DV", "SC", "GP", "QC", "QX0", "QX1", "QK0", "QK1", "QV", "QV2",
        "QW0", "QW1", "QT0", "QT1", "QU0", "QU1", "QQ", "H2S", "YS", "YS2")


class Prog:
    def __init__(self, sems):
        self.ops = {e: [] for e in ENGS}
        self.sems = sems
        self.cnt = {k: 0 for k in sems}

    def emit(self, eng, fn):
        self.ops[eng].append(fn)

    def wait(self, eng, deps):
        best = {}
        for s, v in deps:
            if v > 0 and v > best.get(s, 0):
                best[s] = v
        if best:
            deps2 = [(self.sems[s], v) for s, v in best.items()]
            def f(e, deps2=deps2):
                for h, v in deps2:
                    e.wait_ge(h, v)
            self.emit(eng, f)

    def join(self):
        s = dict(self.cnt)
        for eng in ENGS:
            self.wait(eng, [(k, s[k]) for k in SEMN])

    def flush(self, block):
        ops = self.ops
        self.ops = {e: [] for e in ENGS}
        if ops["gpsimd"]:
            @block.gpsimd
            def _(e):
                for f in ops["gpsimd"]:
                    f(e)
        if ops["tensor"]:
            @block.tensor
            def _(e):
                for f in ops["tensor"]:
                    f(e)
        if ops["vector"]:
            @block.vector
            def _(e):
                for f in ops["vector"]:
                    f(e)
        if ops["scalar"]:
            @block.scalar
            def _(e):
                for f in ops["scalar"]:
                    f(e)
        if ops["sync"]:
            @block.sync
            def _(e):
                for f in ops["sync"]:
                    f(e)


def build_program():
    nc = bass.Bass("TRN2", target_bir_lowering=False)

    xb = nc.dram_tensor("xb", [T, D], F32, kind="ExternalInput").ap()
    wq = nc.dram_tensor("wq", [D, D], F32, kind="ExternalInput").ap()
    wk = nc.dram_tensor("wk", [D, KVH * HD], F32, kind="ExternalInput").ap()
    wv = nc.dram_tensor("wv", [D, KVH * HD], F32, kind="ExternalInput").ap()
    wo = nc.dram_tensor("wo", [D, D], F32, kind="ExternalInput").ap()
    fw = nc.dram_tensor("fw", [D, D], F32, kind="ExternalInput").ap()
    bq_pc = nc.dram_tensor("bq_pc", [128, DC], F32, kind="ExternalInput").ap()
    bk_pc = nc.dram_tensor("bk_pc", [128, KVH], F32, kind="ExternalInput").ap()
    bv_pc = nc.dram_tensor("bv_pc", [128, KVH], F32, kind="ExternalInput").ap()
    bo_pc = nc.dram_tensor("bo_pc", [128, DC], F32, kind="ExternalInput").ap()
    fb_pc = nc.dram_tensor("fb_pc", [128, DC], F32, kind="ExternalInput").ap()
    fws_pc = nc.dram_tensor("fws_pc", [128, DC], F32, kind="ExternalInput").ap()
    fis_bc = nc.dram_tensor("fis_bc", [128, 1], F32, kind="ExternalInput").ap()
    ident = nc.dram_tensor("ident", [128, 128], BF16, kind="ExternalInput").ap()
    y = nc.dram_tensor("y", [OWN, D], F32, kind="ExternalOutput").ap()

    kT_d = nc.dram_tensor("kT_d", [KVH, 128, T], BF16, kind="Internal").ap()
    v_d = nc.dram_tensor("v_d", [NBT, 128, KVH * HD], BF16, kind="Internal").ap()
    h2_d = nc.dram_tensor("h2_d", [OWN, D], F32, kind="Internal").ap()

    es = contextlib.ExitStack()
    with es:
        identb = es.enter_context(nc.sbuf_tensor("identb", [128, 128], BF16))
        ones_col = es.enter_context(nc.sbuf_tensor("ones_col", [128, 1], BF16))
        ones_row_f = es.enter_context(nc.sbuf_tensor("ones_row_f", [1, 128], F32))
        fb_sb = es.enter_context(nc.sbuf_tensor("fb_sb", [128, DC], F32))
        fws_sb = es.enter_context(nc.sbuf_tensor("fws_sb", [128, DC], F32))
        nfws_sb = es.enter_context(nc.sbuf_tensor("nfws_sb", [128, DC], F32))
        fis_sb = es.enter_context(nc.sbuf_tensor("fis_sb", [128, 1], F32))
        recip_sb = es.enter_context(nc.sbuf_tensor("recip_sb", [128, DC], F32))
        wsis_sb = es.enter_context(nc.sbuf_tensor("wsis_sb", [128, DC], F32))
        lconst = es.enter_context(nc.sbuf_tensor("lconst", [128, 8], F32))
        bq_sb = es.enter_context(nc.sbuf_tensor("bq_sb", [128, DC], F32))
        bk_sb = es.enter_context(nc.sbuf_tensor("bk_sb", [128, KVH], F32))
        bv_sb = es.enter_context(nc.sbuf_tensor("bv_sb", [128, KVH], F32))
        bo_sb = es.enter_context(nc.sbuf_tensor("bo_sb", [128, DC], F32))

        sems = {}
        for name in SEMN:
            sems[name] = es.enter_context(nc.semaphore(name))
        block = es.enter_context(nc.Block())
        P = Prog(sems)

        def dma(queue_eng, sem_name, out, in_, deps=()):
            P.wait(queue_eng, deps)
            P.cnt[sem_name] += 16
            v = P.cnt[sem_name]
            h = sems[sem_name]
            P.emit(queue_eng, lambda e, out=out, in_=in_, h=h:
                   e.dma_start(out=out, in_=in_).then_inc(h, 16))
            return v

        def op(eng, sem_name, fn, deps=(), selfwait=True):
            if selfwait and eng in ("vector", "scalar", "gpsimd"):
                deps = list(deps) + [(sem_name, P.cnt[sem_name])]
            P.wait(eng, deps)
            P.cnt[sem_name] += 1
            v = P.cnt[sem_name]
            h = sems[sem_name]
            P.emit(eng, lambda e, fn=fn, h=h: fn(e).then_inc(h, 1))
            return v

        def waiter(eng, sem_name, v):
            if v > 0:
                P.emit(eng, lambda e, h=sems[sem_name], v=v: e.wait_ge(h, v))

        dma("sync", "QC", identb[:], ident)
        dma("sync", "QC", fb_sb[:], fb_pc)
        dma("sync", "QC", fws_sb[:], fws_pc)
        dma("sync", "QC", fis_sb[:], fis_bc)
        dma("sync", "QC", bq_sb[:], bq_pc)
        dma("sync", "QC", bk_sb[:], bk_pc)
        dma("sync", "QC", bv_sb[:], bv_pc)
        v_qc = dma("sync", "QC", bo_sb[:], bo_pc)
        op("vector", "DV", lambda e: e.memset(ones_col[:], 1.0))
        v_ones = op("vector", "DV", lambda e: e.memset(ones_row_f[:], 1.0))
        v_wsis = op("vector", "DV",
                    lambda e: e.tensor_single_scalar(wsis_sb[:], fws_sb[:], fis_sb[:], ALU.mult),
                    deps=[("QC", v_qc)])
        v_recip = op("vector", "DV", lambda e: e.reciprocal(recip_sb[:], wsis_sb[:]),
                     deps=[("DV", v_wsis)])
        v_nfws = op("vector", "DV",
                    lambda e: e.tensor_single_scalar(nfws_sb[:], fws_sb[:], -1.0, ALU.mult),
                    deps=[("QC", v_qc)])
        v_nfb = op("vector", "DV",
                   lambda e: e.tensor_single_scalar(fb_sb[:], fb_sb[:], -1.0, ALU.mult),
                   deps=[("QC", v_qc)])
        v_nrecip = op("vector", "DV",
                      lambda e: e.tensor_single_scalar(recip_sb[:], recip_sb[:], -1.0, ALU.mult),
                      deps=[("DV", v_recip)])
        P.wait("tensor", [("QC", v_qc)])

        def ladder_biases(kind, bits):
            delta = 0.0
            out = []
            for (t, engine) in bits:
                th = float(t) if kind == "floor" else float(t) - 0.5
                thr = th - delta
                if engine == "sc":
                    out.append(-thr)
                    delta -= float(t) / 2.0
            return out

        FLOOR_BITS = [(128.0, "sc"), (64.0, "sc"), (32.0, "sc"), (16.0, "sc"),
                      (8.0, "sc"), (4.0, "sc"), (2.0, "sc"), (1.0, "sc")]
        ROUND_BITS = [(4.0, "sc"), (2.0, "dve"), (1.0, "sc")]
        _cvals = ladder_biases("floor", FLOOR_BITS) + ladder_biases("round", ROUND_BITS) + [-8.0]
        _cvals = list(dict.fromkeys(_cvals))
        cmap = {}
        v_lconst = 0
        for _i, _v in enumerate(_cvals):
            v_lconst = op("vector", "DV",
                          lambda e, _i=_i, _v=_v: e.memset(lconst[:, _i:_i + 1], _v))
            cmap[_v] = lconst[:, _i:_i + 1]

        def ln_stats(xt_ap, sq_scr_ap, st, deps_dv, deps_sc):
            v1 = op("vector", "DV", lambda e: e.reduce_sum(st["ssum"], xt_ap, axis=AX),
                    deps=deps_dv)
            v2 = op("scalar", "SC",
                    lambda e: e.activation(sq_scr_ap, xt_ap, AF.Square, accum_out=st["ssq"]),
                    deps=deps_sc)
            op("vector", "DV",
               lambda e: e.tensor_single_scalar(st["negm"], st["ssum"], -1.0 / D, ALU.mult),
               deps=[("DV", v1)])
            op("vector", "DV",
               lambda e: e.tensor_scalar(st["vare"], st["ssq"], 1.0 / D, EPS, ALU.mult, ALU.add),
               deps=[("SC", v2)])
            op("vector", "DV",
               lambda e: e.tensor_tensor(st["msq"], st["negm"], st["negm"], ALU.mult))
            v5 = op("vector", "DV",
                    lambda e: e.tensor_tensor(st["vare"], st["vare"], st["msq"], ALU.subtract))
            v6 = op("scalar", "SC", lambda e: e.activation(st["sd"], st["vare"], AF.Sqrt),
                    deps=[("DV", v5)])
            v_rstd = op("vector", "DV", lambda e: e.reciprocal(st["rstd"], st["sd"]),
                        deps=[("SC", v6)])
            return v_rstd

        def emit_ladder(kind, a_ap, r_ap, g_ap, out_ap, bits, deps):
            delta = 0.0
            cur = a_ap
            vr = list(deps)
            for (t, engine) in bits:
                th = float(t) if kind == "floor" else float(t) - 0.5
                thr = th - delta
                if engine == "dve":
                    vg = op("vector", "DV",
                            lambda e, cur=cur, thr=thr, t=t: e.tensor_scalar(
                                g_ap, cur, float(thr), -float(t), ALU.is_ge, ALU.mult),
                            deps=vr)
                    vnew = op("vector", "DV",
                              lambda e, cur=cur: e.tensor_tensor(r_ap, cur, g_ap, ALU.add),
                              deps=[("DV", vg)])
                else:
                    bias_ap = cmap[-float(thr)]
                    vs = op("scalar", "SC",
                            lambda e, cur=cur, bias_ap=bias_ap: e.activation(
                                g_ap, cur, AF.Sign, bias=bias_ap),
                            deps=list(vr) + [("DV", v_lconst)])
                    vnew = op("vector", "DV",
                              lambda e, cur=cur, t=t: e.scalar_tensor_tensor(
                                  r_ap, g_ap, -float(t) / 2.0, cur, ALU.mult, ALU.add),
                              deps=[("SC", vs)])
                    delta -= float(t) / 2.0
                cur = r_ap
                vr = [("DV", vnew)]
            vfin = op("vector", "DV",
                      lambda e, delta=delta: e.scalar_tensor_tensor(
                          out_ap, r_ap, float(delta), a_ap, ALU.add, ALU.subtract),
                      deps=vr)
            return vfin

        qT_d = nc.dram_tensor("qT_d", [QH, 128, OWN], BF16, kind="Internal").ap()
        with contextlib.ExitStack() as p1:
            xt = p1.enter_context(nc.sbuf_tensor("p1_xt", [128, 2, D], F32))
            u_tok = p1.enter_context(nc.sbuf_tensor("p1_u", [128, D], BF16))
            uT_g = p1.enter_context(nc.sbuf_tensor("p1_uT", [128, 2, DC, 1024], BF16))
            w_sb = p1.enter_context(nc.sbuf_tensor("p1_w", [128, 2, DC, 256], BF16))
            kv_stage = p1.enter_context(nc.sbuf_tensor("p1_st", [128, 2, 1024], BF16))
            vt_stage = p1.enter_context(nc.sbuf_tensor("p1_vt", [128, 1, 8, 128], BF16))
            sttt = p1.enter_context(nc.sbuf_tensor("p1_stt", [128, 8], F32))
            stt = {k: sttt[:, i:i + 1] for i, k in enumerate(
                ("ssum", "ssq", "negm", "msq", "vare", "sd", "rstd"))}
            psA = p1.enter_context(nc.psum_tensor("p1_psA", [128, 2, 1024], F32))
            psT = p1.enter_context(nc.psum_tensor("p1_psT", [128, 2, 1024], BF16))

            st = dict(dv_u=[0, 0], pe_u=0, pe_w=[0, 0], dv_ev=[0, 0],
                      kv_dma=[0, 0], vt_dma=[0, 0], uT_rdy=[0, 0],
                      wctr=0, pctr=0, tctr=0, dv_psT=[0, 0])

            def emit_tile_ln(g, i):
                t = g * 8 + i
                buf = t % 2
                qx = "QX%d" % buf
                vx = dma("sync", qx, xt[:, buf, :], xb[t * 128:(t + 1) * 128, :],
                         deps=[("DV", st["dv_u"][buf])])
                v_rstd = ln_stats(xt[:, buf, :], u_tok[:], stt,
                                  deps_dv=[(qx, vx)],
                                  deps_sc=[(qx, vx), ("PE", st["pe_u"]),
                                           ("DV", st["dv_u"][1 - buf])])
                vu = op("vector", "DV",
                        lambda e, buf=buf: e.tensor_scalar(
                            u_tok[:], xt[:, buf, :], stt["negm"], stt["rstd"],
                            ALU.add, ALU.mult),
                        deps=[("DV", v_rstd), ("PE", st["pe_u"])])
                st["dv_u"][buf] = vu
                st["vu_p"] = vu

            def emit_tile_tp(g, i):
                gb = g % 2
                vu = st["vu_p"]
                last_dv = 0
                pe_v = 0
                for gg in range(4):
                    hb = st["tctr"] % 2
                    st["tctr"] += 1
                    for k in range(8):
                        dc = gg * 8 + k
                        d_ = ([("DV", vu)] if (gg == 0 and k == 0) else []) + \
                             ([("DV", st["dv_psT"][hb])] if k == 0 else [])
                        pe_v = op("tensor", "PE",
                                  lambda e, dc=dc, k=k, hb=hb: e.transpose(
                                      psT[:, hb, k * 128:(k + 1) * 128],
                                      u_tok[:, dc * 128:(dc + 1) * 128], identb[:]),
                                  deps=d_)
                    last_dv = op("vector", "DV",
                                 lambda e, gg=gg, gb=gb, i=i, hb=hb: e.tensor_copy(
                                     uT_g[:, gb, gg * 8:(gg + 1) * 8, i * 128:(i + 1) * 128],
                                     psT[:, hb, :]),
                                 deps=[("PE", pe_v)])
                    st["dv_psT"][hb] = last_dv
                st["pe_u"] = pe_v
                if i == 7:
                    st["uT_rdy"][gb] = last_dv

            def emit_tile(g, i):
                emit_tile_ln(g, i)
                emit_tile_tp(g, i)

            def emit_b(g, kind, oc2):
                gb = g % 2
                wbuf = st["wctr"] % 2
                st["wctr"] += 1
                if kind == "q":
                    wsrc, bias_src = wq, bq_sb
                elif kind == "k":
                    wsrc, bias_src = wk, bk_sb
                else:
                    wsrc, bias_src = wv, bv_sb
                qw = "QW%d" % wbuf
                vw = dma("gpsimd", qw, w_sb[:, wbuf, :, :],
                         wsrc[:, oc2 * 256:(oc2 + 1) * 256].rearrange("(c p) n -> p c n", p=128),
                         deps=[("PE", st["pe_w"][wbuf])])
                for sub in range(2):
                    oc = 2 * oc2 + sub
                    pbuf = st["pctr"] % 2
                    st["pctr"] += 1
                    first_deps = [(qw, vw), ("DV", st["dv_ev"][pbuf]),
                                  ("DV", st["uT_rdy"][gb])]
                    pe_last = 0
                    for dc in range(DC):
                        for s in range(2):
                            pe_last = op("tensor", "PE",
                                         lambda e, dc=dc, s=s, gb=gb, pbuf=pbuf, wbuf=wbuf, sub=sub: e.matmul(
                                             psA[:, pbuf, s * 512:(s + 1) * 512],
                                             w_sb[:, wbuf, dc, sub * 128:(sub + 1) * 128],
                                             uT_g[:, gb, dc, s * 512:(s + 1) * 512],
                                             start=(dc == 0), stop=(dc == DC - 1)),
                                         deps=(first_deps if (dc == 0 and s == 0) else []))
                    st["pe_w"][wbuf] = pe_last
                    dv_e = op("vector", "DV",
                              lambda e, pbuf=pbuf, bias_src=bias_src, oc=oc: e.tensor_single_scalar(
                                  kv_stage[:, pbuf, :], psA[:, pbuf, :],
                                  bias_src[:, oc:oc + 1], ALU.add),
                              deps=[("PE", pe_last), ("QK%d" % pbuf, st["kv_dma"][pbuf]),
                                    ("QC", v_qc)])
                    st["dv_ev"][pbuf] = dv_e
                    if kind == "q":
                        st["kv_dma"][pbuf] = dma("sync", "QK%d" % pbuf, qT_d[oc, :, :],
                                                 kv_stage[:, pbuf, :], deps=[("DV", dv_e)])
                    elif kind == "k":
                        st["kv_dma"][pbuf] = dma(
                            "sync", "QK%d" % pbuf,
                            kT_d[oc, :, g * 1024:(g + 1) * 1024],
                            kv_stage[:, pbuf, :], deps=[("DV", dv_e)])
                    else:
                        hb = st["tctr"] % 2
                        st["tctr"] += 1
                        pe_tv = 0
                        for k in range(8):
                            d_ = ([("DV", dv_e)] if k == 0 else []) + \
                                 ([("DV", st["dv_psT"][hb])] if k == 0 else [])
                            pe_tv = op("tensor", "PE",
                                       lambda e, k=k, pbuf=pbuf, hb=hb: e.transpose(
                                           psT[:, hb, k * 128:(k + 1) * 128],
                                           kv_stage[:, pbuf, k * 128:(k + 1) * 128],
                                           identb[:]),
                                       deps=d_)
                        st["kv_dma"][pbuf] = P.cnt["QK%d" % pbuf]
                        dv_v = op("vector", "DV",
                                  lambda e, hb=hb: e.tensor_copy(
                                      vt_stage[:, 0, :, :], psT[:, hb, :]),
                                  deps=[("PE", pe_tv), ("QV", max(st["vt_dma"]))])
                        st["dv_psT"][hb] = dv_v
                        kt0 = g * 8
                        st["vt_dma"][pbuf] = dma(
                            "sync", "QV",
                            v_d[kt0:kt0 + 8, :, oc * 128:(oc + 1) * 128]
                            .rearrange("t p n -> p t n"),
                            vt_stage[:, 0, :, :],
                            deps=[("DV", dv_v)])

            for i in range(8):
                emit_tile(0, i)
            for g in range(4):
                bunits = []
                if g == 0:
                    bunits += [("q", oc2) for oc2 in range(QH // 2)]
                bunits += [("k", j) for j in range(KVH // 2)]
                bunits += [("v", j) for j in range(KVH // 2)]
                aunits = list(range(8)) if g < 3 else []
                na, nb = len(aunits), len(bunits)
                a_ln = 0
                a_tp = 0
                for bi, (kind, oc) in enumerate(bunits):
                    if na and a_ln < na and a_ln == a_tp and \
                            (bi + 2) * na >= (a_ln + 1) * nb:
                        emit_tile_ln(g + 1, aunits[a_ln])
                        a_ln += 1
                    emit_b(g, kind, oc)
                    while na and a_tp < a_ln and (bi + 1) * na >= (a_tp + 1) * nb:
                        emit_tile_tp(g + 1, aunits[a_tp])
                        a_tp += 1
                while na and a_tp < na:
                    if a_ln == a_tp:
                        emit_tile_ln(g + 1, aunits[a_ln])
                        a_ln += 1
                    emit_tile_tp(g + 1, aunits[a_tp])
                    a_tp += 1
            P.join()
            P.flush(block)

        AT_cm = nc.sbuf_tensor("AT", [128, QH, OWN], BF16, side="right")
        AT = AT_cm.__enter__()
        with contextlib.ExitStack() as p3:
            qT = p3.enter_context(nc.sbuf_tensor("p3_qT", [128, QH, OWN], BF16))
            kT_sb = p3.enter_context(nc.sbuf_tensor("p3_kT", [128, 2, T], BF16))
            v_tok = p3.enter_context(nc.sbuf_tensor("p3_v", [128, 2, NBT, 128], BF16))
            ptile = p3.enter_context(nc.sbuf_tensor("p3_p", [128, 2, 1024], BF16))
            sqs = p3.enter_context(nc.sbuf_tensor("p3_sqs", [128, T], BF16))
            mk_sb = p3.enter_context(nc.sbuf_tensor("p3_mk", [1, 16], F32))
            cb = p3.enter_context(nc.sbuf_tensor("p3_cb", [128, 4], F32))
            sig_sb = p3.enter_context(nc.sbuf_tensor("p3_sig", [1, 1024], F32))
            rs_sb = p3.enter_context(nc.sbuf_tensor("p3_rs", [1, 1024], F32))
            bc_sb = p3.enter_context(nc.sbuf_tensor("p3_bc", [128, 1024], F32))
            psS = p3.enter_context(nc.psum_tensor("p3_psS", [128, 2, 1024], F32))
            psA_ = p3.enter_context(nc.psum_tensor("p3_psA", [128, 1024], F32))
            psG = p3.enter_context(nc.psum_tensor("p3_psG", [128, 1024], F32))

            v_qt = dma("sync", "QQ", qT[:, :, :], qT_d.rearrange("o p n -> p o n"))

            pe_group_done = [0, 0]
            pe_sqs_read = 0
            dv_head_tail = 0
            sc_exp = [0, 0]
            pe_pv_sig = [0, 0]
            for j in range(KVH):
                gbuf = j % 2
                qt_s = "QT%d" % gbuf
                qu_s = "QU%d" % gbuf
                vk = dma("sync", qt_s, kT_sb[:, gbuf, :], kT_d[j, :, :],
                         deps=[("PE", pe_group_done[gbuf])])
                vv = 0
                for q4 in range(4):
                    vv = dma("sync", qu_s,
                             v_tok[:, gbuf, q4 * 8:(q4 + 1) * 8, :],
                             v_d[q4 * 8:(q4 + 1) * 8, :, j * 128:(j + 1) * 128]
                             .rearrange("t p n -> p t n"))
                dv_sq = op("vector", "DV",
                           lambda e, gbuf=gbuf: e.tensor_tensor(
                               sqs[:], kT_sb[:, gbuf, :], kT_sb[:, gbuf, :], ALU.mult),
                           deps=[(qt_s, vk), ("PE", pe_sqs_read)])
                dvm = [0, 0]
                for s in range(8):
                    slot = s % 2
                    d_ = [("DV", dv_sq)] if s == 0 else []
                    waiter("tensor", "DV", dvm[slot])
                    pe_n = op("tensor", "PE",
                              lambda e, s=s, slot=slot: e.matmul(
                                  psG[0:1, slot * 512:(slot + 1) * 512],
                                  ones_col[:], sqs[:, s * 512:(s + 1) * 512],
                                  start=True, stop=True),
                              deps=d_)
                    dvm[slot] = op("vector", "DV",
                                   lambda e, s=s, slot=slot: e.reduce_max(
                                       mk_sb[0:1, s:s + 1],
                                       psG[0:1, slot * 512:(slot + 1) * 512], axis=AX),
                                   deps=[("PE", pe_n)])
                qu_s2, vv2 = qu_s, vv
                op("vector", "DV",
                   lambda e: e.reduce_max(mk_sb[0:1, 8:9], mk_sb[0:1, 0:8], axis=AX))

                dv_cb = 0
                for hh in range(GQ):
                    h = j * GQ + hh
                    dv_qsq = op("vector", "DV",
                                lambda e, h=h: e.tensor_tensor(
                                    sqs[:, 0:1024], qT[:, h, :], qT[:, h, :], ALU.mult),
                                deps=[("PE", pe_sqs_read), ("QQ", v_qt)])
                    pe_n = 0
                    for s in range(2):
                        pe_n = op("tensor", "PE",
                                  lambda e, s=s: e.matmul(
                                      psG[0:1, s * 512:(s + 1) * 512],
                                      ones_col[:], sqs[:, s * 512:(s + 1) * 512],
                                      start=True, stop=True),
                                  deps=([("DV", dv_qsq)] if s == 0 else []))
                    pe_sqs_read = pe_n
                    op("vector", "DV",
                       lambda e: e.reduce_max(mk_sb[0:1, 9:10], psG[0:1, 0:512], axis=AX),
                       deps=[("PE", pe_n)])
                    op("vector", "DV",
                       lambda e: e.reduce_max(mk_sb[0:1, 10:11], psG[0:1, 512:1024], axis=AX))
                    op("vector", "DV",
                       lambda e: e.reduce_max(mk_sb[0:1, 11:12], mk_sb[0:1, 9:11], axis=AX))
                    dv_c1 = op("vector", "DV",
                               lambda e: e.tensor_tensor(mk_sb[0:1, 12:13], mk_sb[0:1, 11:12],
                                                         mk_sb[0:1, 8:9], ALU.add))
                    dv_c3 = op("vector", "DV",
                               lambda e: e.tensor_single_scalar(
                                   mk_sb[0:1, 14:15], mk_sb[0:1, 12:13], -SCALE / 2.0, ALU.mult),
                               deps=[("DV", dv_c1)])
                    pe_cb = op("tensor", "PE",
                               lambda e: e.matmul(psG[:, 0:1], ones_row_f[0:1, :],
                                                  mk_sb[0:1, 14:15], start=True, stop=True),
                               deps=[("DV", dv_c3)])
                    dv_cb = op("vector", "DV",
                               lambda e, hh=hh: e.tensor_copy(cb[:, hh:hh + 1], psG[:, 0:1]),
                               deps=[("PE", pe_cb)])
                waiter("tensor", "DV", dv_cb)
                dv_b_prev = 0

                for hh in range(GQ):
                    h = j * GQ + hh
                    def emit_s(kc):
                        sb_ = kc % 2
                        pe_s = 0
                        for s in range(2):
                            d_ = ([("SC", sc_exp[sb_])] if s == 0 else [])
                            pe_s = op("tensor", "PE",
                                      lambda e, kc=kc, s=s, sb_=sb_, gbuf=gbuf, h=h: e.matmul(
                                          psS[:, sb_, s * 512:(s + 1) * 512],
                                          kT_sb[:, gbuf, kc * 128:(kc + 1) * 128],
                                          qT[:, h, s * 512:(s + 1) * 512],
                                          start=True, stop=True),
                                      deps=d_)
                        return pe_s

                    def emit_exp(kc, pe_s):
                        sb_ = kc % 2
                        d_exp = [("PE", pe_s), ("PE", pe_pv_sig[sb_])]
                        if kc == 0:
                            d_exp.append(("DV", dv_cb))
                        vexp = op("scalar", "SC",
                                  lambda e, sb_=sb_, hh=hh: e.activation(
                                      ptile[:, sb_, :], psS[:, sb_, :], AF.Exp,
                                      bias=cb[:, hh:hh + 1], scale=SCALE),
                                  deps=d_exp, selfwait=False)
                        sc_exp[sb_] = vexp
                        return vexp

                    def emit_pv(kc, vexp):
                        sb_ = kc % 2
                        for s in range(2):
                            d_ = []
                            if s == 0:
                                d_ = [("SC", vexp)]
                                if kc == 0:
                                    d_ += [(qu_s2, vv2), ("DV", dv_head_tail)]
                            op("tensor", "PE",
                               lambda e, kc=kc, s=s, sb_=sb_, gbuf=gbuf: e.matmul(
                                   psA_[:, s * 512:(s + 1) * 512],
                                   v_tok[:, gbuf, kc, :],
                                   ptile[:, sb_, s * 512:(s + 1) * 512],
                                   start=(kc == 0), stop=(kc == NBT - 1)),
                               deps=d_)
                        pe_g = 0
                        for s in range(2):
                            d2_ = ([("DV", dv_b_prev)] if (kc == 0 and s == 0) else [])
                            pe_g = op("tensor", "PE",
                                      lambda e, kc=kc, s=s, sb_=sb_: e.matmul(
                                          psG[0:1, s * 512:(s + 1) * 512],
                                          ones_col[:],
                                          ptile[:, sb_, s * 512:(s + 1) * 512],
                                          start=(kc == 0), stop=(kc == NBT - 1)),
                                      deps=d2_)
                        pe_pv_sig[sb_] = pe_g

                    pe_s_arr = {0: emit_s(0), 1: emit_s(1)}
                    for kc in range(NBT):
                        vexp = emit_exp(kc, pe_s_arr[kc])
                        if kc + 2 < NBT:
                            pe_s_arr[kc + 2] = emit_s(kc + 2)
                        emit_pv(kc, vexp)
                    pe_head_mms = P.cnt["PE"]
                    dv_s2 = op("vector", "DV",
                               lambda e: e.reciprocal(rs_sb[:], psG[0:1, :]),
                               deps=[("PE", pe_head_mms)])
                    pe_b = 0
                    for s in range(2):
                        d_ = ([("DV", dv_s2)] if s == 0 else [])
                        pe_b = op("tensor", "PE",
                                  lambda e, s=s: e.matmul(
                                      psG[:, s * 512:(s + 1) * 512],
                                      ones_row_f[0:1, :],
                                      rs_sb[0:1, s * 512:(s + 1) * 512],
                                      start=True, stop=True),
                                  deps=d_)
                    dv_b = op("vector", "DV",
                              lambda e: e.tensor_copy(bc_sb[:], psG[:, :]),
                              deps=[("PE", pe_b)])
                    dv_b_prev = dv_b
                    dv_n = op("vector", "DV",
                              lambda e, h=h: e.tensor_tensor(
                                  AT[:, h, :], psA_[:], bc_sb[:], ALU.mult),
                              deps=[("DV", dv_b)])
                    dv_head_tail = dv_n
                pe_group_done[gbuf] = P.cnt["PE"]
            P.join()
            P.flush(block)

        big_cm = nc.sbuf_tensor("big_tok", [128, NQT, D], BF16)
        big_tok = big_cm.__enter__()
        with contextlib.ExitStack() as p4:
            wo_sb = p4.enter_context(nc.sbuf_tensor("p4_wo", [128, 2, DC, 128], BF16))
            ot_stage = p4.enter_context(nc.sbuf_tensor("p4_ot", [128, 2, 1024], BF16))
            psA = p4.enter_context(nc.psum_tensor("p4_psA", [128, 2, 1024], F32))
            psT = p4.enter_context(nc.psum_tensor("p4_psT", [128, 2, 1024], BF16))

            pe_o = [0, 0]
            dv_o_evac = [0, 0]
            pe_ot_read = [0, 0]
            dv_psT = [0, 0]
            for oc in range(DC):
                wbuf = oc % 2
                qw = "QW%d" % wbuf
                vw = dma("gpsimd", qw, wo_sb[:, wbuf, :, :],
                         wo[:, oc * 128:(oc + 1) * 128].rearrange("(c p) n -> p c n", p=128),
                         deps=[("PE", pe_o[wbuf])])
                pbuf = oc % 2
                first_deps = [(qw, vw), ("DV", dv_o_evac[pbuf])]
                pe_last = 0
                for dc in range(DC):
                    for s in range(2):
                        pe_last = op("tensor", "PE",
                                     lambda e, dc=dc, s=s, pbuf=pbuf, wbuf=wbuf: e.matmul(
                                         psA[:, pbuf, s * 512:(s + 1) * 512],
                                         wo_sb[:, wbuf, dc, :],
                                         AT[:, dc, s * 512:(s + 1) * 512],
                                         start=(dc == 0), stop=(dc == DC - 1)),
                                     deps=(first_deps if (dc == 0 and s == 0) else []))
                pe_o[wbuf] = pe_last
                dv_e = op("vector", "DV",
                          lambda e, pbuf=pbuf, oc=oc: e.tensor_single_scalar(
                              ot_stage[:, pbuf, :], psA[:, pbuf, :],
                              bo_sb[:, oc:oc + 1], ALU.add),
                          deps=[("PE", pe_last), ("PE", pe_ot_read[pbuf]),
                                ("QC", v_qc)])
                dv_o_evac[pbuf] = dv_e
                hb = oc % 2
                pe_t = 0
                for k in range(8):
                    d_ = ([("DV", dv_e), ("DV", dv_psT[hb])] if k == 0 else [])
                    pe_t = op("tensor", "PE",
                              lambda e, k=k, pbuf=pbuf, hb=hb: e.transpose(
                                  psT[:, hb, k * 128:(k + 1) * 128],
                                  ot_stage[:, pbuf, k * 128:(k + 1) * 128],
                                  identb[:]),
                              deps=d_)
                pe_ot_read[pbuf] = pe_t
                dv_t = op("vector", "DV",
                          lambda e, oc=oc, hb=hb: e.tensor_copy(
                              big_tok[:, :, oc * 128:(oc + 1) * 128], psT[:, hb, :]),
                          deps=[("PE", pe_t)])
                dv_psT[hb] = dv_t
            P.join()
            P.flush(block)

        AT_cm.__exit__(None, None, None)

        xiT_cm = nc.sbuf_tensor("xiT", [128, DC, OWN], BF16, side="right")
        xiT = xiT_cm.__enter__()
        with contextlib.ExitStack() as p4b:
            xtile = p4b.enter_context(nc.sbuf_tensor("p4b_x", [128, D], F32))
            h2t = p4b.enter_context(nc.sbuf_tensor("p4b_h2", [128, D], F32))
            au = p4b.enter_context(nc.sbuf_tensor("p4b_au", [128, D], F32))
            xs_h = p4b.enter_context(nc.sbuf_tensor("p4b_xs", [128, 2, HALF], BF16))
            r_h = p4b.enter_context(nc.sbuf_tensor("p4b_r", [128, 2, HALF], F32))
            stt = {k: p4b.enter_context(nc.sbuf_tensor("p4b_s_" + k, [128, 1], F32))
                   for k in ("ssum", "ssq", "negm", "msq", "vare", "sd", "rstd")}
            stt = {k: v[:] for k, v in stt.items()}
            rstd3f = p4b.enter_context(nc.sbuf_tensor("p4b_r3f", [128, 1], F32))
            psT = p4b.enter_context(nc.psum_tensor("p4b_psT", [128, 2, 1024], BF16))

            pe_xi_read_ = [0, 0]
            dv_psT4 = [("DV", 0), ("DV", 0)]
            prev = dict(dv_h2=0, h2s=0, sc_a0=0, sc_lad=0, dv_r3f=0,
                        vfin=[0, 0], xi2=[0, 0], xi1=[0, 0])
            for qt in range(NQT):
                if qt >= 2:
                    scr = big_tok[:, qt - 2, :]
                    scr_dep = [("DV", prev["xi2"][0]), ("DV", prev["xi2"][1])]
                elif qt == 1:
                    scr = au[:]
                    scr_dep = [("DV", prev["vfin"][0]), ("DV", prev["vfin"][1])]
                else:
                    scr = au[:]
                    scr_dep = []
                vx = dma("sync", "QX0", xtile[:], xb[qt * 128:(qt + 1) * 128, :],
                         deps=[("DV", prev["dv_h2"])])
                v_rstd2 = ln_stats(big_tok[:, qt, :], scr, stt, deps_dv=[],
                                   deps_sc=scr_dep + [("DV", prev["dv_r3f"])])
                dv_y1 = op("vector", "DV",
                           lambda e, qt=qt: e.tensor_scalar(
                               h2t[:], big_tok[:, qt, :], stt["negm"], stt["rstd"],
                               ALU.add, ALU.mult),
                           deps=[("DV", v_rstd2), ("H2S", prev["h2s"]),
                                 ("SC", prev["sc_a0"])])
                dv_h2 = op("vector", "DV",
                           lambda e: e.tensor_tensor(h2t[:], h2t[:], xtile[:], ALU.add),
                           deps=[("DV", dv_y1), ("QX0", vx)])
                h2s_v = dma("sync", "H2S", h2_d[qt * 128:(qt + 1) * 128, :], h2t[:],
                            deps=[("DV", dv_h2)])
                v_rstd3 = ln_stats(h2t[:], scr, stt,
                                   deps_dv=[("DV", dv_h2)],
                                   deps_sc=[("DV", dv_h2), ("DV", dv_y1)])
                dv_r3f = op("vector", "DV",
                            lambda e: e.tensor_tensor(rstd3f[:], stt["rstd"], fis_sb[:],
                                                      ALU.mult),
                            deps=[("DV", v_rstd3)])
                dv_xs_ = [0, 0]
                for dh in range(2):
                    dv_xs_[dh] = op("vector", "DV",
                                    lambda e, dh=dh: e.tensor_scalar(
                                        r_h[:, dh, :], h2t[:, dh * HALF:(dh + 1) * HALF],
                                        stt["negm"], rstd3f[:], ALU.add, ALU.mult),
                                    deps=[("DV", dv_r3f), ("SC", prev["sc_lad"])])
                sc_a_ = [0, 0]
                sc_ns_ = [0, 0]
                for dh in range(2):
                    sc_a_[dh] = op("scalar", "SC",
                                   lambda e, dh=dh: e.activation(
                                       au[:, dh * HALF:(dh + 1) * HALF],
                                       r_h[:, dh, :], AF.Abs),
                                   deps=[("DV", dv_xs_[dh])])
                ns_ap = [big_tok[:, qt, dh * HALF:(dh + 1) * HALF] for dh in range(2)]
                for dh in range(2):
                    sc_ns_[dh] = op("scalar", "SC",
                                    lambda e, dh=dh, ns=ns_ap[dh]: e.activation(
                                        ns, r_h[:, dh, :], AF.Sign, scale=-1.0),
                                    deps=[("DV", dv_xs_[dh])])
                cur_ = [au[:, 0:HALF], au[:, HALF:2 * HALF]]
                vr_ = [[("SC", sc_a_[0]), ("SC", sc_ns_[0]), ("DV", v_lconst),
                        ("PE", pe_xi_read_[0])],
                       [("SC", sc_a_[1]), ("SC", sc_ns_[1]), ("DV", v_lconst),
                        ("PE", pe_xi_read_[1])]]
                delta = 0.0
                for (t, engine) in FLOOR_BITS:
                    th = float(t)
                    thr = th - delta
                    vc_ = [0, 0]
                    if engine == "dve":
                        for dh in range(2):
                            vc_[dh] = op("vector", "DV",
                                         lambda e, cur=cur_[dh], thr=thr, t=t, dh=dh:
                                         e.tensor_scalar(
                                             xs_h[:, dh, :], cur, float(thr), -float(t),
                                             ALU.is_ge, ALU.mult),
                                         deps=vr_[dh])
                        for dh in range(2):
                            vnew = op("vector", "DV",
                                      lambda e, cur=cur_[dh], dh=dh:
                                      e.tensor_tensor(
                                          r_h[:, dh, :], cur, xs_h[:, dh, :], ALU.add),
                                      deps=[("DV", vc_[dh])])
                            cur_[dh] = r_h[:, dh, :]
                            vr_[dh] = [("DV", vnew)]
                    else:
                        bias_ap = cmap[-float(thr)]
                        for dh in range(2):
                            vc_[dh] = op("scalar", "SC",
                                         lambda e, cur=cur_[dh], bias_ap=bias_ap, dh=dh:
                                         e.activation(
                                             xs_h[:, dh, :], cur, AF.Sign, bias=bias_ap),
                                         deps=vr_[dh])
                        for dh in range(2):
                            vnew = op("vector", "DV",
                                      lambda e, cur=cur_[dh], t=t, dh=dh:
                                      e.scalar_tensor_tensor(
                                          r_h[:, dh, :], xs_h[:, dh, :], -float(t) / 2.0,
                                          cur, ALU.mult, ALU.add),
                                      deps=[("SC", vc_[dh])])
                            cur_[dh] = r_h[:, dh, :]
                            vr_[dh] = [("DV", vnew)]
                        delta -= float(t) / 2.0
                sc_lad = P.cnt["SC"]
                dv_xi_ = [0, 0]
                vfin_ = [0, 0]
                for dh in range(2):
                    vfin_[dh] = op("vector", "DV",
                                   lambda e, delta=delta, dh=dh: e.scalar_tensor_tensor(
                                       xs_h[:, dh, :], r_h[:, dh, :], float(delta),
                                       au[:, dh * HALF:(dh + 1) * HALF], ALU.add,
                                       ALU.subtract),
                                   deps=vr_[dh])
                    dv_xi_[dh] = op("vector", "DV",
                                    lambda e, dh=dh, ns=ns_ap[dh]: e.tensor_tensor(
                                        xs_h[:, dh, :], xs_h[:, dh, :], ns,
                                        ALU.mult),
                                    deps=[("DV", vfin_[dh])])
                for dh in range(2):
                    for g in range(2):
                        hb = g
                        d0 = [("DV", dv_xi_[dh])] if g == 0 else []
                        for k in range(8):
                            d_ = (d0 + [dv_psT4[hb]]) if k == 0 else []
                            pe_v = op("tensor", "PE",
                                      lambda e, g=g, k=k, hb=hb, dh=dh: e.transpose(
                                          psT[:, hb, k * 128:(k + 1) * 128],
                                          xs_h[:, dh, (g * 8 + k) * 128:(g * 8 + k + 1) * 128],
                                          identb[:]),
                                      deps=d_)
                        sc_e = op("scalar", "SC",
                                  lambda e, g=g, dh=dh, qt=qt, hb=hb: e.activation(
                                      xiT[:, dh * 16 + g * 8:dh * 16 + (g + 1) * 8,
                                          qt * 128:(qt + 1) * 128],
                                      psT[:, hb, :], AF.Copy),
                                  deps=[("PE", pe_v)])
                        dv_psT4[hb] = ("SC", sc_e)
                    pe_xi_read_[dh] = pe_v
                prev = dict(dv_h2=dv_h2, h2s=h2s_v, sc_a0=sc_a_[0], sc_lad=sc_lad,
                            dv_r3f=dv_r3f, vfin=vfin_, xi2=prev["xi1"], xi1=dv_xi_)
            P.join()
            P.flush(block)

        big_cm.__exit__(None, None, None)

        ffn_d = nc.dram_tensor("ffn_d", [OWN, D], BF16, kind="Internal").ap()
        with contextlib.ExitStack() as p5:
            fw_sb = p5.enter_context(nc.sbuf_tensor("p5_fw", [128, 2, D], F32))
            q_b = p5.enter_context(nc.sbuf_tensor("p5_q", [128, 2, HALF], F32))
            r_b = p5.enter_context(nc.sbuf_tensor("p5_r", [128, 2, HALF], F32))
            g_b = p5.enter_context(nc.sbuf_tensor("p5_g", [128, 2, HALF], F32))
            wn_b = p5.enter_context(nc.sbuf_tensor("p5_wn", [128, 2, HALF], BF16))
            wT_sb = p5.enter_context(nc.sbuf_tensor("p5_wT", [128, 4, DC, 128], BF16))
            ft_stage = p5.enter_context(nc.sbuf_tensor("p5_ft", [128, 2, 1024], BF16))
            tt_stage = p5.enter_context(nc.sbuf_tensor("p5_tt", [128, 2, 8, 128], BF16))
            psA = p5.enter_context(nc.psum_tensor("p5_psA", [128, 2, 1024], F32))
            psT = p5.enter_context(nc.psum_tensor("p5_psT", [128, 4, 1024], BF16))

            dv_fw_read = [0, 0]
            pe_wn_read = [0, 0]
            tctr5 = [0]
            dv_psT5 = [("DV", 0), ("DV", 0), ("DV", 0), ("DV", 0)]
            pe_wT = [0, 0, 0, 0]
            dv_f_evac = [0, 0]
            pe_ft_read = [0, 0]
            tt_dma = [0, 0]
            dv_wT_of = [("DV", 0), ("DV", 0), ("DV", 0), ("DV", 0)]
            pe_mm5 = [0, 0]
            pend5 = [None]

            def emit_prep_a(oc):
                wbuf = oc % 2
                wb4 = oc % 4
                qw = "QW%d" % wbuf
                vw = dma("gpsimd", qw, fw_sb[:, wbuf, :], fw[oc * 128:(oc + 1) * 128, :],
                         deps=[("DV", dv_fw_read[wbuf])])
                vq_ = [0, 0]
                for par in range(2):
                    vq_[par] = op("vector", "DV",
                                  lambda e, oc=oc, par=par, wbuf=wbuf: e.tensor_scalar(
                                      q_b[:, par, :], fw_sb[:, wbuf, par * HALF:(par + 1) * HALF],
                                      fws_sb[:, oc:oc + 1], 4.5, ALU.mult, ALU.add),
                                  deps=[(qw, vw), ("QC", v_qc)])
                dv_fw_read[wbuf] = vq_[1]
                cur_ = [q_b[:, 0, :], q_b[:, 1, :]]
                vr_ = [[("DV", vq_[0]), ("DV", v_lconst)], [("DV", vq_[1]), ("DV", v_lconst)]]
                delta = 0.0
                for t in (8.0, 4.0, 2.0, 1.0):
                    vs_ = [0, 0]
                    for par in range(2):
                        vs_[par] = op("scalar", "SC",
                                      lambda e, cur=cur_[par], par=par: e.activation(
                                          g_b[:, par, :], cur, AF.Sign, bias=cmap[-8.0]),
                                      deps=vr_[par])
                    for par in range(2):
                        vnew = op("vector", "DV",
                                  lambda e, cur=cur_[par], t=t, par=par: e.scalar_tensor_tensor(
                                      r_b[:, par, :], g_b[:, par, :], -t / 2.0, cur,
                                      ALU.mult, ALU.add),
                                  deps=[("SC", vs_[par])])
                        cur_[par] = r_b[:, par, :]
                        vr_[par] = [("DV", vnew)]
                    delta -= t / 2.0
                vwn_ = [0, 0]
                for par in range(2):
                    vwn_[par] = op("vector", "DV",
                                   lambda e, par=par, delta=delta: e.scalar_tensor_tensor(
                                       wn_b[:, par, :], r_b[:, par, :], delta + 4.0,
                                       q_b[:, par, :], ALU.add, ALU.subtract),
                                   deps=vr_[par] + [("PE", pe_wn_read[par])])
                pend = []
                for par in range(2):
                    pe_v = 0
                    for g in range(2):
                        hb = tctr5[0] % 4
                        tctr5[0] += 1
                        d0 = [("DV", vwn_[par])] if g == 0 else []
                        for k in range(8):
                            d_ = (d0 + [dv_psT5[hb]]) if k == 0 else []
                            pe_v = op("tensor", "PE",
                                      lambda e, g=g, k=k, par=par, hb=hb: e.transpose(
                                          psT[:, hb, k * 128:(k + 1) * 128],
                                          wn_b[:, par, (g * 8 + k) * 128:(g * 8 + k + 1) * 128],
                                          identb[:]),
                                      deps=d_)
                        pend.append((par, g, hb, pe_v))
                    pe_wn_read[par] = pe_v
                pend5[0] = (wb4, pend)

            def emit_prep_b():
                wb4, pend = pend5[0]
                for (par, g, hb, pe_v) in pend:
                    sc_e = op("scalar", "SC",
                              lambda e, g=g, par=par, wb4=wb4, hb=hb: e.activation(
                                  wT_sb[:, wb4, par * 16 + g * 8:par * 16 + (g + 1) * 8, :],
                                  psT[:, hb, :], AF.Copy),
                              deps=[("PE", pe_v)])
                    dv_psT5[hb] = ("SC", sc_e)
                dv_wT_of[wb4] = ("SC", P.cnt["SC"])

            def emit_consume_mm(oc):
                wb4 = oc % 4
                pbuf = oc % 2
                first_deps = [dv_wT_of[wb4], ("DV", dv_f_evac[pbuf])]
                pe_last = 0
                for dc in range(DC):
                    for s in range(2):
                        pe_last = op("tensor", "PE",
                                     lambda e, dc=dc, s=s, pbuf=pbuf, wb4=wb4: e.matmul(
                                         psA[:, pbuf, s * 512:(s + 1) * 512],
                                         wT_sb[:, wb4, dc, :],
                                         xiT[:, dc, s * 512:(s + 1) * 512],
                                         start=(dc == 0), stop=(dc == DC - 1)),
                                     deps=(first_deps if (dc == 0 and s == 0) else []))
                pe_wT[wb4] = pe_last
                pe_mm5[pbuf] = pe_last

            def emit_consume_evac(oc):
                pbuf = oc % 2
                dv_e = op("vector", "DV",
                          lambda e, pbuf=pbuf, oc=oc: e.tensor_scalar(
                              ft_stage[:, pbuf, :], psA[:, pbuf, :],
                              fb_sb[:, oc:oc + 1], recip_sb[:, oc:oc + 1],
                              ALU.add, ALU.mult),
                          deps=[("PE", pe_mm5[pbuf]), ("DV", v_nrecip), ("DV", v_nfb),
                                ("PE", pe_ft_read[pbuf])])
                dv_f_evac[pbuf] = dv_e

            def emit_consume_tail(oc):
                pbuf = oc % 2
                dv_e = dv_f_evac[pbuf]
                hb = tctr5[0] % 4
                tctr5[0] += 1
                pe_t = 0
                for k in range(8):
                    d_ = ([("DV", dv_e), dv_psT5[hb]] if k == 0 else [])
                    pe_t = op("tensor", "PE",
                              lambda e, k=k, pbuf=pbuf, hb=hb: e.transpose(
                                  psT[:, hb, k * 128:(k + 1) * 128],
                                  ft_stage[:, pbuf, k * 128:(k + 1) * 128],
                                  identb[:]),
                              deps=d_)
                pe_ft_read[pbuf] = pe_t
                qv_s = ("QV", "QV2")[pbuf]
                sc_t = op("scalar", "SC",
                          lambda e, pbuf=pbuf, hb=hb: e.activation(
                              tt_stage[:, pbuf, :, :], psT[:, hb, :], AF.Copy),
                          deps=[("PE", pe_t), (qv_s, tt_dma[pbuf])])
                dv_psT5[hb] = ("SC", sc_t)
                tt_dma[pbuf] = dma(
                    "sync", qv_s,
                    ffn_d[:, oc * 128:(oc + 1) * 128].rearrange("(t p) n -> p t n", p=128),
                    tt_stage[:, pbuf, :, :],
                    deps=[("SC", sc_t)])

            emit_prep_a(0)
            emit_prep_b()
            emit_prep_a(1)
            emit_prep_b()
            for oc in range(DC):
                emit_consume_mm(oc)
                if oc + 2 < DC:
                    emit_prep_a(oc + 2)
                emit_consume_evac(oc)
                if oc + 2 < DC:
                    emit_prep_b()
                emit_consume_tail(oc)
            P.join()
            P.flush(block)

        xiT_cm.__exit__(None, None, None)

        with contextlib.ExitStack() as p5b:
            h2t = p5b.enter_context(nc.sbuf_tensor("p5b_h2", [128, 2, D], F32))
            ft_t = p5b.enter_context(nc.sbuf_tensor("p5b_ft", [128, 2, D], BF16))
            yt = p5b.enter_context(nc.sbuf_tensor("p5b_y", [128, 2, D], F32))
            sq5 = p5b.enter_context(nc.sbuf_tensor("p5b_sq", [128, D], BF16))
            stt = {k: p5b.enter_context(nc.sbuf_tensor("p5b_s_" + k, [128, 1], F32))
                   for k in ("ssum", "ssq", "negm", "msq", "vare", "sd", "rstd")}
            stt = {k: v[:] for k, v in stt.items()}

            ys_of = [0, 0]
            apply_ = [0, 0]
            prev_apply = 0
            for qt in range(NQT):
                buf = qt % 2
                qx = "QX%d" % buf
                qk = "QK%d" % buf
                qs = ("YS", "YS2")[buf]
                vh = dma("sync", qx, h2t[:, buf, :], h2_d[qt * 128:(qt + 1) * 128, :],
                         deps=[("DV", apply_[buf])])
                vff = dma("sync", qk, ft_t[:, buf, :], ffn_d[qt * 128:(qt + 1) * 128, :],
                          deps=[("DV", apply_[buf])])
                v_rstd4 = ln_stats(ft_t[:, buf, :], sq5[:], stt,
                                   deps_dv=[(qk, vff)],
                                   deps_sc=[(qk, vff), ("DV", prev_apply)])
                dv_y2 = op("vector", "DV",
                           lambda e, qt=qt, buf=buf: e.tensor_scalar(
                               yt[:, buf, :], ft_t[:, buf, :], stt["negm"], stt["rstd"],
                               ALU.add, ALU.mult),
                           deps=[("DV", v_rstd4), (qs, ys_of[buf])])
                dv_out = op("vector", "DV",
                            lambda e, buf=buf: e.tensor_tensor(yt[:, buf, :], yt[:, buf, :],
                                                               h2t[:, buf, :], ALU.add),
                            deps=[("DV", dv_y2), (qx, vh)])
                apply_[buf] = dv_out
                prev_apply = dv_out
                ys_of[buf] = dma("sync", qs, y[qt * 128:(qt + 1) * 128, :],
                                 yt[:, buf, :], deps=[("DV", dv_out)])
            waiter("sync", "YS", ys_of[0])
            waiter("sync", "YS2", ys_of[1])
            P.flush(block)

    return nc


def kernel(**inputs):
    x = np.asarray(inputs["L0_attn_x"])
    wq_ = np.asarray(inputs["L0_attn_w_q"])
    bq_ = np.ascontiguousarray(np.asarray(inputs["L0_attn_b_q"]).reshape(DC, 128).T)
    wk_ = np.asarray(inputs["L0_attn_w_k"])
    bk_ = np.ascontiguousarray(np.asarray(inputs["L0_attn_b_k"]).reshape(KVH, 128).T)
    wv_ = np.asarray(inputs["L0_attn_w_v"])
    bv_ = np.ascontiguousarray(np.asarray(inputs["L0_attn_b_v"]).reshape(KVH, 128).T)
    wo_ = np.asarray(inputs["L0_attn_w_o"])
    bo_ = np.ascontiguousarray(np.asarray(inputs["L0_attn_b_o"]).reshape(DC, 128).T)
    fw_ = np.asarray(inputs["L0_ffn_weight"])
    fb_ = np.asarray(inputs["L0_ffn_bias"])
    fws_ = np.asarray(inputs["L0_ffn_weight_scale"])
    fis_ = np.asarray(inputs["L0_ffn_input_scale"])

    fb_pc = np.ascontiguousarray(fb_.reshape(DC, 128).T)
    fws_pc = np.ascontiguousarray(fws_.reshape(DC, 128).T)
    fis_bc = np.ascontiguousarray(np.broadcast_to(fis_.reshape(1, 1), (128, 1)))
    identm = np.eye(128, dtype=ml_dtypes.bfloat16)

    nc = build_program()
    in_maps = []
    for c in range(8):
        b = c // 4
        q0 = (c % 4) * OWN
        xr = np.ascontiguousarray(np.roll(x[b], -q0, axis=0))
        in_maps.append(dict(
            xb=xr, wq=wq_, wk=wk_, wv=wv_, wo=wo_, fw=fw_,
            bq_pc=bq_, bk_pc=bk_, bv_pc=bv_, bo_pc=bo_,
            fb_pc=fb_pc, fws_pc=fws_pc, fis_bc=fis_bc,
            ident=identm,
        ))
    res = run_bass_kernel_spmd(nc, in_maps, core_ids=list(range(8)))
    outs = [res.results[c]["y"] for c in range(8)]
    full = np.concatenate(outs, axis=0).reshape(2, T, D)
    return full.astype(np.float32)
